# version 16
# baseline (speedup 1.0000x reference)
import math
import contextlib
import numpy as np

import concourse.bass as bass
import concourse.mybir as mybir
from concourse.ap import AP
from concourse.bass_utils import run_bass_kernel_spmd

F32 = mybir.dt.float32
I8 = mybir.dt.int8
BF16 = mybir.dt.bfloat16
AX = mybir.AluOpType
ACT = mybir.ActivationFunctionType
AXL = mybir.AxisListType

S, D, DF = 8192, 2048, 4096
NH = 16
NCORES = 8
T, B = 1040, 104
NW, WSTR, RPW = 10, 110, 11
NR = NW * RPW
NCH, NFCH = 16, 32
NSUB = 9
SUBS = [(i * 128, (128 if i < 8 else 16)) for i in range(NSUB)]
TG = [(0, 512), (512, 512), (1024, 16)]
THIRDS = [(0, 384), (384, 384), (768, 272)]
T0S = [(1024 * c) // 10 * 10 for c in range(NCORES)]
SIZES = [T0S[c + 1] - T0S[c] for c in range(NCORES - 1)] + [S - T0S[-1]]
ISQD = 1.0 / math.sqrt(128.0)
NEG_BIG = -1.0e30
C1 = 1.0 / math.sqrt(float(D))
B3 = 12582912.0
C2 = 1.0 / math.sqrt(float(DF))
WMEAN = 1.0 / (D * DF)
GBW = [(0, 2), (2, 2), (4, 2), (6, 2), (8, 2)]


class Ctr:
    __slots__ = ("sem", "val", "name")

    def __init__(self, sem, name):
        self.sem, self.val, self.name = sem, 0, name


class Plan:
    def __init__(self):
        self.steps = {k: [] for k in
                      ("tensor", "vector", "scalar", "gpsimd", "sync")}
        self.pending = {k: [] for k in self.steps}
        self.auto = {}

    def emit(self, eng, fn, wait=(), inc=(), incn=1):
        waits = [(c, (c.val if v is None else v)) for (c, v) in wait if c]
        inc = list(inc)
        ac = self.auto.get(eng)
        if ac is not None and incn == 1:
            if eng != "tensor":
                waits.append((ac, ac.val))
            if ac not in inc:
                inc.append(ac)
        if self.pending[eng]:
            waits = self.pending[eng] + waits
            self.pending[eng] = []
        for c in inc:
            c.val += incn
        self.steps[eng].append((fn, waits, list(inc), incn))

    def barrier(self, pairs):
        pairs = [(c, (c.val if v is None else v)) for (c, v) in pairs]
        for k in self.pending:
            merged = {id(c): (c, v) for (c, v) in self.pending[k]}
            for c, v in pairs:
                if id(c) not in merged or merged[id(c)][1] < v:
                    merged[id(c)] = (c, v)
            self.pending[k] = list(merged.values())

    def run(self, block):
        def mk(steps):
            def body(eng):
                for fn, waits, incs, incn in steps:
                    for c, v in waits:
                        if v > 0:
                            eng.wait_ge(c.sem, v)
                    inst = fn(eng)
                    if inst is not None:
                        for c in incs:
                            inst.then_inc(c.sem, incn)
            return body
        for name in self.steps:
            getattr(block, name)(mk(self.steps[name]))


class Arena:

    def __init__(self, name):
        self.name = name
        self.pools = {}
        self.live = {}

    def set_pool(self, key, tensor, nbytes):
        self.pools[key] = [tensor, nbytes, [(0, nbytes)], 0]

    @staticmethod
    def _key(dt):
        if dt == BF16:
            return "b"
        if dt == I8:
            return "i"
        return "f"

    def alloc(self, cols, dt=F32, align=64, tag=""):
        esz = {"b": 2, "i": 1}.get(self._key(dt), 4)
        key = self._key(dt)
        tensor, nbytes, free_list, _ = self.pools[key]
        need = cols * esz
        order = (range(len(free_list)) if need > 1024 else
                 range(len(free_list) - 1, -1, -1))
        for idx in order:
            off, ln = free_list[idx]
            if need <= 1024:
                aoff = (off + ln - need) // align * align
                if aoff < off:
                    continue
            else:
                aoff = -(-off // align) * align
            if aoff + need <= off + ln:
                pre = (off, aoff - off)
                post = (aoff + need, off + ln - (aoff + need))
                repl = [p for p in (pre, post) if p[1] > 0]
                free_list[idx:idx + 1] = repl
                view = tensor.ap()[:, aoff // esz: aoff // esz + cols]
                self.live[id(view)] = (key, aoff, need, tag)
                used = nbytes - sum(l for _, l in free_list)
                self.pools[key][3] = max(self.pools[key][3], used)
                return view
        raise AssertionError(
            f"arena {self.name}/{key} OOM for {need}B tag={tag}; "
            f"free={free_list}")

    def free(self, view):
        key, off, ln, _ = self.live.pop(id(view))
        free_list = self.pools[key][2]
        free_list.append((off, ln))
        free_list.sort()
        merged = []
        for o, l in free_list:
            if merged and merged[-1][0] + merged[-1][1] == o:
                merged[-1] = (merged[-1][0], merged[-1][1] + l)
            else:
                merged.append((o, l))
        self.pools[key][2] = [tuple(m) for m in merged]


def _hadamard(n):
    h = np.array([[1.0]], dtype=np.float32)
    while h.shape[0] < n:
        h = np.block([[h, h], [h, -h]])
    return h.astype(np.float32)


def _make_consts():
    h128, h4 = _hadamard(128), _hadamard(4)
    hh4 = np.zeros((128, 4 * 512), dtype=np.float32)
    for j in range(4):
        for cp in range(4):
            hh4[:, j * 512 + cp * 128: j * 512 + cp * 128 + 128] = \
                h4[cp, j] * h128
    maskb = np.full((128, 128), NEG_BIG, dtype=np.float32)
    mask01 = np.zeros((128, 128), dtype=np.float32)
    for wq in range(4):
        for ip in range(RPW):
            p = 32 * wq + ip
            maskb[p, 10 * ip: 10 * ip + 10] = 0.0
            mask01[p, 10 * ip: 10 * ip + 10] = 1.0
    sel = np.zeros((16, NSUB * 128), dtype=np.float32)
    for si in range(NSUB):
        r0, rn = SUBS[si]
        i0 = -(-r0 // 10)
        i1 = -(-(r0 + rn) // 10)
        for b_ in range(i1 - i0):
            c = 10 * (i0 + b_) - r0
            if 0 <= c < rn:
                sel[b_, si * 128 + c] = 1.0
    return {
        "hh4": hh4,
        "ident": np.eye(128, dtype=np.float32),
        "maskb": np.ascontiguousarray(np.tile(maskb, (1, NH))),
        "mask01": np.ascontiguousarray(np.tile(mask01, (1, NH))),
        "sel": sel,
    }


def _stage(inputs):
    x = np.ascontiguousarray(
        np.asarray(inputs["L0_attn_hidden_states"], np.float32).reshape(S, D))
    shared = {
        "x_full": x,
        "wqt": np.ascontiguousarray(np.asarray(inputs["L0_attn_wq"],
                                               np.float32)
                                    .reshape(D, NH, 128).transpose(1, 0, 2)),
        "wkt": np.ascontiguousarray(np.asarray(inputs["L0_attn_wk"],
                                               np.float32)
                                    .reshape(D, NH, 128).transpose(1, 0, 2)),
        "wv": np.ascontiguousarray(np.asarray(inputs["L0_attn_wv"],
                                              np.float32)),
        "wo": np.ascontiguousarray(np.asarray(inputs["L0_attn_wo"],
                                              np.float32)),
        "wupT": np.ascontiguousarray(np.asarray(inputs["L0_ffn_w_up"],
                                                np.float32).T),
        "wdownT": np.ascontiguousarray(np.asarray(inputs["L0_ffn_w_down"],
                                                  np.float32).T),
        "bias": np.stack([np.asarray(inputs[k], np.float32) for k in
                          ("L0_attn_bq", "L0_attn_bk", "L0_attn_bv",
                           "L0_attn_bo")]),
        **_make_consts(),
    }
    maps = []
    for c in range(NCORES):
        t0 = T0S[c]
        valid = min(S - t0, T)
        xtm = np.zeros((T, D), dtype=np.float32)
        xtm[:valid] = x[t0:t0 + valid]
        m = dict(shared)
        m["xtm"] = xtm
        m["xfm"] = np.ascontiguousarray(xtm.T)
        m["onesv"] = (np.arange(T) < valid).astype(np.float32).reshape(1, T)
        maps.append(m)
    return maps



def apx(view, off_elems, dims):
    return AP(view.tensor, view.offset + off_elems,
              [list(view.ap[0])] + [list(d) for d in dims])

def build_program(debug=0, phases=99):
    nc = bass.Bass("TRN2", target_bir_lowering=False)
    dram = {}

    def din(name, shape, dt=F32):
        dram[name] = nc.dram_tensor(name, shape, dt, kind="ExternalInput").ap()
        return dram[name]

    x_full = din("x_full", [S, D])
    xtm = din("xtm", [T, D])
    xfm = din("xfm", [D, T])
    wqt = din("wqt", [NH, D, 128])
    wkt = din("wkt", [NH, D, 128])
    wv_d = din("wv", [D, D])
    wo_d = din("wo", [D, D])
    wupT = din("wupT", [D, DF])
    wdownT = din("wdownT", [DF, D])
    bias_d = din("bias", [4, D])
    hh4_d = din("hh4", [128, 4 * 512])
    ident_d = din("ident", [128, 128])
    maskb_d = din("maskb", [128, NH * 128])
    mask01_d = din("mask01", [128, NH * 128])
    sel_d = din("sel", [16, NSUB * 128])
    onesv_d = din("onesv", [1, T])

    y_out = nc.dram_tensor("y_out", [T, D], F32, kind="ExternalOutput").ap()
    wup_int = nc.dram_tensor("wup_int", [NCH, 128, DF], BF16,
                             kind="Internal").ap()
    wdn_int = nc.dram_tensor("wdn_int", [NFCH, 128, D], BF16,
                             kind="Internal").ap()
    z_park = nc.dram_tensor("z_park", [NSUB, 128, D], F32,
                            kind="Internal").ap()
    hq_park = nc.dram_tensor("hq_park", [NFCH, 128, T], BF16,
                             kind="Internal").ap()
    y1_park = nc.dram_tensor("y1_park", [NSUB, 128, D], F32,
                             kind="Internal").ap()

    dbg = {}

    def dbg_out(name, shape, dt=F32):
        if debug:
            dbg[name] = nc.dram_tensor(name, shape, dt,
                                       kind="ExternalOutput").ap()

    for nm, shp in [("d_colsum", [1, D]), ("d_q", [128, NH * B]),
                    ("d_swide", [128, 3 * NH * 128]),
                    ("d_ekm", [128, NW * NH * RPW]),
                    ("d_epi", [128, 6 * 3 * NH]), ("d_ctx", [128, NH * B]),
                    ("d_gb", [128, NCH * 2 * NH * RPW]),
                    ("d_ctx0", [128, NH * B]),
                    ("d_sv", [1, D]), ("d_u", [65, D]), ("d_a", [65, T]),
                    ("d_z", [NSUB, 128, D]), ("d_y1", [NSUB, 128, D]),
                    ("d_xq", [128, NCH * T // 2]), ("d_wsc", [1, 6]),
                    ("d_h", [128, NFCH * 136]), ("d_hq", [128, NFCH * T // 2]),
                    ("d_ffn", [128, NCH * T]), ("d_s1", [128, NSUB]),
                    ("d_o2", [128, NSUB])]:
        dbg_out(nm, shp)

    plan = Plan()
    E = plan.emit
    stk = contextlib.ExitStack()
    with stk:
        ent = stk.enter_context
        AR = Arena("A")
        AR.set_pool("f", ent(nc.sbuf_tensor(
            "arena", [128, 122624 // 4], F32)), 122624)
        AR.set_pool("b", ent(nc.sbuf_tensor(
            "arenab", [128, 90112 // 2], BF16)), 90112)
        pb = [ent(nc.psum_tensor(f"pb{i}", [128, 512], F32)) for i in range(8)]

        def ctr(name):
            return Ctr(ent(nc.semaphore(name)), name)

        c_ld = ctr("c_ld")
        c_ldg = ctr("c_ldg")
        c_st = ctr("c_st")
        c_te = ctr("c_te")
        c_ve = ctr("c_ve")
        c_se = ctr("c_se")
        c_ge = ctr("c_ge")
        plan.auto = {"tensor": c_te, "vector": c_ve, "scalar": c_se,
                     "gpsimd": c_ge}

        streams = {}

        def stream_ctr(name):
            if name not in streams:
                streams[name] = ctr("cs_" + name)
            return streams[name]

        def ld(dst, src, q="sync", wait=(), stream=None):
            cc = stream_ctr(stream) if stream else \
                (c_ld if q == "sync" else c_ldg)
            E(q, lambda e, d=dst, s=src: e.dma_start(out=d, in_=s),
              wait=wait, inc=[cc], incn=16)
            return (cc, cc.val)

        def lda(dst, src, wait=(), stream=None):
            cc = stream_ctr(stream) if stream else c_ldg
            E("gpsimd", lambda e, d=dst, s=src: e.dma_start(
                out=d, in_=s, accum_op=AX.add),
              wait=wait, inc=[cc], incn=16)
            return (cc, cc.val)

        def st(dst, src, wait=(), stream=None):
            cc = stream_ctr(stream) if stream else c_st
            E("sync", lambda e, d=dst, s=src: e.dma_start(out=d, in_=s),
              wait=wait, inc=[cc], incn=16)
            return (cc, cc.val)

        def dstore(name, src, wait=(), stream=None):
            if debug and name in dbg:
                r = st(dbg[name], src, wait=wait, stream=stream)
                dbar()
                return r
            return None

        def dbar():
            plan.barrier([(c_ve, None), (c_se, None), (c_te, None),
                          (c_ge, None), (c_st, None), (c_ld, None),
                          (c_ldg, None)]
                         + [(cc, None) for cc in streams.values()])

        hh4_b = AR.alloc(4 * 512, BF16)
        ident_f = AR.alloc(128, F32)
        maskb_s = AR.alloc(NH * 128, F32)
        mask01_s = AR.alloc(NH * 128, F32)
        biasr = AR.alloc(D, F32)
        onesc = AR.alloc(16, F32)
        onesr = AR.alloc(T, F32)
        sconst = AR.alloc(16, F32)
        cscr = AR.alloc(4 * 512, F32)

        ld(cscr[:, 0:2048], hh4_d)
        ld(ident_f[:, :], ident_d)
        ld(maskb_s[:, :], maskb_d)
        ld(mask01_s[:, :], mask01_d)
        for bi4 in range(4):
            ld(biasr[32 * bi4:32 * bi4 + 1, :], bias_d[bi4:bi4 + 1, :])
        consts_ld = c_ld.val
        E("vector", lambda ve: ve.tensor_copy(hh4_b[:, :], cscr[:, 0:2048]),
          wait=[(c_ld, consts_ld)], inc=[c_ve])
        AR.free(cscr)
        E("vector", lambda ve: ve.memset(onesc[:, :], 1.0), inc=[c_ve])
        E("vector", lambda ve: ve.memset(onesr[:, :], 1.0), inc=[c_ve])
        E("vector", lambda ve: ve.memset(sconst[:, :], float(S)), inc=[c_ve])
        consts_ve = c_ve.val
        ld(onesr[0:1, :], onesv_d, wait=[(c_ve, consts_ve)])
        ld(onesr[32:33, :], onesv_d, wait=[(c_ve, consts_ve)])
        consts_ld = c_ld.val

        wblk = [AR.alloc(1024, F32) for _ in range(2)]
        t1w = [AR.alloc(1024, F32)]
        wintl = [AR.alloc(1024, BF16)]
        wabs = AR.alloc(16, F32)
        thrp = AR.alloc(16, F32)
        wsc = AR.alloc(16, F32)
        WBLOCKS = []
        for b2 in range(NCH):
            for hf in range(4):
                WBLOCKS.append((0, wupT[b2 * 128:(b2 + 1) * 128,
                                        hf * 1024:(hf + 1) * 1024],
                                wup_int[b2][:, hf * 1024:(hf + 1) * 1024]))
        for b2 in range(NFCH):
            for hf in range(2):
                WBLOCKS.append((1, wdownT[b2 * 128:(b2 + 1) * 128,
                                          hf * 1024:(hf + 1) * 1024],
                                wdn_int[b2][:, hf * 1024:(hf + 1) * 1024]))
        N_UP = NCH * 4

        def emit_w1(i):
            mat, srcs, dsts = WBLOCKS[i]
            par = i % 2
            vc, vv = ld(wblk[par][:, :], srcs, q="sync",
                        wait=[(c_ve, None)],
                        stream=f"wb{par}")
            E("vector", lambda ve, par=par: ve.tensor_reduce(
                wabs[:, 8:9], wblk[par][:, :], AXL.X, AX.add,
                apply_absolute_value=True),
              wait=[(vc, vv), (c_ve, None)], inc=[c_ve])
            if i == 0 or i == N_UP:
                E("vector", lambda ve, mat=mat: ve.tensor_copy(
                    wabs[:, mat:mat + 1], wabs[:, 8:9]),
                  wait=[(c_ve, None)], inc=[c_ve])
            else:
                E("vector", lambda ve, mat=mat: ve.tensor_tensor(
                    wabs[:, mat:mat + 1], wabs[:, mat:mat + 1],
                    wabs[:, 8:9], AX.add),
                  wait=[(c_ve, None)], inc=[c_ve])

        wp_done = {0: 0, 1: 0}

        def emit_w2(i, thr_ve):
            mat, srcs, dsts = WBLOCKS[i]
            par = i % 2
            wint = wintl[par % len(wintl)]
            t1 = t1w[par % len(t1w)]
            wps = stream_ctr(f"wp{par % len(wintl)}")
            vc, vv = ld(wblk[par][:, :], srcs,
                        q=("sync" if par == 0 else "gpsimd"),
                        wait=[(c_ve, None)],
                        stream=(f"wb{par}" if par == 0 else f"wg{par}"))
            thr_bc = AP(thrp.tensor, thrp.offset + mat,
                        [list(thrp.ap[0]), [0, 1024]])
            E("vector", lambda ve, par=par, thr_bc=thr_bc, t1=t1:
              ve.tensor_tensor(t1[:, :], wblk[par][:, :],
                               thr_bc, AX.is_gt),
              wait=[(vc, vv), (c_ve, thr_ve), (c_ve, None)],
              inc=[c_ve])
            E("vector", lambda ve, par=par, mat=mat, wint=wint, t1=t1:
              ve.scalar_tensor_tensor(
                  wint[:, :], wblk[par][:, :],
                  thrp[:, 4 + mat:5 + mat], t1[:, :],
                  AX.is_lt, AX.subtract),
              wait=[(c_ve, None), (wps, wp_done[par % len(wintl)])],
              inc=[c_ve])
            wq_ve = c_ve.val
            E("sync", lambda e, dsts=dsts, wint=wint: e.dma_start(
                out=dsts, in_=wint[:, :]),
              wait=[(c_ve, wq_ve)], inc=[wps], incn=16)
            wp_done[par % len(wintl)] = wps.val


        xh = AR.alloc(NCH * T, BF16)
        xl = AR.alloc(NCH * T, BF16)
        xload = [AR.alloc(T, F32) for _ in range(2)]
        wsl_f = [AR.alloc(NCH * 128, F32) for _ in range(2)]
        wsl_h = [AR.alloc(NCH * 128, BF16) for _ in range(2)]
        wsl_l = [AR.alloc(NCH * 128, BF16) for _ in range(2)]
        kf = [AR.alloc(T, F32) for _ in range(2)]
        xtailf = AR.alloc(NCH * 16, F32)
        qc = AR.alloc(NH * B, F32)
        swide = AR.alloc(3 * NH * 128, F32)
        cx = [AR.alloc(512, F32) for _ in range(2)]
        csum = AR.alloc(D, F32)

        E("vector", lambda ve: ve.memset(swide[:, :], 0.0), inc=[c_ve])
        swide_zeroed = c_ve.val

        xsp = []
        for blk in range(16):
            par = blk % 2
            w = [(c_ve, xsp[blk - 2])] if blk >= 2 else ()
            vc, vv = ld(xload[par][:, :],
                        xfm[blk * 128:(blk + 1) * 128, :],
                        q="gpsimd", wait=w, stream=f"xl{par}")
            sl = slice(blk * T, (blk + 1) * T)
            E("scalar", lambda se, sl=sl, par=par: se.activation(
                xh[:, sl], xload[par][:, :], ACT.Copy),
              wait=[(vc, vv)], inc=[c_se])
            E("scalar", lambda se, blk=blk, par=par: se.activation(
                xtailf[:, blk * 16:(blk + 1) * 16],
                xload[par][:, 1024:1040], ACT.Copy),
              wait=[(vc, vv)], inc=[c_se])
            E("vector", lambda ve, sl=sl, par=par: ve.scalar_tensor_tensor(
                xl[:, sl], xload[par][:, :], 1.0, xh[:, sl], AX.bypass,
                AX.subtract),
              wait=[(c_se, None)], inc=[c_ve])
            xsp.append(c_ve.val)
        xsplit_ve = c_ve.val
        xsplit_se = c_se.val

        cs = {"tt": 0, "incs": []}

        def colsum_step(n):
            for _ in range(n):
                tt = cs["tt"]
                if tt >= 256:
                    return
                cs["tt"] += 1
                par = tt % 2
                rb, qb = tt // 4, tt % 4
                w = [(c_te, cs["incs"][tt - 2])] if tt >= 2 else ()
                vc, vv = ld(cx[par][:, :],
                            x_full[rb * 128:(rb + 1) * 128,
                                   qb * 512:(qb + 1) * 512],
                            q="gpsimd", wait=w, stream=f"cs{par}")
                E("tensor", lambda te, par=par, tt=tt, qb=qb:
                  te.matmul(
                      pb[qb][0:1, 0:512], onesc[:, 0:1],
                      cx[par][:, 0:512],
                      start=(tt < 4), stop=(tt >= 252)),
                  wait=[(vc, vv), (c_ve, consts_ve)],
                  inc=[c_te])
                cs["incs"].append(c_te.val)

        kev = [0, 0]
        scev = [0, 0]
        prev_scores_ve = swide_zeroed
        prev_k_mm = 0
        wsl_free = [0, 0]
        for h in range(NH):
            par = h % 2
            colsum_step(8)
            vqc, vqv = ld(wsl_f[par][:, :]
                          .rearrange("p (c m) -> p c m", m=128),
                          wqt[h].rearrange("(c p) m -> c p m", p=128)
                          .transpose([1, 0, 2]),
                          wait=([(c_ve, None),
                                 (c_te, wsl_free[par])]
                                if h >= 2 else ()),
                          stream=f"ws{par}")
            E("scalar", lambda se, par=par: se.activation(
                wsl_h[par][:, :], wsl_f[par][:, :], ACT.Copy),
              wait=[(vqc, vqv), (c_te, wsl_free[par])], inc=[c_se])
            E("vector", lambda ve, par=par: ve.scalar_tensor_tensor(
                wsl_l[par][:, :], wsl_f[par][:, :], 1.0, wsl_h[par][:, :],
                AX.bypass, AX.subtract),
              wait=[(c_se, None), (c_ve, xsplit_ve)], inc=[c_ve])
            wq_split = c_ve.val

            first_w = [(c_ve, wq_split), (c_ve, prev_scores_ve),
                       (c_se, xsplit_se)]
            for kk in range(NCH):
                swh = wsl_h[par][:, kk * 128:(kk + 1) * 128]
                swl = wsl_l[par][:, kk * 128:(kk + 1) * 128]
                mvh = xh[:, kk * T:(kk + 1) * T].rearrange(
                    "p (b s) -> p b s", s=10)[:, :, 0]
                mvl = xl[:, kk * T:(kk + 1) * T].rearrange(
                    "p (b s) -> p b s", s=10)[:, :, 0]
                E("tensor", lambda te, sw=swh, mv=mvh, kk=kk: te.matmul(
                    pb[7][:, 0:B], sw, mv, start=(kk == 0), stop=False),
                  wait=(first_w if kk == 0 else ()))
                E("tensor", lambda te, sw=swh, mv=mvl: te.matmul(
                    pb[7][:, 0:B], sw, mv, start=False, stop=False))
                E("tensor", lambda te, sw=swl, mv=mvh: te.matmul(
                    pb[7][:, 0:B], sw, mv, start=False, stop=False))
            E("tensor", lambda te, h=h: te.matmul(
                pb[7][:, 0:B], biasr[0:1, h * 128:(h + 1) * 128],
                onesr[0:1, 0:B], start=False, stop=True),
              wait=([(c_ld, consts_ld)] if h == 0 else ()), inc=[c_te])
            qmm = c_te.val
            E("scalar", lambda se, h=h: se.activation(
                qc[:, h * B:(h + 1) * B], pb[7][:, 0:B], ACT.Copy),
              wait=[(c_te, qmm)], inc=[c_se])
            q_ev = c_se.val

            vkc, vkv = ld(wsl_f[par][:, :]
                          .rearrange("p (c m) -> p c m", m=128),
                          wkt[h].rearrange("(c p) m -> c p m", p=128)
                          .transpose([1, 0, 2]),
                          wait=[(c_ve, wq_split)], stream=f"ws{par}")
            E("scalar", lambda se, par=par: se.activation(
                wsl_h[par][:, :], wsl_f[par][:, :], ACT.Copy),
              wait=[(vkc, vkv), (c_te, qmm)], inc=[c_se])
            E("vector", lambda ve, par=par: ve.scalar_tensor_tensor(
                wsl_l[par][:, :], wsl_f[par][:, :], 1.0, wsl_h[par][:, :],
                AX.bypass, AX.subtract),
              wait=[(c_se, None)], inc=[c_ve])
            wk_split = c_ve.val

            colsum_step(4)
            first_w = [(c_ve, wk_split), (c_se, max(kev)),
                       (c_ve, scev[par])]
            for kk in range(NCH):
                swh = wsl_h[par][:, kk * 128:(kk + 1) * 128]
                swl = wsl_l[par][:, kk * 128:(kk + 1) * 128]
                for gi, (g0, gn) in enumerate(TG[0:2]):
                    mh = xh[:, kk * T + g0: kk * T + g0 + gn]
                    mlo = xl[:, kk * T + g0: kk * T + g0 + gn]
                    E("tensor", lambda te, sw=swh, m=mh, gi=gi, gn=gn,
                        kk=kk: te.matmul(pb[4 + gi][:, 0:gn], sw, m,
                                         start=(kk == 0), stop=False),
                      wait=(first_w if (kk == 0 and gi == 0) else ()))
                    E("tensor", lambda te, sw=swh, m=mlo, gi=gi, gn=gn:
                      te.matmul(pb[4 + gi][:, 0:gn], sw, m, start=False,
                                stop=False))
                    E("tensor", lambda te, sw=swl, m=mh, gi=gi, gn=gn:
                      te.matmul(pb[4 + gi][:, 0:gn], sw, m, start=False,
                                stop=False))
                E("tensor", lambda te, kk=kk, par=par: te.matmul(
                    pb[6][:, 0:16],
                    wsl_f[par][:, kk * 128:(kk + 1) * 128],
                    xtailf[:, kk * 16:(kk + 1) * 16],
                    start=(kk == 0), stop=False),
                  wait=([(c_se, xsplit_se)] if kk == 0 else ()))
            for gi, (g0, gn) in enumerate(TG):
                E("tensor", lambda te, h=h, g0=g0, gn=gn, gi=gi: te.matmul(
                    pb[4 + gi][:, 0:gn], biasr[32:33, h * 128:(h + 1) * 128],
                    onesr[32:33, g0:g0 + gn], start=False, stop=True),
                  inc=([c_te] if gi == 2 else []))
            kmm = c_te.val
            wsl_free[par] = kmm
            for gi, (g0, gn) in enumerate(TG):
                E("scalar", lambda se, g0=g0, gn=gn, gi=gi, par=par:
                  se.activation(kf[par][:, g0:g0 + gn], pb[4 + gi][:, 0:gn],
                                ACT.Copy, scale=ISQD),
                  wait=[(c_te, kmm)], inc=[c_se])
            k_ev = c_se.val
            kev[par] = k_ev

            for wdx in range(NW):
                k0 = wdx * WSTR
                kn = min(128, T - k0)
                rn = min(RPW, B - RPW * wdx)
                w = ([(c_se, k_ev), (c_se, q_ev)] if wdx == 0
                     else [(c_ve, None)])
                E("tensor", lambda te, h=h, wdx=wdx, k0=k0, kn=kn, rn=rn,
                    par=par: te.matmul(
                      pb[7][0:rn, 0:kn],
                      qc[:, h * B + RPW * wdx: h * B + RPW * wdx + rn],
                      kf[par][:, k0:k0 + kn], start=True, stop=True),
                  wait=w, inc=[c_te])
                smm = c_te.val
                wb2, wq2 = wdx // 4, wdx % 4
                E("vector", lambda ve, h=h, kn=kn, rn=rn, wb2=wb2, wq2=wq2:
                  ve.tensor_copy(
                      swide[32 * wq2:32 * wq2 + rn,
                            wb2 * 2048 + h * 128: wb2 * 2048 + h * 128 + kn],
                      pb[7][0:rn, 0:kn]),
                  wait=[(c_te, smm), (c_ve, swide_zeroed)], inc=[c_ve])
            scev[par] = c_ve.val
            prev_scores_ve = c_ve.val
            for wi_ in range(8 * h, 8 * h + 8):
                emit_w1(wi_)
        colsum_step(256)
        scores_done = c_ve.val

        wabs_ge = c_ve.val
        E("tensor", lambda te: te.matmul(
            pb[4][0:1, 0:2], onesc[:, 0:1], wabs[:, 0:2],
            start=True, stop=True),
          wait=[(c_ve, wabs_ge), (c_te, None)], inc=[c_te])
        E("vector", lambda ve: ve.tensor_copy(wsc[0:1, 0:2],
                                              pb[4][0:1, 0:2]),
          wait=[(c_te, None)], inc=[c_ve])
        E("vector", lambda ve: ve.tensor_scalar(
            wsc[0:1, 0:2], wsc[0:1, 0:2], WMEAN, 1e-5, AX.mult, AX.max),
          wait=[(c_ve, None)], inc=[c_ve])
        E("vector", lambda ve: ve.tensor_scalar(
            wsc[0:1, 2:4], wsc[0:1, 0:2], 0.5, None, AX.mult),
          wait=[(c_ve, None)], inc=[c_ve])
        E("vector", lambda ve: ve.tensor_copy(wsc[0:1, 4:6], wsc[0:1, 0:2]),
          wait=[(c_ve, None)], inc=[c_ve])
        wsc_ve = c_ve.val
        dstore("d_wsc", wsc[0:1, 0:6], wait=[(c_ve, wsc_ve)])
        E("tensor", lambda te: te.matmul(
            pb[4][0:128, 0:4], onesr[0:1, 0:128], wsc[0:1, 2:6],
            start=True, stop=True),
          wait=[(c_ve, wsc_ve), (c_te, None)], inc=[c_te])
        E("vector", lambda ve: ve.tensor_copy(thrp[:, 0:4],
                                              pb[4][0:128, 0:4]),
          wait=[(c_te, None)], inc=[c_ve])
        E("vector", lambda ve: ve.tensor_scalar(
            thrp[:, 4:6], thrp[:, 0:2], -1.0, None, AX.mult),
          wait=[(c_ve, None)], inc=[c_ve])
        thr_ve = c_ve.val
        wscb = thrp[:, 2:4]

        colsum_te = cs["incs"][-1]
        for ns in range(4):
            E("vector", lambda ve, ns=ns: ve.tensor_copy(
                csum[0:1, ns * 512:(ns + 1) * 512], pb[ns][0:1, 0:512]),
              wait=[(c_te, colsum_te)], inc=[c_ve])
        csum_ve = c_ve.val
        dstore("d_colsum", csum[0:1, :], wait=[(c_ve, csum_ve)])
        sw_dump = dstore("d_swide", swide[:, :], wait=[(c_ve, scores_done)],
                         stream="dsw")
        dstore("d_q", qc[:, :], wait=[(c_ve, scores_done)])
        if phases <= 1:
            _finish_stub(E, AR, c_ve, c_st, y_out, biasr,
                         [(c_te, None), (c_ve, None), (c_se, None)])
            plan.run(ent(nc.Block()))
            return nc, dram, dbg

        dbar()
        for buf in (xh, xl, xload[0], xload[1], wsl_f[0], wsl_f[1],
                    wsl_h[0], wsl_h[1], wsl_l[0], wsl_l[1], kf[0], kf[1],
                    cx[0], cx[1]):
            AR.free(buf)
        wintl.append(AR.alloc(1024, BF16))
        t1w.append(AR.alloc(1024, F32))

        NWB3 = 3
        ep_m = AR.alloc(NWB3 * NH, F32)
        ep_em = AR.alloc(NWB3 * NH, F32)
        ep_den = AR.alloc(NWB3 * NH, F32)
        ep_dinv = AR.alloc(NWB3 * NH, F32)
        ep_a = AR.alloc(NWB3 * NH, F32)
        ep_cse = AR.alloc(NWB3 * NH, F32)
        ep_t = AR.alloc(NH * 128, F32)
        ekml = [AR.alloc(NH * RPW, F32) for _ in range(NW)]
        stg = AR.alloc(128, F32)
        a16t = AR.alloc(NWB3 * 128, F32)
        cset = AR.alloc(NWB3 * 128, F32)

        def view4(ap_t):
            return ap_t[:, :].rearrange("p (bh k) -> p bh k", k=128)

        def m_bc(ap_tile):
            base = ap_tile[:, 0:NWB3 * NH]
            return AP(base.tensor, base.offset,
                      [list(base.ap[0]), list(base.ap[1]), [0, 128]])

        def msk_bc(ap_tile):
            base = ap_tile[:, 0:NH * 128]
            return AP(base.tensor, base.offset,
                      [list(base.ap[0]), [0, NWB3],
                       list(base.ap[1])])

        E("vector", lambda ve: ve.tensor_tensor(
            swide[:, :].rearrange("p (b hk) -> p b hk", b=NWB3),
            swide[:, :].rearrange("p (b hk) -> p b hk", b=NWB3),
            msk_bc(maskb_s), AX.add),
          wait=[(c_ve, scores_done), (c_ld, consts_ld)]
          + ([sw_dump] if sw_dump else []), inc=[c_ve])
        E("vector", lambda ve: ve.tensor_reduce(
            ep_m[:, :], view4(swide), AXL.X, AX.max),
          wait=[(c_ve, None)], inc=[c_ve])
        E("vector", lambda ve: ve.tensor_scalar(
            ep_m[:, :], ep_m[:, :], 0.0, None, AX.max),
          wait=[(c_ve, None)], inc=[c_ve])
        m_ve = c_ve.val
        E("scalar", lambda se: se.activation(
            ep_em[:, :], ep_m[:, :], ACT.Exp, scale=-1.0),
          wait=[(c_ve, m_ve)], inc=[c_se])
        em_se = c_se.val
        E("vector", lambda ve: ve.tensor_tensor(
            view4(swide), view4(swide), m_bc(ep_m), AX.subtract),
          wait=[(c_ve, m_ve)], inc=[c_ve])
        E("scalar", lambda se: se.activation(
            swide[:, :], swide[:, :], ACT.Exp),
          wait=[(c_ve, None)], inc=[c_se])
        e_se = c_se.val
        E("vector", lambda ve: ve.tensor_reduce(
            ep_den[:, :], view4(swide), AXL.X, AX.add),
          wait=[(c_se, e_se)], inc=[c_ve])
        E("vector", lambda ve: ve.scalar_tensor_tensor(
            ep_den[:, :], ep_em[:, :], float(S - 10), ep_den[:, :],
            AX.mult, AX.add),
          wait=[(c_ve, None), (c_se, em_se)], inc=[c_ve])
        E("vector", lambda ve: ve.reciprocal(ep_dinv[:, :], ep_den[:, :]),
          wait=[(c_ve, None)], inc=[c_ve])
        dinv_ve = c_ve.val
        E("vector", lambda ve: ve.tensor_tensor(
            ep_a[:, :], ep_em[:, :], ep_dinv[:, :], AX.mult),
          wait=[(c_ve, dinv_ve)], inc=[c_ve])
        a_ve = c_ve.val
        for wb_ in range(NWB3):
            def ebc(tile, wb_=wb_):
                base = tile[:, wb_ * NH:(wb_ + 1) * NH]
                return AP(base.tensor, base.offset,
                          [list(base.ap[0]), list(base.ap[1]), [0, 128]])
            emb, dvb = ebc(ep_em), ebc(ep_dinv)
            E("vector", lambda ve, emb=emb: ve.tensor_tensor(
                ep_t[:, :].rearrange("p (h k) -> p h k", k=128),
                emb,
                mask01_s[:, :].rearrange("p (h k) -> p h k", k=128),
                AX.mult),
              wait=[(c_se, e_se), (c_ve, None)], inc=[c_ve])
            E("vector", lambda ve, wb_=wb_: ve.tensor_tensor(
                swide[:, wb_ * 2048:(wb_ + 1) * 2048],
                swide[:, wb_ * 2048:(wb_ + 1) * 2048],
                ep_t[:, :], AX.subtract),
              wait=[(c_ve, None)], inc=[c_ve])
            E("vector", lambda ve, wb_=wb_, dvb=dvb: ve.tensor_tensor(
                swide[:, wb_ * 2048:(wb_ + 1) * 2048]
                .rearrange("p (h k) -> p h k", k=128),
                swide[:, wb_ * 2048:(wb_ + 1) * 2048]
                .rearrange("p (h k) -> p h k", k=128),
                dvb, AX.mult),
              wait=[(c_ve, None)], inc=[c_ve])
        epp_ve = c_ve.val
        E("vector", lambda ve: ve.tensor_reduce(
            ep_cse[:, :], view4(swide), AXL.X, AX.add),
          wait=[(c_ve, epp_ve)], inc=[c_ve])
        cse_ve = c_ve.val
        if debug and "d_epi" in dbg:
            for j, tile in enumerate((ep_a, ep_m, ep_em, ep_den, ep_dinv,
                                      ep_cse)):
                st(dbg["d_epi"][:, j * 48:(j + 1) * 48],
                   tile[:, 0:NWB3 * NH], wait=[(c_ve, cse_ve)])
            dbar()

        for wb in range(NWB3):
            E("tensor", lambda te, wb=wb: te.transpose(
                pb[0][0:NH, 0:128], ep_a[:, wb * NH:(wb + 1) * NH],
                ident_f[:, :]),
              wait=[(c_ve, a_ve), (c_ve, None)], inc=[c_te])
            E("vector", lambda ve, wb=wb: ve.tensor_copy(
                a16t[0:NH, wb * 128:(wb + 1) * 128], pb[0][0:NH, 0:128]),
              wait=[(c_te, None)], inc=[c_ve])
            E("tensor", lambda te, wb=wb: te.transpose(
                pb[1][0:NH, 0:128], ep_cse[:, wb * NH:(wb + 1) * NH],
                ident_f[:, :]),
              wait=[(c_ve, cse_ve), (c_ve, None)], inc=[c_te])
            E("vector", lambda ve, wb=wb: ve.tensor_copy(
                cset[0:NH, wb * 128:(wb + 1) * 128], pb[1][0:NH, 0:128]),
              wait=[(c_te, None)], inc=[c_ve])
        a16_ve = c_ve.val

        ek_ev = {b_: a16_ve for b_ in range(4)}
        for h in range(NH):
            for wdx in range(NW):
                wb, wq = wdx // 4, wdx % 4
                bk = (h * NW + wdx) % 4
                if wq == 3:
                    E("scalar", lambda se, wb=wb, h=h: se.activation(
                        stg[0:RPW, :],
                        swide[96:96 + RPW,
                              wb * 2048 + h * 128:
                              wb * 2048 + (h + 1) * 128],
                        ACT.Copy),
                      wait=[(c_ve, epp_ve), (c_te, None)], inc=[c_se])
                    E("tensor", lambda te, bk=bk: te.transpose(
                        pb[bk][0:128, 0:RPW], stg[0:RPW, :],
                        ident_f[0:RPW, 0:RPW]),
                      wait=[(c_se, None), (c_ve, ek_ev[bk])], inc=[c_te])
                else:
                    E("tensor", lambda te, wb=wb, wq=wq, h=h, bk=bk:
                      te.transpose(
                          pb[bk][0:128, 0:RPW],
                          swide[32 * wq:32 * wq + RPW,
                                wb * 2048 + h * 128:
                                wb * 2048 + (h + 1) * 128],
                          ident_f[32 * wq:32 * wq + RPW,
                                  32 * wq:32 * wq + RPW]),
                      wait=[(c_ve, max(epp_ve, ek_ev[bk]))], inc=[c_te])
                tv = c_te.val
                E("vector", lambda ve, h=h, wdx=wdx, bk=bk: ve.tensor_copy(
                    ekml[wdx][:, h * RPW:h * RPW + RPW],
                    pb[bk][0:128, 0:RPW]),
                  wait=[(c_te, tv)], inc=[c_ve])
                ek_ev[bk] = c_ve.val
            for wi_ in range(8 * h, 8 * h + 8):
                emit_w2(wi_, thr_ve)
        wpark_ctr = stream_ctr("wp0")
        wpark_st = wpark_ctr.val
        wpark_ctr2 = stream_ctr("wp1")
        wpark_st2 = wpark_ctr2.val
        ekm_ve = c_ve.val
        for buf in (ep_m, ep_em, ep_den, ep_dinv, ep_t, stg):
            AR.free(buf)
        if debug and "d_ekm" in dbg:
            for wdx in range(NW):
                st(dbg["d_ekm"][:, wdx * NH * RPW:(wdx + 1) * NH * RPW],
                   ekml[wdx][:, :], wait=[(c_ve, ekm_ve)])
            dbar()
        if phases <= 2:
            _finish_stub(E, AR, c_ve, c_st, y_out, biasr,
                         [(c_te, None), (c_ve, None), (c_se, None)])
            plan.run(ent(nc.Block()))
            return nc, dram, dbg

        dbar()
        for buf in [swide, qc, ep_a, ep_cse, maskb_s, mask01_s,
                    wblk[0], wblk[1]] + t1w + wintl:
            AR.free(buf)
        NWB = 5
        shrow = AR.alloc(128, F32)
        csumT = AR.alloc(16, F32)
        svbT = AR.alloc(32, F32)

        E("vector", lambda ve: ve.memset(shrow[:, 0:128], 0.0),
          wait=[(c_ve, csum_ve)], inc=[c_ve])
        shz = c_ve.val
        _, rowv = ld(shrow[0:16, 0:128], AP(csum.tensor, csum.offset,
                                            [[csum.ap[0][0], 1],
                                             [128, 16], [1, 128]]),
                     q="gpsimd", wait=[(c_ve, shz)], stream="rowT")
        E("tensor", lambda te: te.transpose(
            pb[6][0:128, 0:128], shrow[0:128, 0:128], ident_f[:, :]),
          wait=[(streams["rowT"], rowv)], inc=[c_te])
        E("vector", lambda ve: ve.tensor_copy(csumT[:, 0:16],
                                              pb[6][0:128, 0:16]),
          wait=[(c_te, None)], inc=[c_ve])
        csumT_ve = c_ve.val
        AR.free(csum)
        AR.free(onesr)
        wvb = [AR.alloc(D, F32) for _ in range(2)]
        wob = wvb
        xwin = [AR.alloc(D, F32)]
        svrow = AR.alloc(D, F32)
        ctxc = AR.alloc(NH * B, F32)
        gbl = [AR.alloc(4 * NH * RPW, F32) for _ in range(NCH)]

        E("vector", lambda ve: ve.memset(ctxc[:, :], 0.0), inc=[c_ve])
        ctxz_ve = c_ve.val
        ctx_bank = {2: (c_ve, 0), 3: (c_ve, 0)}
        sv_ve = 0
        for bi, (w0, nwn) in enumerate(((0, 4), (4, 4), (8, 2))):
            for wi in range(nwn):
                wdx = w0 + wi
                bank = wdx % 2
                k0 = wdx * WSTR
                kn = min(128, T - k0)
                vc, vv = ld(xwin[0][0:kn, :], xtm[k0:k0 + kn, :],
                            q="gpsimd", wait=[(c_te, None)], stream="xw")
                for cc in range(NCH):
                    E("tensor", lambda te, cc=cc, wdx=wdx, bank=bank:
                      te.matmul(
                          pb[bank][:, 0:NH * RPW],
                          xwin[0][:, cc * 128:(cc + 1) * 128],
                          ekml[wdx][:, :],
                          start=True, stop=True),
                      wait=([(vc, vv), (c_ve, ekm_ve), (c_ve, None)]
                            if cc == 0 else [(c_ve, None)]),
                      inc=[c_te])
                    E("vector", lambda ve, cc=cc, wi=wi, bank=bank,
                        nwn=nwn: ve.tensor_copy(
                          apx(gbl[cc], wi * RPW,
                              [[nwn * RPW, NH], [1, RPW]]),
                          pb[bank][:, 0:NH * RPW]),
                      wait=[(c_te, None)], inc=[c_ve])
            gb_ve = c_ve.val
            cbw = [ctx_bank[2], ctx_bank[3]]
            for cc in range(NCH):
                par = cc % 2
                vc, vv = ld(wvb[par][:, :],
                            wv_d[cc * 128:(cc + 1) * 128, :],
                            wait=[(c_te, None)], stream=f"wv{par}")
                if bi == 0:
                    for ns in range(4):
                        E("tensor", lambda te, cc=cc, ns=ns, par=par:
                          te.matmul(
                              pb[4 + ns][0:1, 0:512], csumT[:, cc:cc + 1],
                              wvb[par][:, ns * 512:(ns + 1) * 512],
                              start=(cc == 0), stop=False),
                          wait=([(c_ve, csumT_ve), (vc, vv)]
                                if ns == 0 else ()))
                for h in range(NH):
                    hb = 2 + h // 8
                    hc = (h % 8) * nwn * RPW
                    E("tensor", lambda te, cc=cc, h=h, hb=hb, hc=hc,
                        par=par, nwn=nwn: te.matmul(
                          pb[hb][:, hc:hc + nwn * RPW],
                          wvb[par][:, h * 128:(h + 1) * 128],
                          gbl[cc][:, h * nwn * RPW:(h + 1) * nwn * RPW],
                          start=(cc == 0 and h % 8 == 0),
                          stop=(cc == 15 and h % 8 == 7)),
                      wait=((cbw + [(vc, vv), (c_ve, gb_ve)])
                            if (cc == 0 and h == 0) else
                            ([(vc, vv)] if h == 0 else ())),
                      inc=([c_te] if h == NH - 1 else []))
            cmm = c_te.val
            if bi == 0:
                for ns in range(4):
                    E("tensor", lambda te, ns=ns: te.matmul(
                        pb[4 + ns][0:1, 0:512], sconst[64:65, 0:1],
                        biasr[64:65, ns * 512:(ns + 1) * 512],
                        start=False, stop=True),
                      inc=([c_te] if ns == 3 else []))
                svmm = c_te.val
                for ns in range(4):
                    E("vector", lambda ve, ns=ns: ve.tensor_copy(
                        svrow[0:1, ns * 512:(ns + 1) * 512],
                        pb[4 + ns][0:1, 0:512]),
                      wait=[(c_te, svmm)], inc=[c_ve])
                sv_ve = c_ve.val
                dstore("d_sv", svrow[0:1, :], wait=[(c_ve, sv_ve)])
            ecnt = min(B - RPW * w0, nwn * RPW)
            pbv2, pbv3 = pb[2].ap(), pb[3].ap()
            for half, pbv in ((0, pbv2), (1, pbv3)):
                E("vector", lambda ve, w0=w0, half=half, ecnt=ecnt,
                    pbv=pbv, nwn=nwn: ve.tensor_copy(
                      apx(ctxc, (half * 8) * B + RPW * w0,
                          [[B, 8], [1, ecnt]]),
                      AP(pbv.tensor, pbv.offset,
                         [list(pbv.ap[0]), [nwn * RPW, 8], [1, ecnt]])),
                  wait=[(c_te, cmm), (c_ve, ctxz_ve)], inc=[c_ve])
                ctx_bank[2 + half] = (c_ve, c_ve.val)
            if bi == 0:
                dstore("d_ctx0", ctxc[:, :], wait=[(c_ve, c_ve.val)])
        ctx_ve = c_ve.val
        dstore("d_ctx", ctxc[:, :], wait=[(c_ve, ctx_ve)])

        _, rowv = ld(shrow[0:16, 0:128], AP(svrow.tensor, svrow.offset,
                                            [[svrow.ap[0][0], 1],
                                             [128, 16], [1, 128]]),
                     q="gpsimd", wait=[(c_ve, sv_ve), (c_te, None)],
                     stream="rowT")
        E("tensor", lambda te: te.transpose(
            pb[6][0:128, 0:128], shrow[0:128, 0:128], ident_f[:, :]),
          wait=[(streams["rowT"], rowv), (c_te, None)], inc=[c_te])
        E("vector", lambda ve: ve.tensor_copy(
            apx(svbT, 0, [[2, 16]]), pb[6][0:128, 0:16]),
          wait=[(c_te, None)], inc=[c_ve])
        svT_ve = c_ve.val
        _, rowv = ld(shrow[0:16, 0:128],
                     AP(biasr.tensor, biasr.offset + 64 * biasr.ap[0][0],
                        [[biasr.ap[0][0], 1], [128, 16], [1, 128]]),
                     q="gpsimd", wait=[(c_ve, svT_ve)], stream="rowT")
        E("tensor", lambda te: te.transpose(
            pb[6][0:128, 0:128], shrow[0:128, 0:128], ident_f[:, :]),
          wait=[(streams["rowT"], rowv), (c_te, None)], inc=[c_te])
        E("vector", lambda ve: ve.tensor_copy(
            apx(svbT, 1, [[2, 16]]), pb[6][0:128, 0:16]),
          wait=[(c_te, None)], inc=[c_ve])
        bvT_ve = c_ve.val

        for buf in [xwin[0], svrow, shrow] + gbl:
            AR.free(buf)
        uprime = AR.alloc(D, F32)
        bandtm = AR.alloc(D, F32)
        aprime = AR.alloc(T, F32)
        ustg = AR.alloc(D, F32)

        E("vector", lambda ve: ve.memset(uprime[:, :], 0.0), inc=[c_ve])
        upz = c_ve.val
        for cc in range(NCH):
            par = cc % 2
            vc, vv = ld(wob[par][:, :], wo_d[cc * 128:(cc + 1) * 128, :],
                        wait=[(c_te, None)],
                        stream=f"wo{par}")
            for ns in range(4):
                E("tensor", lambda te, cc=cc, ns=ns, par=par: te.matmul(
                    pb[ns][0:B, 0:512], ctxc[:, cc * B:(cc + 1) * B],
                    wob[par][:, ns * 512:(ns + 1) * 512],
                    start=(cc == 0), stop=(cc == 15)),
                  wait=([(vc, vv), (c_ve, ctx_ve), (c_ve, None)]
                        if ns == 0 else ()),
                  inc=([c_te] if ns == 3 else []))
            h = cc
            for ns in range(4):
                E("tensor", lambda te, h=h, ns=ns, par=par: te.matmul(
                      pb[4][0:2, 0:512], svbT[:, 2 * h:2 * h + 2],
                      wob[par][:, ns * 512:(ns + 1) * 512],
                      start=True, stop=True),
                  wait=([(vc, vv), (c_se, None), (c_ve, bvT_ve)]
                        if ns == 0 else [(c_se, None)]),
                  inc=[c_te])
                umm = c_te.val
                E("scalar", lambda se, ns=ns: se.activation(
                    ustg[0:2, ns * 512:(ns + 1) * 512],
                    pb[4][0:2, 0:512], ACT.Copy),
                  wait=[(c_te, umm)] +
                       ([(streams["upr"], None)]
                        if (ns == 0 and "upr" in streams) else []),
                  inc=[c_se])
            usev = c_se.val
            for ui in range(2):
                tgt = h if ui == 0 else 32 + h
                E("gpsimd", lambda e, tgt=tgt, ui=ui: e.dma_start(
                    out=uprime[tgt:tgt + 1, :], in_=ustg[ui:ui + 1, :]),
                  wait=[(c_se, usev), (c_ve, upz)],
                  inc=[stream_ctr("upr")], incn=16)
        bmm = c_te.val
        for ns in range(4):
            E("vector", lambda ve, ns=ns: ve.tensor_copy(
                bandtm[0:B, ns * 512:(ns + 1) * 512], pb[ns][0:B, 0:512]),
              wait=[(c_te, bmm)], inc=[c_ve])
        bandtm_ve = c_ve.val
        E("scalar", lambda se: se.activation(
            uprime[64:65, :], biasr[96:97, :], ACT.Copy),
          wait=[(c_ld, consts_ld), (c_ve, upz)], inc=[c_se])
        up_se = c_se.val
        up_dma = stream_ctr("upr").val

        E("vector", lambda ve: ve.memset(aprime[:, :], 0.0),
          wait=[(c_ve, None)], inc=[c_ve])
        E("vector", lambda ve: ve.memset(aprime[0:16, :], 1.0 / S),
          wait=[(c_ve, None)], inc=[c_ve])
        E("vector", lambda ve: ve.memset(aprime[64:65, :], 1.0),
          inc=[c_ve])
        apz = c_ve.val
        for wdx in range(NW):
            wb, wq = wdx // 4, wdx % 4
            cnt = min(RPW, (T - WSTR * wdx + 9) // 10)
            for rbase, srct in ((0, a16t), (32, cset)):
                E("vector", lambda ve, wdx=wdx, wb=wb, wq=wq, cnt=cnt,
                    rbase=rbase, srct=srct: ve.tensor_copy(
                      aprime[rbase:rbase + 16,
                             WSTR * wdx: WSTR * wdx + 10 * cnt]
                      .rearrange("p (b s) -> p b s", s=10)[:, :, 0],
                      srct[0:16, wb * 128 + 32 * wq:
                           wb * 128 + 32 * wq + cnt]),
                  wait=[(c_ve, apz), (c_ve, a16_ve)], inc=[c_ve])
        ap_ve = c_ve.val
        dstore("d_u", uprime[0:65, :], wait=[(c_se, up_se),
                                             (stream_ctr("upr"), None)])
        dstore("d_a", aprime[0:65, :], wait=[(c_ve, ap_ve)])

        for buf in [ustg, a16t, cset, csumT, svbT,
                    wvb[0], wvb[1], ctxc, biasr] + ekml:
            AR.free(buf)
        selr = AR.alloc(NSUB * 128, F32, tag="selr")
        zfw = [(c_te, None), (c_ve, None), (c_se, None)] + \
            [(cc, None) for cc in streams.values()]
        ld(selr[0:16, :], sel_d, q="sync", wait=list(zfw))
        sel_ld = c_ld.val
        zsub = [AR.alloc(D, F32, tag="zsub") for _ in range(2)]
        xsub = [AR.alloc(D, F32, tag="xsub") for _ in range(2)]
        bsel = [AR.alloc(D, F32, tag="bsel") for _ in range(2)]
        zb_ev = []
        bs_te = {0: 0, 1: 0}
        zs_ve = {0: 0, 1: 0}
        for si in range(NSUB):
            par = si % 2
            r0, rn = SUBS[si]
            i0 = -(-r0 // 10)
            i1 = -(-(r0 + rn) // 10)
            cnt = i1 - i0
            fw = list(zfw) if si < 2 else []
            vx, vxv = ld(xsub[par][0:rn, :], xtm[r0:r0 + rn, :],
                         q="gpsimd",
                         wait=fw + ([(c_ve, zs_ve[par])] if si >= 2
                                    else []),
                         stream=f"xz{par}")
            vb, vbv = ld(bsel[par][0:cnt, :], bandtm[i0:i0 + cnt, :],
                         q="gpsimd",
                         wait=fw + [(c_ve, bandtm_ve)]
                         + ([(c_te, bs_te[par])] if si >= 2 else []),
                         stream=f"bz{par}")
            for ns in range(4):
                bank = 5 + (si * 4 + ns) % 3
                wb_ = [(c_se, up_se), (c_ve, ap_ve),
                       (stream_ctr("upr"), up_dma)]
                if si * 4 + ns >= 3:
                    wb_.append((c_ve, zb_ev[-3]))
                E("tensor", lambda te, ns=ns, r0=r0, rn=rn, bank=bank:
                  te.matmul(
                      pb[bank][0:rn, 0:512],
                      aprime[0:65, r0:r0 + rn],
                      uprime[0:65, ns * 512:(ns + 1) * 512],
                      start=True, stop=False),
                  wait=wb_, inc=[c_te])
                E("tensor", lambda te, ns=ns, rn=rn, bank=bank, si=si,
                    cnt=cnt, par=par: te.matmul(
                      pb[bank][0:rn, 0:512],
                      selr[0:cnt, si * 128:si * 128 + rn],
                      bsel[par][0:cnt, ns * 512:(ns + 1) * 512],
                      start=False, stop=True),
                  wait=([(vb, vbv), (c_ld, sel_ld)] if ns == 0 else ()),
                  inc=[c_te])
                zmm = c_te.val
                E("vector", lambda ve, ns=ns, rn=rn, bank=bank, par=par:
                  ve.tensor_tensor(
                      zsub[par][0:rn, ns * 512:(ns + 1) * 512],
                      xsub[par][0:rn, ns * 512:(ns + 1) * 512],
                      pb[bank][0:rn, 0:512], AX.add),
                  wait=[(c_te, zmm), (vx, vxv), (c_se, None),
                        (stream_ctr(f"zg{par}"), None)], inc=[c_ve])
                zb_ev.append(c_ve.val)
            bs_te[par] = c_te.val
            zs_ve[par] = c_ve.val
            ld(z_park[si], zsub[par][:, :], q="gpsimd",
               wait=[(c_ve, zs_ve[par]),
                     (stream_ctr(f"zg{par}"), None)],
               stream=f"zg{par}")
            if debug and "d_z" in dbg:
                st(dbg["d_z"][si], zsub[par][:, :],
                   wait=[(c_ve, zs_ve[par])])
                dbar()

        if phases <= 4:
            _finish_stub(E, AR, c_ve, c_st, y_out, zsub[0],
                         [(c_te, None), (c_ve, None), (c_se, None)])
            plan.run(ent(nc.Block()))
            return nc, dram, dbg
        dbar()
        for buf in (uprime, aprime, bandtm,
                    zsub[0], zsub[1], xsub[0], xsub[1], bsel[0], bsel[1],
                    selr):
            AR.free(buf)

        xqfm = AR.alloc(NCH * T, BF16, tag="xqfm")
        s1t = AR.alloc(16, F32, tag="s1t")
        o2t = AR.alloc(16, F32, tag="o2t")
        zst = AR.alloc(16, F32, tag="zst")
        sq = AR.alloc(D, F32, tag="sq")
        y1s = AR.alloc(D, F32, tag="y1s")
        zld = [AR.alloc(D, F32, tag="zld") for _ in range(2)]
        pfm_h = AR.alloc(D, BF16, tag="pfm_h")
        pfm_l = AR.alloc(D, BF16, tag="pfm_l")
        g1a = AR.alloc(D, F32, tag="g1a")
        g1b = AR.alloc(D, F32, tag="g1b")

        bankst = {b: (c_ve, 0) for b in range(8)}
        bank_i = [0]

        def bank():
            b = bank_i[0] % 8
            bank_i[0] += 1
            return b, [bankst[b]]

        def bank_done(b, cv=None):
            bankst[b] = (c_ve, c_ve.val) if cv is None else cv

        def ln_stats(zin, rn, prev_w):
            E("vector", lambda ve: ve.tensor_reduce(
                zst[0:rn, 2:3], zin, AXL.X, AX.add),
              wait=prev_w, inc=[c_ve])
            E("scalar", lambda se: se.activation(sq[0:rn, :], zin,
                                                 ACT.Square),
              wait=prev_w, inc=[c_se])
            E("vector", lambda ve: ve.tensor_reduce(
                zst[0:rn, 3:4], sq[0:rn, :], AXL.X, AX.add),
              wait=[(c_se, None)], inc=[c_ve])
            E("vector", lambda ve: ve.tensor_scalar(
                zst[0:rn, 0:1], zst[0:rn, 2:3], 1.0 / D, None, AX.mult),
              wait=[(c_ve, None)], inc=[c_ve])
            E("vector", lambda ve: ve.tensor_scalar(
                zst[0:rn, 3:4], zst[0:rn, 3:4], 1.0 / D, None, AX.mult),
              wait=[(c_ve, None)], inc=[c_ve])
            E("vector", lambda ve: ve.tensor_tensor(
                zst[0:rn, 4:5], zst[0:rn, 0:1], zst[0:rn, 0:1], AX.mult),
              wait=[(c_ve, None)], inc=[c_ve])
            E("vector", lambda ve: ve.scalar_tensor_tensor(
                zst[0:rn, 4:5], zst[0:rn, 3:4], 1.0, zst[0:rn, 4:5],
                AX.bypass, AX.subtract),
              wait=[(c_ve, None)], inc=[c_ve])
            E("vector", lambda ve: ve.tensor_scalar(
                zst[0:rn, 4:5], zst[0:rn, 4:5], 1e-5, None, AX.add),
              wait=[(c_ve, None)], inc=[c_ve])
            E("scalar", lambda se: se.activation(
                zst[0:rn, 5:6], zst[0:rn, 4:5], ACT.Sqrt),
              wait=[(c_ve, None)], inc=[c_se])
            E("vector", lambda ve: ve.reciprocal(
                zst[0:rn, 6:7], zst[0:rn, 5:6]),
              wait=[(c_se, None)], inc=[c_ve])
            E("vector", lambda ve: ve.tensor_tensor(
                zst[0:rn, 6:7], zst[0:rn, 4:5], zst[0:rn, 6:7], AX.mult),
              wait=[(c_ve, None)], inc=[c_ve])
            E("vector", lambda ve: ve.tensor_tensor(
                zst[0:rn, 6:7], zst[0:rn, 6:7], zst[0:rn, 5:6], AX.add),
              wait=[(c_ve, None)], inc=[c_ve])
            E("vector", lambda ve: ve.tensor_scalar(
                zst[0:rn, 6:7], zst[0:rn, 6:7], 0.5, None, AX.mult),
              wait=[(c_ve, None)], inc=[c_ve])
            E("vector", lambda ve: ve.reciprocal(
                zst[0:rn, 1:2], zst[0:rn, 6:7]),
              wait=[(c_ve, None)], inc=[c_ve])
            return c_ve.val

        def recipn(dst, srcv, tmp, rn, w=()):
            E("vector", lambda ve: ve.reciprocal(dst, srcv),
              wait=list(w), inc=[c_ve])
            E("vector", lambda ve: ve.tensor_tensor(
                tmp, srcv, dst, AX.mult), wait=[(c_ve, None)], inc=[c_ve])
            E("vector", lambda ve: ve.tensor_scalar(
                tmp, tmp, -1.0, 2.0, AX.mult, AX.add),
              wait=[(c_ve, None)], inc=[c_ve])
            E("vector", lambda ve: ve.tensor_tensor(
                dst, dst, tmp, AX.mult), wait=[(c_ve, None)], inc=[c_ve])
            return c_ve.val

        def ladder(u_ap, w_ap, t_ap, q_ap, wait0):
            E("vector", lambda ve: ve.tensor_scalar(
                t_ap, u_ap, 128.0, -128.0, AX.is_ge, AX.mult),
              wait=list(wait0), inc=[c_ve])
            E("vector", lambda ve: ve.tensor_tensor(
                w_ap, u_ap, t_ap, AX.add),
              wait=[(c_ve, None)], inc=[c_ve])
            for k in (64.0, 32.0, 16.0, 8.0, 4.0, 2.0, 1.0):
                E("vector", lambda ve, k=k: ve.tensor_scalar(
                    t_ap, w_ap, k, -k, AX.is_ge, AX.mult),
                  wait=[(c_ve, None)], inc=[c_ve])
                E("vector", lambda ve: ve.tensor_tensor(
                    w_ap, w_ap, t_ap, AX.add),
                  wait=[(c_ve, None)], inc=[c_ve])
            E("vector", lambda ve: ve.tensor_tensor(
                t_ap, u_ap, w_ap, AX.subtract),
              wait=[(c_ve, None)], inc=[c_ve])
            E("vector", lambda ve: ve.tensor_scalar(
                q_ap, t_ap, 128.0, None, AX.subtract),
              wait=[(c_ve, None)], inc=[c_ve])
            return c_ve.val

        xq_evs = []
        for si in range(NSUB):
            par = si % 2
            r0, rn = SUBS[si]
            vz, vzv = ld(zld[par][:, :], z_park[si], q="gpsimd",
                         wait=[(stream_ctr(f"zg{par}"), 16 * (si // 2 + 1)),
                               (c_ve, None)],
                         stream=f"zl{par}")
            zin = zld[par][0:rn, :]
            prev_w = [(vz, vzv), (c_ve, None), (c_se, None)]
            stats_ve = ln_stats(zin, rn, prev_w)
            E("vector", lambda ve, zin=zin, rn=rn: ve.tensor_scalar(
                y1s[0:rn, :], zin, zst[0:rn, 0:1], zst[0:rn, 1:2],
                AX.subtract, AX.mult),
              wait=[(c_ve, stats_ve), (stream_ctr("y1g"), None),
                    (c_st, None)], inc=[c_ve])
            y1_ve = c_ve.val
            ld(y1_park[si], y1s[:, :], q="gpsimd",
               wait=[(c_ve, y1_ve)], stream="y1g")
            if debug and "d_y1" in dbg:
                st(dbg["d_y1"][si], y1s[:, :], wait=[(c_ve, y1_ve)])
                dbar()
            for cc in range(NCH):
                b, bw = bank()
                E("tensor", lambda te, cc=cc, b=b: te.transpose(
                    pb[b][0:128, 0:128],
                    y1s[0:128, cc * 128:(cc + 1) * 128], ident_f[:, :]),
                  wait=bw + [(c_ve, y1_ve)], inc=[c_te])
                tv = c_te.val
                E("scalar", lambda se, cc=cc, b=b: se.activation(
                    pfm_h[:, cc * 128:(cc + 1) * 128],
                    pb[b][0:128, 0:128], ACT.Copy),
                  wait=[(c_te, tv)], inc=[c_se])
                E("vector", lambda ve, cc=cc, b=b:
                  ve.scalar_tensor_tensor(
                      pfm_l[:, cc * 128:(cc + 1) * 128],
                      pb[b][0:128, 0:128], 1.0,
                      pfm_h[:, cc * 128:(cc + 1) * 128],
                      AX.bypass, AX.subtract),
                  wait=[(c_se, None)], inc=[c_ve])
                bank_done(b)
            pfm_ve = c_ve.val
            ga_ev = []
            for g in range(4):
                b, bw = bank()
                n_mm = 0
                for j in range(4):
                    for tsrc in (pfm_h, pfm_l):
                        E("tensor", lambda te, g=g, j=j, tsrc=tsrc, b=b,
                            rn=rn, n_mm=n_mm: te.matmul(
                              pb[b][:, 0:512],
                              tsrc[:, (4 * g + j) * 128:
                                   (4 * g + j + 1) * 128],
                              hh4_b[:, j * 512:(j + 1) * 512],
                              start=(n_mm == 0), stop=(n_mm == 7)),
                          wait=(bw + [(c_ve, pfm_ve), (c_se, None)]
                                if n_mm == 0 else ()),
                          inc=([c_te] if n_mm == 7 else []))
                        n_mm += 1
                mmv = c_te.val
                E("scalar", lambda se, g=g, b=b, rn=rn: se.activation(
                    g1a[0:rn, g * 512:(g + 1) * 512], pb[b][0:rn, 0:512],
                    ACT.Copy),
                  wait=[(c_te, mmv)], inc=[c_se])
                bank_done(b, (c_se, c_se.val))
            E("vector", lambda ve, rn=rn: ve.tensor_tensor(
                g1b[0:rn, 0:512], g1a[0:rn, 0:512], g1a[0:rn, 512:1024],
                AX.add), wait=[(c_ve, None), (c_se, None)], inc=[c_ve])
            E("vector", lambda ve, rn=rn: ve.tensor_tensor(
                g1b[0:rn, 512:1024], g1a[0:rn, 0:512], g1a[0:rn, 512:1024],
                AX.subtract), inc=[c_ve])
            E("vector", lambda ve, rn=rn: ve.tensor_tensor(
                g1b[0:rn, 1024:1536], g1a[0:rn, 1024:1536],
                g1a[0:rn, 1536:2048], AX.add), inc=[c_ve])
            E("vector", lambda ve, rn=rn: ve.tensor_tensor(
                g1b[0:rn, 1536:2048], g1a[0:rn, 1024:1536],
                g1a[0:rn, 1536:2048], AX.subtract), inc=[c_ve])
            bst1 = c_ve.val
            E("vector", lambda ve, rn=rn: ve.tensor_tensor(
                g1a[0:rn, 0:512], g1b[0:rn, 0:512], g1b[0:rn, 1024:1536],
                AX.add), wait=[(c_ve, bst1)], inc=[c_ve])
            E("vector", lambda ve, rn=rn: ve.tensor_tensor(
                g1a[0:rn, 512:1024], g1b[0:rn, 512:1024],
                g1b[0:rn, 1536:2048], AX.add), inc=[c_ve])
            E("vector", lambda ve, rn=rn: ve.tensor_tensor(
                g1a[0:rn, 1024:1536], g1b[0:rn, 0:512],
                g1b[0:rn, 1024:1536], AX.subtract), inc=[c_ve])
            E("vector", lambda ve, rn=rn: ve.tensor_tensor(
                g1a[0:rn, 1536:2048], g1b[0:rn, 512:1024],
                g1b[0:rn, 1536:2048], AX.subtract), inc=[c_ve])
            g1_ve = c_ve.val
            E("vector", lambda ve, rn=rn: ve.tensor_reduce(
                zst[0:rn, 8:9], g1a[0:rn, :], AXL.X, AX.max,
                apply_absolute_value=True),
              wait=[(c_ve, g1_ve), (c_se, None)], inc=[c_ve])
            E("vector", lambda ve, rn=rn: ve.tensor_scalar(
                zst[0:rn, 8:9], zst[0:rn, 8:9], C1, 1e-5, AX.mult, AX.max),
              wait=[(c_ve, None)], inc=[c_ve])
            recipn(zst[0:rn, 10:11], zst[0:rn, 8:9], zst[0:rn, 11:12], rn,
                   w=[(c_ve, None)])
            E("vector", lambda ve, rn=rn, si=si: ve.tensor_scalar(
                s1t[0:rn, si:si + 1], zst[0:rn, 10:11], 127.0, None,
                AX.mult),
              wait=[(c_ve, None)], inc=[c_ve])
            E("vector", lambda ve, rn=rn, si=si: ve.tensor_scalar(
                zst[0:rn, 9:10], s1t[0:rn, si:si + 1], -C1, None, AX.mult),
              wait=[(c_ve, None)], inc=[c_ve])
            s1_ve = c_ve.val
            E("scalar", lambda se, rn=rn: se.activation(
                sq[0:rn, :], g1a[0:rn, :], ACT.Copy, bias=128.5,
                scale=zst[0:rn, 9:10]),
              wait=[(c_ve, s1_ve)], inc=[c_se])
            ladder(sq[0:rn, :], y1s[0:rn, :], zld[par][0:rn, :],
                   g1b[0:rn, :],
                   [(c_se, None), (stream_ctr("y1g"), None),
                    (stream_ctr(f"zl{par}"), None)])
            xqtm_ve = c_ve.val
            for cc in range(NCH):
                b, bw = bank()
                E("tensor", lambda te, cc=cc, b=b: te.transpose(
                    pb[b][0:128, 0:128],
                    g1b[0:128, cc * 128:(cc + 1) * 128], ident_f[:, :]),
                  wait=bw + [(c_ve, xqtm_ve)], inc=[c_te])
                tv = c_te.val
                E("scalar", lambda se, cc=cc, b=b, r0=r0, rn=rn:
                  se.activation(xqfm[:, cc * T + r0: cc * T + r0 + rn],
                                pb[b][0:128, 0:rn], ACT.Copy),
                  wait=[(c_te, tv)], inc=[c_se])
                bank_done(b, (c_se, c_se.val))
            xq_evs.append(c_ve.val)
        xq_ve = c_ve.val
        dstore("d_xq", xqfm[:, :].bitcast(F32)[:, 0:NCH * T // 2],
               wait=[(c_ve, xq_ve)])
        dstore("d_s1", s1t[:, 0:NSUB], wait=[(c_ve, xq_ve)])

        dbar()
        for buf in (y1s, zld[0], zld[1], pfm_h, pfm_l, g1a, g1b):
            AR.free(buf)


        QT = [(0, 256, (0, 1)), (256, 256, (2, 3)), (512, 256, (4, 5)),
              (768, 272, (6, 7, 8))]
        g2a = AR.alloc(DF, F32, tag="g2a")
        g2b = AR.alloc(DF, F32, tag="g2b")
        h2h = AR.alloc(NFCH * 272, BF16, tag="h2h")
        h2l = AR.alloc(NFCH * 272, BF16, tag="h2l")
        hqst = [AR.alloc(DF, BF16, tag="hqst")]
        wq2 = AR.alloc(DF, F32, tag="wq2")
        tq2 = AR.alloc(DF, F32, tag="tq2")
        wsl = [AR.alloc(NCH * 128, BF16, tag="wsl") for _ in range(2)]
        rfp = AR.alloc(272, F32, tag="rfp")
        pfp = AR.alloc(272, F32, tag="pfp")
        qst = AR.alloc(16, F32, tag="qst")
        h2_ve = {"v": 0}
        for qi, (o4, n4, qsubs) in enumerate(QT):
            h2evs = []
            for fc in range(NFCH):
                par = fc % 2
                vc, vv = ld(wsl[par][:, :]
                            .rearrange("p (k f) -> p k f", f=128),
                            wup_int[:, :, fc * 128:(fc + 1) * 128]
                            .transpose([1, 0, 2]),
                            q="sync",
                            wait=([(wpark_ctr, wpark_st),
                               (wpark_ctr2, wpark_st2), (c_te, None)]
                                  if (qi == 0 and fc < 2)
                                  else [(c_te, None)]),
                            stream=f"us{par}")
                b, bw = bank()
                for kk in range(NCH):
                    E("tensor", lambda te, fc=fc, kk=kk, b=b, par=par,
                        o4=o4, n4=n4: te.matmul(
                          pb[b][:, 0:n4],
                          wsl[par][:, kk * 128:(kk + 1) * 128],
                          xqfm[:, kk * T + o4: kk * T + o4 + n4],
                          start=(kk == 0), stop=(kk == 15)),
                      wait=(bw + [(vc, vv), (c_ve, xq_ve)] if kk == 0
                            else ()),
                      inc=([c_te] if kk == 15 else []))
                hmm_v = c_te.val
                E("scalar", lambda se, b=b, n4=n4: se.activation(
                    rfp[:, 0:n4], pb[b][:, 0:n4], ACT.Relu),
                  wait=[(c_te, hmm_v), (c_ve, None)], inc=[c_se])
                bank_done(b, (c_se, c_se.val))
                E("vector", lambda ve, n4=n4: ve.tensor_tensor(
                    pfp[:, 0:n4], rfp[:, 0:n4], rfp[:, 0:n4], AX.mult),
                  wait=[(c_se, None)], inc=[c_ve])
                E("scalar", lambda se, fc=fc, n4=n4: se.activation(
                    h2h[:, fc * n4: fc * n4 + n4], pfp[:, 0:n4], ACT.Copy),
                  wait=[(c_ve, None)], inc=[c_se])
                E("vector", lambda ve, fc=fc, n4=n4: ve.scalar_tensor_tensor(
                    h2l[:, fc * n4: fc * n4 + n4], pfp[:, 0:n4], 1.0,
                    h2h[:, fc * n4: fc * n4 + n4], AX.bypass, AX.subtract),
                  wait=[(c_se, None)], inc=[c_ve])
                h2evs.append(c_ve.val)
            h2_ready = c_ve.val
            if debug and "d_h" in dbg and qi == 0:
                st(dbg["d_h"][:, 0:NFCH * 136],
                   h2h[:, 0:NFCH * 272].bitcast(F32),
                   wait=[(c_ve, h2_ready)])
                dbar()
            for sj, si in enumerate(qsubs):
                r0, rn = SUBS[si]
                so = r0 - o4
                ga_done = []
                for g in range(8):
                    b, bw = bank()
                    n_mm = 0
                    for j in range(4):
                        for tsrc in (h2h, h2l):
                            E("tensor", lambda te, g=g, j=j, tsrc=tsrc,
                                b=b, rn=rn, so=so, n4=n4, n_mm=n_mm:
                              te.matmul(
                                  pb[b][0:rn, 0:512],
                                  tsrc[:, (4 * g + j) * n4 + so:
                                       (4 * g + j) * n4 + so + rn],
                                  hh4_b[:, j * 512:(j + 1) * 512],
                                  start=(n_mm == 0), stop=(n_mm == 7)),
                              wait=(bw + [(c_ve, h2_ready)] if n_mm == 0
                                    else ()),
                              inc=([c_te] if n_mm == 7 else []))
                            n_mm += 1
                    mmv = c_te.val
                    E("scalar", lambda se, g=g, b=b, rn=rn:
                      se.activation(
                          g2a[0:rn, g * 512:(g + 1) * 512],
                          pb[b][0:rn, 0:512], ACT.Copy),
                      wait=[(c_te, mmv)], inc=[c_se])
                    bank_done(b, (c_se, c_se.val))
                def bfly(dst, srca, srcb, op, rn, w=None):
                    E("vector", lambda ve: ve.tensor_tensor(
                        dst[0:rn, :], srca[0:rn, :], srcb[0:rn, :], op),
                      wait=(w or ()), inc=[c_ve])
                BL = [g2a[:, i * 512:(i + 1) * 512] for i in range(8)]
                CL = [g2b[:, i * 512:(i + 1) * 512] for i in range(8)]
                w0_ = [(c_ve, c_ve.val), (c_se, c_se.val)]
                for i in range(4):
                    bfly(CL[i], BL[i], BL[i + 4], AX.add, rn,
                         w0_ if i == 0 else None)
                    bfly(CL[i + 4], BL[i], BL[i + 4], AX.subtract, rn)
                wv_ = [(c_ve, c_ve.val)]
                for base in (0, 4):
                    for i in range(2):
                        bfly(BL[base + i], CL[base + i], CL[base + i + 2],
                             AX.add, rn, wv_ if (base + i) == 0 else None)
                        bfly(BL[base + i + 2], CL[base + i],
                             CL[base + i + 2], AX.subtract, rn)
                wv2_ = [(c_ve, c_ve.val)]
                for base in (0, 2, 4, 6):
                    bfly(CL[base], BL[base], BL[base + 1], AX.add, rn,
                         wv2_ if base == 0 else None)
                    bfly(CL[base + 1], BL[base], BL[base + 1], AX.subtract,
                         rn)
                g2_ve = c_ve.val
                E("vector", lambda ve, rn=rn: ve.tensor_reduce(
                    qst[0:rn, 0:1], g2b[0:rn, :], AXL.X, AX.max,
                    apply_absolute_value=True),
                  wait=[(c_ve, g2_ve), (c_se, None)], inc=[c_ve])
                recipn(qst[0:rn, 6:7], s1t[0:rn, si:si + 1],
                       qst[0:rn, 7:8], rn,
                       w=[(c_ve, None), (c_ve, thr_ve)])
                E("vector", lambda ve, rn=rn: ve.tensor_tensor(
                    qst[0:rn, 1:2], wscb[0:rn, 0:1], qst[0:rn, 6:7],
                    AX.mult), wait=[(c_ve, None)], inc=[c_ve])
                E("vector", lambda ve, rn=rn: ve.tensor_tensor(
                    qst[0:rn, 2:3], qst[0:rn, 1:2], qst[0:rn, 1:2],
                    AX.mult), wait=[(c_ve, None)], inc=[c_ve])
                E("vector", lambda ve, rn=rn: ve.tensor_scalar(
                    qst[0:rn, 2:3], qst[0:rn, 2:3], C2, None, AX.mult),
                  wait=[(c_ve, None)], inc=[c_ve])
                E("vector", lambda ve, rn=rn: ve.tensor_tensor(
                    qst[0:rn, 3:4], qst[0:rn, 0:1], qst[0:rn, 2:3],
                    AX.mult), wait=[(c_ve, None)], inc=[c_ve])
                E("vector", lambda ve, rn=rn: ve.tensor_scalar(
                    qst[0:rn, 3:4], qst[0:rn, 3:4], 1e-5, None, AX.max),
                  wait=[(c_ve, None)], inc=[c_ve])
                recipn(qst[0:rn, 6:7], qst[0:rn, 3:4], qst[0:rn, 7:8], rn,
                       w=[(c_ve, None)])
                E("vector", lambda ve, rn=rn: ve.tensor_scalar(
                    qst[0:rn, 4:5], qst[0:rn, 6:7], 127.0, None, AX.mult),
                  wait=[(c_ve, None)], inc=[c_ve])
                E("vector", lambda ve, rn=rn: ve.tensor_tensor(
                    qst[0:rn, 5:6], qst[0:rn, 4:5], qst[0:rn, 2:3],
                    AX.mult), wait=[(c_ve, None)], inc=[c_ve])
                E("vector", lambda ve, rn=rn: ve.tensor_scalar(
                    qst[0:rn, 5:6], qst[0:rn, 5:6], -1.0, None, AX.mult),
                  wait=[(c_ve, None)], inc=[c_ve])
                recipn(qst[0:rn, 6:7], qst[0:rn, 4:5], qst[0:rn, 7:8], rn,
                       w=[(c_ve, None)])
                E("vector", lambda ve, rn=rn, si=si: ve.tensor_tensor(
                    o2t[0:rn, si:si + 1], wscb[0:rn, 1:2], qst[0:rn, 6:7],
                    AX.mult), wait=[(c_ve, None)], inc=[c_ve])
                q2_ve = c_ve.val
                E("scalar", lambda se, rn=rn: se.activation(
                    g2a[0:rn, :], g2b[0:rn, :], ACT.Copy, bias=128.5,
                    scale=qst[0:rn, 5:6]),
                  wait=[(c_ve, q2_ve)], inc=[c_se])
                ladder(g2a[0:rn, :], wq2[0:rn, :], tq2[0:rn, :],
                       g2b[0:rn, :], [(c_se, None)])
                hq_ve = c_ve.val
                hpar = 0
                for fcc in range(NFCH):
                    b, bw = bank()
                    E("tensor", lambda te, fcc=fcc, b=b: te.transpose(
                        pb[b][0:128, 0:128],
                        g2b[0:128, fcc * 128:(fcc + 1) * 128],
                        ident_f[:, :]),
                      wait=bw + [(c_ve, hq_ve)]
                      + ([(stream_ctr(f"hqg{hpar}"), None)]
                         if fcc == 0 else []),
                      inc=[c_te])
                    tv = c_te.val
                    E("scalar", lambda se, fcc=fcc, b=b, rn=rn, hpar=hpar:
                      se.activation(
                          hqst[hpar][:, fcc * 128:fcc * 128 + rn],
                          pb[b][0:128, 0:rn], ACT.Copy),
                      wait=[(c_te, tv)], inc=[c_se])
                    bank_done(b, (c_se, c_se.val))
                E("gpsimd", lambda e, r0=r0, rn=rn, hpar=hpar:
                  e.dma_start(
                    out=hq_park[:, :, r0:r0 + rn].transpose([1, 0, 2]),
                    in_=hqst[hpar][:, :]
                    .rearrange("p (f c) -> p f c", c=128)[:, :, 0:rn]),
                  wait=[(c_ve, None), (c_se, None)],
                  inc=[stream_ctr(f"hqg{hpar}")], incn=16)
            h2_ve["v"] = c_ve.val
        hq_done = c_ve.val
        dstore("d_o2", o2t[:, 0:NSUB], wait=[(c_ve, hq_done)])

        dbar()
        for buf in (h2h, h2l, g2a, g2b, rfp, pfp, hqst[0],
                    wq2, tq2, xqfm, wsl[0], wsl[1]):
            AR.free(buf)

        ffnfm = AR.alloc(NCH * T, F32, tag="ffnfm")
        hqfm = AR.alloc(NFCH * T, BF16, tag="hqfm")
        wsl_d = [AR.alloc(NFCH * 128, BF16, tag="wsl_dn")
                 for _ in range(2)]
        hql_v = []
        for fc in range(NFCH):
            sk = f"hql{fc % 4}"
            _, hv = ld(hqfm[:, fc * T:(fc + 1) * T], hq_park[fc],
                       q="gpsimd",
                       wait=([(stream_ctr("hqg0"), None)] if fc < 4
                             else [(stream_ctr(sk), None)]),
                       stream=sk)
            hql_v.append((stream_ctr(sk), hv))
        dn_evs = []
        for dc in range(NCH):
            par = dc % 2
            vc, vv = ld(wsl_d[par][:, 0:NFCH * 128]
                        .rearrange("p (k f) -> p k f", f=128),
                        wdn_int[:, :, dc * 128:(dc + 1) * 128]
                        .transpose([1, 0, 2]),
                        q="sync",
                        wait=([(c_te, None)] if dc >= 2
                              else [(wpark_ctr, wpark_st),
                                    (wpark_ctr2, wpark_st2)]),
                        stream=f"us{par}")
            for gi, (g0, gn) in enumerate(TG):
                b, bw = bank()
                for fc in range(NFCH):
                    E("tensor", lambda te, dc=dc, fc=fc, b=b, par=par,
                        g0=g0, gn=gn: te.matmul(
                          pb[b][:, 0:gn],
                          wsl_d[par][:, fc * 128:(fc + 1) * 128],
                          hqfm[:, fc * T + g0: fc * T + g0 + gn],
                          start=(fc == 0), stop=(fc == 31)),
                      wait=((bw + [(vc, vv), (c_ve, hq_done)]
                             if fc == 0 else ())
                            if dc + gi > 0 else
                            (bw + [(vc, vv), (c_ve, hq_done),
                                   hql_v[fc]] if fc == 0
                             else [hql_v[fc]])),
                      inc=([c_te] if fc == 31 else []))
                dmm = c_te.val
                E("scalar", lambda se, dc=dc, g0=g0, gn=gn, b=b:
                  se.activation(ffnfm[:, dc * T + g0: dc * T + g0 + gn],
                                pb[b][:, 0:gn], ACT.Copy),
                  wait=[(c_te, dmm)], inc=[c_se])
                bank_done(b, (c_se, c_se.val))
        ffn_se = c_se.val
        dstore("d_ffn", ffnfm[:, :], wait=[(c_se, ffn_se)])

        z2s = AR.alloc(D, F32, tag="z2s")
        y1r = AR.alloc(D, F32, tag="y1r")
        y2s = AR.alloc(D, F32, tag="y2s")
        final_sts = []
        for si in range(NSUB):
            r0, rn = SUBS[si]
            vc, vv = ld(y1r[:, :], y1_park[si], q="gpsimd",
                        wait=[(c_ve, None),
                              (stream_ctr("y1g"), None)],
                        stream="y1r")
            for cc in range(NCH):
                b, bw = bank()
                E("tensor", lambda te, cc=cc, b=b, r0=r0, rn=rn:
                  te.transpose(
                      pb[b][0:rn, 0:128],
                      ffnfm[0:128, cc * T + r0: cc * T + r0 + rn],
                      ident_f[:, :]),
                  wait=bw + [(c_se, ffn_se)], inc=[c_te])
                tv = c_te.val
                E("vector", lambda ve, cc=cc, b=b, rn=rn, si=si:
                  ve.scalar_tensor_tensor(
                      z2s[0:rn, cc * 128:(cc + 1) * 128],
                      pb[b][0:rn, 0:128], o2t[0:rn, si:si + 1],
                      y1r[0:rn, cc * 128:(cc + 1) * 128],
                      AX.mult, AX.add),
                  wait=[(c_te, tv), (vc, vv)], inc=[c_ve])
                bank_done(b)
            z2_ve = c_ve.val
            stats_ve = ln_stats(z2s[0:rn, :], rn,
                                [(c_ve, z2_ve), (c_se, None)])
            E("vector", lambda ve, rn=rn: ve.tensor_scalar(
                y2s[0:rn, :], z2s[0:rn, :], zst[0:rn, 0:1], zst[0:rn, 1:2],
                AX.subtract, AX.mult),
              wait=[(c_ve, stats_ve), (c_st, None)], inc=[c_ve])
            y2_ve = c_ve.val
            final_sts.append(st(y_out[r0:r0 + rn, :], y2s[0:rn, :],
                                wait=[(c_ve, y2_ve)]))
        fv = c_st.val
        E("sync", lambda e: None,
          wait=[(c_st, fv), (c_ldg, None)]
          + [(cc, None) for cc in streams.values()])
        plan.run(ent(nc.Block()))
    return nc, dram, dbg
def _finish_stub(E, AR, c_ve, c_st, y_out, zz, waits=()):
    E("vector", lambda ve: ve.memset(zz[:, 0:64], 0.0),
      wait=list(waits), inc=[c_ve])
    v = c_ve.val
    for i in range(NSUB):
        r0, rn = SUBS[i]
        for j in range(D // 64):
            E("sync", lambda e, r0=r0, rn=rn, j=j: e.dma_start(
                out=y_out[r0:r0 + rn, 64 * j:64 * j + 64],
                in_=zz[0:rn, 0:64]),
              wait=[(c_ve, v)], inc=[c_st], incn=16)
    fv = c_st.val
    E("sync", lambda e: None, wait=[(c_st, fv)])


def kernel(**inputs):
    maps = _stage(inputs)
    nc, dram, dbg = build_program(debug=0)
    res = run_bass_kernel_spmd(nc, maps, core_ids=list(range(NCORES)))
    out = np.zeros((S, D), dtype=np.float32)
    for c in range(NCORES):
        out[T0S[c]:T0S[c] + SIZES[c]] = res.results[c]["y_out"][:SIZES[c]]
    return out.reshape(1, S, D)


# version 17
# speedup vs baseline: 1.1047x; 1.1047x over previous
import math
import contextlib
import numpy as np

import concourse.bass as bass
import concourse.mybir as mybir
from concourse.ap import AP
from concourse.bass_utils import run_bass_kernel_spmd

F32 = mybir.dt.float32
I8 = mybir.dt.int8
BF16 = mybir.dt.bfloat16
AX = mybir.AluOpType
ACT = mybir.ActivationFunctionType
AXL = mybir.AxisListType

S, D, DF = 8192, 2048, 4096
NH = 16
NCORES = 8
T, B = 1040, 104
NW, WSTR, RPW = 10, 110, 11
NR = NW * RPW
NCH, NFCH = 16, 32
NSUB = 9
SUBS = [(i * 128, (128 if i < 8 else 16)) for i in range(NSUB)]
TG = [(0, 512), (512, 512), (1024, 16)]
THIRDS = [(0, 384), (384, 384), (768, 272)]
T0S = [(1024 * c) // 10 * 10 for c in range(NCORES)]
SIZES = [T0S[c + 1] - T0S[c] for c in range(NCORES - 1)] + [S - T0S[-1]]
ISQD = 1.0 / math.sqrt(128.0)
NEG_BIG = -1.0e30
C1 = 1.0 / math.sqrt(float(D))
B3 = 12582912.0
C2 = 1.0 / math.sqrt(float(DF))
WMEAN = 1.0 / (D * DF)
GBW = [(0, 2), (2, 2), (4, 2), (6, 2), (8, 2)]


class Ctr:
    __slots__ = ("sem", "val", "name")

    def __init__(self, sem, name):
        self.sem, self.val, self.name = sem, 0, name


class Plan:
    def __init__(self):
        self.steps = {k: [] for k in
                      ("tensor", "vector", "scalar", "gpsimd", "sync")}
        self.pending = {k: [] for k in self.steps}
        self.auto = {}

    def emit(self, eng, fn, wait=(), inc=(), incn=1):
        waits = [(c, (c.val if v is None else v)) for (c, v) in wait if c]
        inc = list(inc)
        ac = self.auto.get(eng)
        if ac is not None and incn == 1:
            if eng != "tensor":
                waits.append((ac, ac.val))
            if ac not in inc:
                inc.append(ac)
        if self.pending[eng]:
            waits = self.pending[eng] + waits
            self.pending[eng] = []
        for c in inc:
            c.val += incn
        self.steps[eng].append((fn, waits, list(inc), incn))

    def barrier(self, pairs):
        pairs = [(c, (c.val if v is None else v)) for (c, v) in pairs]
        for k in self.pending:
            merged = {id(c): (c, v) for (c, v) in self.pending[k]}
            for c, v in pairs:
                if id(c) not in merged or merged[id(c)][1] < v:
                    merged[id(c)] = (c, v)
            self.pending[k] = list(merged.values())

    def run(self, block):
        def mk(steps):
            def body(eng):
                for fn, waits, incs, incn in steps:
                    for c, v in waits:
                        if v > 0:
                            eng.wait_ge(c.sem, v)
                    inst = fn(eng)
                    if inst is not None:
                        for c in incs:
                            inst.then_inc(c.sem, incn)
            return body
        for name in self.steps:
            getattr(block, name)(mk(self.steps[name]))


class Arena:

    def __init__(self, name):
        self.name = name
        self.pools = {}
        self.live = {}

    def set_pool(self, key, tensor, nbytes):
        self.pools[key] = [tensor, nbytes, [(0, nbytes)], 0]

    @staticmethod
    def _key(dt):
        if dt == BF16:
            return "b"
        if dt == I8:
            return "i"
        return "f"

    def alloc(self, cols, dt=F32, align=64, tag=""):
        esz = {"b": 2, "i": 1}.get(self._key(dt), 4)
        key = self._key(dt)
        tensor, nbytes, free_list, _ = self.pools[key]
        need = cols * esz
        order = (range(len(free_list)) if need > 1024 else
                 range(len(free_list) - 1, -1, -1))
        for idx in order:
            off, ln = free_list[idx]
            if need <= 1024:
                aoff = (off + ln - need) // align * align
                if aoff < off:
                    continue
            else:
                aoff = -(-off // align) * align
            if aoff + need <= off + ln:
                pre = (off, aoff - off)
                post = (aoff + need, off + ln - (aoff + need))
                repl = [p for p in (pre, post) if p[1] > 0]
                free_list[idx:idx + 1] = repl
                view = tensor.ap()[:, aoff // esz: aoff // esz + cols]
                self.live[id(view)] = (key, aoff, need, tag)
                used = nbytes - sum(l for _, l in free_list)
                self.pools[key][3] = max(self.pools[key][3], used)
                return view
        raise AssertionError(
            f"arena {self.name}/{key} OOM for {need}B tag={tag}; "
            f"free={free_list}")

    def free(self, view):
        key, off, ln, _ = self.live.pop(id(view))
        free_list = self.pools[key][2]
        free_list.append((off, ln))
        free_list.sort()
        merged = []
        for o, l in free_list:
            if merged and merged[-1][0] + merged[-1][1] == o:
                merged[-1] = (merged[-1][0], merged[-1][1] + l)
            else:
                merged.append((o, l))
        self.pools[key][2] = [tuple(m) for m in merged]


def _hadamard(n):
    h = np.array([[1.0]], dtype=np.float32)
    while h.shape[0] < n:
        h = np.block([[h, h], [h, -h]])
    return h.astype(np.float32)


def _make_consts():
    h128, h4 = _hadamard(128), _hadamard(4)
    hh4 = np.zeros((128, 4 * 512), dtype=np.float32)
    for j in range(4):
        for cp in range(4):
            hh4[:, j * 512 + cp * 128: j * 512 + cp * 128 + 128] = \
                h4[cp, j] * h128
    maskb = np.full((128, 128), NEG_BIG, dtype=np.float32)
    mask01 = np.zeros((128, 128), dtype=np.float32)
    for wq in range(4):
        for ip in range(RPW):
            p = 32 * wq + ip
            maskb[p, 10 * ip: 10 * ip + 10] = 0.0
            mask01[p, 10 * ip: 10 * ip + 10] = 1.0
    sel = np.zeros((16, NSUB * 128), dtype=np.float32)
    for si in range(NSUB):
        r0, rn = SUBS[si]
        i0 = -(-r0 // 10)
        i1 = -(-(r0 + rn) // 10)
        for b_ in range(i1 - i0):
            c = 10 * (i0 + b_) - r0
            if 0 <= c < rn:
                sel[b_, si * 128 + c] = 1.0
    return {
        "hh4": hh4,
        "ident": np.eye(128, dtype=np.float32),
        "maskb": np.ascontiguousarray(np.tile(maskb, (1, NH))),
        "mask01": np.ascontiguousarray(np.tile(mask01, (1, NH))),
        "sel": sel,
    }


def _stage(inputs):
    x = np.ascontiguousarray(
        np.asarray(inputs["L0_attn_hidden_states"], np.float32).reshape(S, D))
    shared = {
        "x_full": x,
        "wqt": np.ascontiguousarray(np.asarray(inputs["L0_attn_wq"],
                                               np.float32)
                                    .reshape(D, NH, 128).transpose(1, 0, 2)),
        "wkt": np.ascontiguousarray(np.asarray(inputs["L0_attn_wk"],
                                               np.float32)
                                    .reshape(D, NH, 128).transpose(1, 0, 2)),
        "wv": np.ascontiguousarray(np.asarray(inputs["L0_attn_wv"],
                                              np.float32)),
        "wo": np.ascontiguousarray(np.asarray(inputs["L0_attn_wo"],
                                              np.float32)),
        "wupT": np.ascontiguousarray(np.asarray(inputs["L0_ffn_w_up"],
                                                np.float32).T),
        "wdownT": np.ascontiguousarray(np.asarray(inputs["L0_ffn_w_down"],
                                                  np.float32).T),
        "bias": np.stack([np.asarray(inputs[k], np.float32) for k in
                          ("L0_attn_bq", "L0_attn_bk", "L0_attn_bv",
                           "L0_attn_bo")]),
        **_make_consts(),
    }
    maps = []
    for c in range(NCORES):
        t0 = T0S[c]
        valid = min(S - t0, T)
        xtm = np.zeros((T, D), dtype=np.float32)
        xtm[:valid] = x[t0:t0 + valid]
        m = dict(shared)
        m["xtm"] = xtm
        m["xfm"] = np.ascontiguousarray(xtm.T)
        m["onesv"] = (np.arange(T) < valid).astype(np.float32).reshape(1, T)
        maps.append(m)
    return maps



def apx(view, off_elems, dims):
    return AP(view.tensor, view.offset + off_elems,
              [list(view.ap[0])] + [list(d) for d in dims])

def build_program(debug=0, phases=99):
    nc = bass.Bass("TRN2", target_bir_lowering=False)
    dram = {}

    def din(name, shape, dt=F32):
        dram[name] = nc.dram_tensor(name, shape, dt, kind="ExternalInput").ap()
        return dram[name]

    x_full = din("x_full", [S, D])
    xtm = din("xtm", [T, D])
    xfm = din("xfm", [D, T])
    wqt = din("wqt", [NH, D, 128])
    wkt = din("wkt", [NH, D, 128])
    wv_d = din("wv", [D, D])
    wo_d = din("wo", [D, D])
    wupT = din("wupT", [D, DF])
    wdownT = din("wdownT", [DF, D])
    bias_d = din("bias", [4, D])
    hh4_d = din("hh4", [128, 4 * 512])
    ident_d = din("ident", [128, 128])
    maskb_d = din("maskb", [128, NH * 128])
    mask01_d = din("mask01", [128, NH * 128])
    sel_d = din("sel", [16, NSUB * 128])
    onesv_d = din("onesv", [1, T])

    y_out = nc.dram_tensor("y_out", [T, D], F32, kind="ExternalOutput").ap()
    wup_int = nc.dram_tensor("wup_int", [NCH, 128, DF], BF16,
                             kind="Internal").ap()
    wdn_int = nc.dram_tensor("wdn_int", [NFCH, 128, D], BF16,
                             kind="Internal").ap()
    z_park = nc.dram_tensor("z_park", [NSUB, 128, D], F32,
                            kind="Internal").ap()
    hq_park = nc.dram_tensor("hq_park", [NFCH, 128, T], BF16,
                             kind="Internal").ap()
    y1_park = nc.dram_tensor("y1_park", [NSUB, 128, D], F32,
                             kind="Internal").ap()

    dbg = {}

    def dbg_out(name, shape, dt=F32):
        if debug:
            dbg[name] = nc.dram_tensor(name, shape, dt,
                                       kind="ExternalOutput").ap()

    for nm, shp in [("d_colsum", [1, D]), ("d_q", [128, NH * B]),
                    ("d_swide", [128, 3 * NH * 128]),
                    ("d_ekm", [128, NW * NH * RPW]),
                    ("d_epi", [128, 6 * 3 * NH]), ("d_ctx", [128, NH * B]),
                    ("d_gb", [128, NCH * 2 * NH * RPW]),
                    ("d_ctx0", [128, NH * B]),
                    ("d_sv", [1, D]), ("d_u", [65, D]), ("d_a", [65, T]),
                    ("d_z", [NSUB, 128, D]), ("d_y1", [NSUB, 128, D]),
                    ("d_xq", [128, NCH * T // 2]), ("d_wsc", [1, 6]),
                    ("d_h", [128, NFCH * 136]), ("d_hq", [128, NFCH * T // 2]),
                    ("d_ffn", [128, NCH * T]), ("d_s1", [128, NSUB]),
                    ("d_o2", [128, NSUB])]:
        dbg_out(nm, shp)

    plan = Plan()
    E = plan.emit
    stk = contextlib.ExitStack()
    with stk:
        ent = stk.enter_context
        AR = Arena("A")
        AR.set_pool("f", ent(nc.sbuf_tensor(
            "arena", [128, 122624 // 4], F32)), 122624)
        AR.set_pool("b", ent(nc.sbuf_tensor(
            "arenab", [128, 90112 // 2], BF16)), 90112)
        pb = [ent(nc.psum_tensor(f"pb{i}", [128, 512], F32)) for i in range(8)]

        def ctr(name):
            return Ctr(ent(nc.semaphore(name)), name)

        c_ld = ctr("c_ld")
        c_ldg = ctr("c_ldg")
        c_st = ctr("c_st")
        c_te = ctr("c_te")
        c_ve = ctr("c_ve")
        c_se = ctr("c_se")
        c_ge = ctr("c_ge")
        plan.auto = {"tensor": c_te, "vector": c_ve, "scalar": c_se,
                     "gpsimd": c_ge}

        streams = {}

        def stream_ctr(name):
            if name not in streams:
                streams[name] = ctr("cs_" + name)
            return streams[name]

        def ld(dst, src, q="sync", wait=(), stream=None):
            cc = stream_ctr(stream) if stream else \
                (c_ld if q == "sync" else c_ldg)
            E(q, lambda e, d=dst, s=src: e.dma_start(out=d, in_=s),
              wait=wait, inc=[cc], incn=16)
            return (cc, cc.val)

        def lda(dst, src, wait=(), stream=None):
            cc = stream_ctr(stream) if stream else c_ldg
            E("gpsimd", lambda e, d=dst, s=src: e.dma_start(
                out=d, in_=s, accum_op=AX.add),
              wait=wait, inc=[cc], incn=16)
            return (cc, cc.val)

        def st(dst, src, wait=(), stream=None):
            cc = stream_ctr(stream) if stream else c_st
            E("sync", lambda e, d=dst, s=src: e.dma_start(out=d, in_=s),
              wait=wait, inc=[cc], incn=16)
            return (cc, cc.val)

        def dstore(name, src, wait=(), stream=None):
            if debug and name in dbg:
                r = st(dbg[name], src, wait=wait, stream=stream)
                dbar()
                return r
            return None

        def dbar():
            plan.barrier([(c_ve, None), (c_se, None), (c_te, None),
                          (c_ge, None), (c_st, None), (c_ld, None),
                          (c_ldg, None)]
                         + [(cc, None) for cc in streams.values()])

        hh4_b = AR.alloc(4 * 512, BF16)
        ident_f = AR.alloc(128, F32)
        maskb_s = AR.alloc(NH * 128, F32)
        mask01_s = AR.alloc(NH * 128, F32)
        biasr = AR.alloc(D, F32)
        onesc = AR.alloc(16, F32)
        onesr = AR.alloc(T, F32)
        sconst = AR.alloc(16, F32)
        cscr = AR.alloc(4 * 512, F32)

        ld(cscr[:, 0:2048], hh4_d)
        ld(ident_f[:, :], ident_d)
        ld(maskb_s[:, :], maskb_d)
        ld(mask01_s[:, :], mask01_d)
        for bi4 in range(4):
            ld(biasr[32 * bi4:32 * bi4 + 1, :], bias_d[bi4:bi4 + 1, :])
        consts_ld = c_ld.val
        E("vector", lambda ve: ve.tensor_copy(hh4_b[:, :], cscr[:, 0:2048]),
          wait=[(c_ld, consts_ld)], inc=[c_ve])
        AR.free(cscr)
        E("vector", lambda ve: ve.memset(onesc[:, :], 1.0), inc=[c_ve])
        E("vector", lambda ve: ve.memset(onesr[:, :], 1.0), inc=[c_ve])
        E("vector", lambda ve: ve.memset(sconst[:, :], float(S)), inc=[c_ve])
        consts_ve = c_ve.val
        ld(onesr[0:1, :], onesv_d, wait=[(c_ve, consts_ve)])
        ld(onesr[32:33, :], onesv_d, wait=[(c_ve, consts_ve)])
        consts_ld = c_ld.val

        wblk = [AR.alloc(1024, F32) for _ in range(2)]
        t1w = [AR.alloc(1024, F32)]
        wintl = [AR.alloc(1024, BF16)]
        wabs = AR.alloc(16, F32)
        thrp = AR.alloc(16, F32)
        wsc = AR.alloc(16, F32)
        WBLOCKS = []
        for b2 in range(NCH):
            for hf in range(4):
                WBLOCKS.append((0, wupT[b2 * 128:(b2 + 1) * 128,
                                        hf * 1024:(hf + 1) * 1024],
                                wup_int[b2][:, hf * 1024:(hf + 1) * 1024]))
        for b2 in range(NFCH):
            for hf in range(2):
                WBLOCKS.append((1, wdownT[b2 * 128:(b2 + 1) * 128,
                                          hf * 1024:(hf + 1) * 1024],
                                wdn_int[b2][:, hf * 1024:(hf + 1) * 1024]))
        N_UP = NCH * 4

        def emit_w1(i):
            mat, srcs, dsts = WBLOCKS[i]
            par = i % 2
            vc, vv = ld(wblk[par][:, :], srcs, q="sync",
                        wait=[(c_ve, None)],
                        stream=f"wb{par}")
            E("vector", lambda ve, par=par: ve.tensor_reduce(
                wabs[:, 8:9], wblk[par][:, :], AXL.X, AX.add,
                apply_absolute_value=True),
              wait=[(vc, vv), (c_ve, None)], inc=[c_ve])
            if i == 0 or i == N_UP:
                E("vector", lambda ve, mat=mat: ve.tensor_copy(
                    wabs[:, mat:mat + 1], wabs[:, 8:9]),
                  wait=[(c_ve, None)], inc=[c_ve])
            else:
                E("vector", lambda ve, mat=mat: ve.tensor_tensor(
                    wabs[:, mat:mat + 1], wabs[:, mat:mat + 1],
                    wabs[:, 8:9], AX.add),
                  wait=[(c_ve, None)], inc=[c_ve])

        wp_done = {0: 0, 1: 0}

        def emit_w2(i, thr_ve):
            mat, srcs, dsts = WBLOCKS[i]
            par = i % 2
            wint = wintl[par % len(wintl)]
            t1 = t1w[par % len(t1w)]
            wps = stream_ctr(f"wp{par % len(wintl)}")
            vc, vv = ld(wblk[par][:, :], srcs,
                        q=("sync" if par == 0 else "gpsimd"),
                        wait=[(c_ve, None)],
                        stream=(f"wb{par}" if par == 0 else f"wg{par}"))
            thr_bc = AP(thrp.tensor, thrp.offset + mat,
                        [list(thrp.ap[0]), [0, 1024]])
            E("vector", lambda ve, par=par, thr_bc=thr_bc, t1=t1:
              ve.tensor_tensor(t1[:, :], wblk[par][:, :],
                               thr_bc, AX.is_gt),
              wait=[(vc, vv), (c_ve, thr_ve), (c_ve, None)],
              inc=[c_ve])
            E("vector", lambda ve, par=par, mat=mat, wint=wint, t1=t1:
              ve.scalar_tensor_tensor(
                  wint[:, :], wblk[par][:, :],
                  thrp[:, 4 + mat:5 + mat], t1[:, :],
                  AX.is_lt, AX.subtract),
              wait=[(c_ve, None), (wps, wp_done[par % len(wintl)])],
              inc=[c_ve])
            wq_ve = c_ve.val
            E("sync", lambda e, dsts=dsts, wint=wint: e.dma_start(
                out=dsts, in_=wint[:, :]),
              wait=[(c_ve, wq_ve)], inc=[wps], incn=16)
            wp_done[par % len(wintl)] = wps.val


        xh = AR.alloc(NCH * T, BF16)
        xl = AR.alloc(NCH * T, BF16)
        xload = [AR.alloc(T, F32) for _ in range(2)]
        wsl_f = [AR.alloc(NCH * 128, F32) for _ in range(2)]
        wsl_h = [AR.alloc(NCH * 128, BF16) for _ in range(2)]
        wsl_l = [AR.alloc(NCH * 128, BF16) for _ in range(2)]
        kf = [AR.alloc(T, F32) for _ in range(2)]
        xtailf = AR.alloc(NCH * 16, F32)
        qc = AR.alloc(NH * B, F32)
        swide = AR.alloc(3 * NH * 128, F32)
        cx = [AR.alloc(512, F32) for _ in range(2)]
        csum = AR.alloc(D, F32)

        E("vector", lambda ve: ve.memset(swide[:, :], 0.0), inc=[c_ve])
        swide_zeroed = c_ve.val

        xsp = []
        for blk in range(16):
            par = blk % 2
            w = [(c_ve, xsp[blk - 2])] if blk >= 2 else ()
            vc, vv = ld(xload[par][:, :],
                        xfm[blk * 128:(blk + 1) * 128, :],
                        q="gpsimd", wait=w, stream=f"xl{par}")
            sl = slice(blk * T, (blk + 1) * T)
            E("scalar", lambda se, sl=sl, par=par: se.activation(
                xh[:, sl], xload[par][:, :], ACT.Copy),
              wait=[(vc, vv)], inc=[c_se])
            E("scalar", lambda se, blk=blk, par=par: se.activation(
                xtailf[:, blk * 16:(blk + 1) * 16],
                xload[par][:, 1024:1040], ACT.Copy),
              wait=[(vc, vv)], inc=[c_se])
            E("vector", lambda ve, sl=sl, par=par: ve.scalar_tensor_tensor(
                xl[:, sl], xload[par][:, :], 1.0, xh[:, sl], AX.bypass,
                AX.subtract),
              wait=[(c_se, None)], inc=[c_ve])
            xsp.append(c_ve.val)
        xsplit_ve = c_ve.val
        xsplit_se = c_se.val

        cs = {"tt": 0, "incs": []}

        def colsum_step(n):
            for _ in range(n):
                tt = cs["tt"]
                if tt >= 256:
                    return
                cs["tt"] += 1
                par = tt % 2
                rb, qb = tt // 4, tt % 4
                w = [(c_te, cs["incs"][tt - 2])] if tt >= 2 else ()
                vc, vv = ld(cx[par][:, :],
                            x_full[rb * 128:(rb + 1) * 128,
                                   qb * 512:(qb + 1) * 512],
                            q="gpsimd", wait=w, stream=f"cs{par}")
                E("tensor", lambda te, par=par, tt=tt, qb=qb:
                  te.matmul(
                      pb[qb][0:1, 0:512], onesc[:, 0:1],
                      cx[par][:, 0:512],
                      start=(tt < 4), stop=(tt >= 252)),
                  wait=[(vc, vv), (c_ve, consts_ve)],
                  inc=[c_te])
                cs["incs"].append(c_te.val)

        kev = [0, 0]
        scev = [0, 0]
        prev_scores_ve = swide_zeroed
        prev_k_mm = 0
        wsl_free = [0, 0]
        for h in range(NH):
            par = h % 2
            colsum_step(8)
            vqc, vqv = ld(wsl_f[par][:, :]
                          .rearrange("p (c m) -> p c m", m=128),
                          wqt[h].rearrange("(c p) m -> c p m", p=128)
                          .transpose([1, 0, 2]),
                          wait=([(c_ve, None),
                                 (c_te, wsl_free[par])]
                                if h >= 2 else ()),
                          stream=f"ws{par}")
            E("scalar", lambda se, par=par: se.activation(
                wsl_h[par][:, :], wsl_f[par][:, :], ACT.Copy),
              wait=[(vqc, vqv), (c_te, wsl_free[par])], inc=[c_se])
            E("vector", lambda ve, par=par: ve.scalar_tensor_tensor(
                wsl_l[par][:, :], wsl_f[par][:, :], 1.0, wsl_h[par][:, :],
                AX.bypass, AX.subtract),
              wait=[(c_se, None), (c_ve, xsplit_ve)], inc=[c_ve])
            wq_split = c_ve.val

            first_w = [(c_ve, wq_split), (c_ve, prev_scores_ve),
                       (c_se, xsplit_se)]
            for kk in range(NCH):
                swh = wsl_h[par][:, kk * 128:(kk + 1) * 128]
                swl = wsl_l[par][:, kk * 128:(kk + 1) * 128]
                mvh = xh[:, kk * T:(kk + 1) * T].rearrange(
                    "p (b s) -> p b s", s=10)[:, :, 0]
                mvl = xl[:, kk * T:(kk + 1) * T].rearrange(
                    "p (b s) -> p b s", s=10)[:, :, 0]
                E("tensor", lambda te, sw=swh, mv=mvh, kk=kk: te.matmul(
                    pb[7][:, 0:B], sw, mv, start=(kk == 0), stop=False),
                  wait=(first_w if kk == 0 else ()))
                E("tensor", lambda te, sw=swh, mv=mvl: te.matmul(
                    pb[7][:, 0:B], sw, mv, start=False, stop=False))
                E("tensor", lambda te, sw=swl, mv=mvh: te.matmul(
                    pb[7][:, 0:B], sw, mv, start=False, stop=False))
            E("tensor", lambda te, h=h: te.matmul(
                pb[7][:, 0:B], biasr[0:1, h * 128:(h + 1) * 128],
                onesr[0:1, 0:B], start=False, stop=True),
              wait=([(c_ld, consts_ld)] if h == 0 else ()), inc=[c_te])
            qmm = c_te.val
            E("scalar", lambda se, h=h: se.activation(
                qc[:, h * B:(h + 1) * B], pb[7][:, 0:B], ACT.Copy),
              wait=[(c_te, qmm)], inc=[c_se])
            q_ev = c_se.val

            vkc, vkv = ld(wsl_f[par][:, :]
                          .rearrange("p (c m) -> p c m", m=128),
                          wkt[h].rearrange("(c p) m -> c p m", p=128)
                          .transpose([1, 0, 2]),
                          wait=[(c_ve, wq_split)], stream=f"ws{par}")
            E("scalar", lambda se, par=par: se.activation(
                wsl_h[par][:, :], wsl_f[par][:, :], ACT.Copy),
              wait=[(vkc, vkv), (c_te, qmm)], inc=[c_se])
            E("vector", lambda ve, par=par: ve.scalar_tensor_tensor(
                wsl_l[par][:, :], wsl_f[par][:, :], 1.0, wsl_h[par][:, :],
                AX.bypass, AX.subtract),
              wait=[(c_se, None)], inc=[c_ve])
            wk_split = c_ve.val

            colsum_step(4)
            first_w = [(c_ve, wk_split), (c_se, max(kev)),
                       (c_ve, scev[par])]
            for kk in range(NCH):
                swh = wsl_h[par][:, kk * 128:(kk + 1) * 128]
                swl = wsl_l[par][:, kk * 128:(kk + 1) * 128]
                for gi, (g0, gn) in enumerate(TG[0:2]):
                    mh = xh[:, kk * T + g0: kk * T + g0 + gn]
                    mlo = xl[:, kk * T + g0: kk * T + g0 + gn]
                    E("tensor", lambda te, sw=swh, m=mh, gi=gi, gn=gn,
                        kk=kk: te.matmul(pb[4 + gi][:, 0:gn], sw, m,
                                         start=(kk == 0), stop=False),
                      wait=(first_w if (kk == 0 and gi == 0) else ()))
                    E("tensor", lambda te, sw=swh, m=mlo, gi=gi, gn=gn:
                      te.matmul(pb[4 + gi][:, 0:gn], sw, m, start=False,
                                stop=False))
                    E("tensor", lambda te, sw=swl, m=mh, gi=gi, gn=gn:
                      te.matmul(pb[4 + gi][:, 0:gn], sw, m, start=False,
                                stop=False))
                E("tensor", lambda te, kk=kk, par=par: te.matmul(
                    pb[6][:, 0:16],
                    wsl_f[par][:, kk * 128:(kk + 1) * 128],
                    xtailf[:, kk * 16:(kk + 1) * 16],
                    start=(kk == 0), stop=False),
                  wait=([(c_se, xsplit_se)] if kk == 0 else ()))
            for gi, (g0, gn) in enumerate(TG):
                E("tensor", lambda te, h=h, g0=g0, gn=gn, gi=gi: te.matmul(
                    pb[4 + gi][:, 0:gn], biasr[32:33, h * 128:(h + 1) * 128],
                    onesr[32:33, g0:g0 + gn], start=False, stop=True),
                  inc=([c_te] if gi == 2 else []))
            kmm = c_te.val
            wsl_free[par] = kmm
            for gi, (g0, gn) in enumerate(TG):
                E("scalar", lambda se, g0=g0, gn=gn, gi=gi, par=par:
                  se.activation(kf[par][:, g0:g0 + gn], pb[4 + gi][:, 0:gn],
                                ACT.Copy, scale=ISQD),
                  wait=[(c_te, kmm)], inc=[c_se])
            k_ev = c_se.val
            kev[par] = k_ev

            for wdx in range(NW):
                k0 = wdx * WSTR
                kn = min(128, T - k0)
                rn = min(RPW, B - RPW * wdx)
                w = ([(c_se, k_ev), (c_se, q_ev)] if wdx == 0
                     else [(c_ve, None)])
                E("tensor", lambda te, h=h, wdx=wdx, k0=k0, kn=kn, rn=rn,
                    par=par: te.matmul(
                      pb[7][0:rn, 0:kn],
                      qc[:, h * B + RPW * wdx: h * B + RPW * wdx + rn],
                      kf[par][:, k0:k0 + kn], start=True, stop=True),
                  wait=w, inc=[c_te])
                smm = c_te.val
                wb2, wq2 = wdx // 4, wdx % 4
                E("vector", lambda ve, h=h, kn=kn, rn=rn, wb2=wb2, wq2=wq2:
                  ve.tensor_copy(
                      swide[32 * wq2:32 * wq2 + rn,
                            wb2 * 2048 + h * 128: wb2 * 2048 + h * 128 + kn],
                      pb[7][0:rn, 0:kn]),
                  wait=[(c_te, smm), (c_ve, swide_zeroed)], inc=[c_ve])
            scev[par] = c_ve.val
            prev_scores_ve = c_ve.val
            for wi_ in range(8 * h, 8 * h + 8):
                emit_w1(wi_)
        colsum_step(256)
        scores_done = c_ve.val

        wabs_ge = c_ve.val
        E("tensor", lambda te: te.matmul(
            pb[4][0:1, 0:2], onesc[:, 0:1], wabs[:, 0:2],
            start=True, stop=True),
          wait=[(c_ve, wabs_ge), (c_te, None)], inc=[c_te])
        E("vector", lambda ve: ve.tensor_copy(wsc[0:1, 0:2],
                                              pb[4][0:1, 0:2]),
          wait=[(c_te, None)], inc=[c_ve])
        E("vector", lambda ve: ve.tensor_scalar(
            wsc[0:1, 0:2], wsc[0:1, 0:2], WMEAN, 1e-5, AX.mult, AX.max),
          wait=[(c_ve, None)], inc=[c_ve])
        E("vector", lambda ve: ve.tensor_scalar(
            wsc[0:1, 2:4], wsc[0:1, 0:2], 0.5, None, AX.mult),
          wait=[(c_ve, None)], inc=[c_ve])
        E("vector", lambda ve: ve.tensor_copy(wsc[0:1, 4:6], wsc[0:1, 0:2]),
          wait=[(c_ve, None)], inc=[c_ve])
        wsc_ve = c_ve.val
        dstore("d_wsc", wsc[0:1, 0:6], wait=[(c_ve, wsc_ve)])
        E("tensor", lambda te: te.matmul(
            pb[4][0:128, 0:4], onesr[0:1, 0:128], wsc[0:1, 2:6],
            start=True, stop=True),
          wait=[(c_ve, wsc_ve), (c_te, None)], inc=[c_te])
        E("vector", lambda ve: ve.tensor_copy(thrp[:, 0:4],
                                              pb[4][0:128, 0:4]),
          wait=[(c_te, None)], inc=[c_ve])
        E("vector", lambda ve: ve.tensor_scalar(
            thrp[:, 4:6], thrp[:, 0:2], -1.0, None, AX.mult),
          wait=[(c_ve, None)], inc=[c_ve])
        thr_ve = c_ve.val
        wscb = thrp[:, 2:4]

        colsum_te = cs["incs"][-1]
        for ns in range(4):
            E("vector", lambda ve, ns=ns: ve.tensor_copy(
                csum[0:1, ns * 512:(ns + 1) * 512], pb[ns][0:1, 0:512]),
              wait=[(c_te, colsum_te)], inc=[c_ve])
        csum_ve = c_ve.val
        dstore("d_colsum", csum[0:1, :], wait=[(c_ve, csum_ve)])
        sw_dump = dstore("d_swide", swide[:, :], wait=[(c_ve, scores_done)],
                         stream="dsw")
        dstore("d_q", qc[:, :], wait=[(c_ve, scores_done)])
        if phases <= 1:
            _finish_stub(E, AR, c_ve, c_st, y_out, biasr,
                         [(c_te, None), (c_ve, None), (c_se, None)])
            plan.run(ent(nc.Block()))
            return nc, dram, dbg

        dbar()
        for buf in (xh, xl, xload[0], xload[1], wsl_f[0], wsl_f[1],
                    wsl_h[0], wsl_h[1], wsl_l[0], wsl_l[1], kf[0], kf[1],
                    cx[0], cx[1]):
            AR.free(buf)
        wintl.append(AR.alloc(1024, BF16))
        t1w.append(AR.alloc(1024, F32))

        NWB3 = 3
        ep_m = AR.alloc(NWB3 * NH, F32)
        ep_em = AR.alloc(NWB3 * NH, F32)
        ep_den = AR.alloc(NWB3 * NH, F32)
        ep_dinv = AR.alloc(NWB3 * NH, F32)
        ep_a = AR.alloc(NWB3 * NH, F32)
        ep_cse = AR.alloc(NWB3 * NH, F32)
        ep_t = AR.alloc(NH * 128, F32)
        ekml = [AR.alloc(NH * RPW, F32) for _ in range(NW)]
        stg = AR.alloc(128, F32)
        a16t = AR.alloc(NWB3 * 128, F32)
        cset = AR.alloc(NWB3 * 128, F32)

        def view4(ap_t):
            return ap_t[:, :].rearrange("p (bh k) -> p bh k", k=128)

        def m_bc(ap_tile):
            base = ap_tile[:, 0:NWB3 * NH]
            return AP(base.tensor, base.offset,
                      [list(base.ap[0]), list(base.ap[1]), [0, 128]])

        def msk_bc(ap_tile):
            base = ap_tile[:, 0:NH * 128]
            return AP(base.tensor, base.offset,
                      [list(base.ap[0]), [0, NWB3],
                       list(base.ap[1])])

        E("vector", lambda ve: ve.tensor_tensor(
            swide[:, :].rearrange("p (b hk) -> p b hk", b=NWB3),
            swide[:, :].rearrange("p (b hk) -> p b hk", b=NWB3),
            msk_bc(maskb_s), AX.add),
          wait=[(c_ve, scores_done), (c_ld, consts_ld)]
          + ([sw_dump] if sw_dump else []), inc=[c_ve])
        E("vector", lambda ve: ve.tensor_reduce(
            ep_m[:, :], view4(swide), AXL.X, AX.max),
          wait=[(c_ve, None)], inc=[c_ve])
        E("vector", lambda ve: ve.tensor_scalar(
            ep_m[:, :], ep_m[:, :], 0.0, None, AX.max),
          wait=[(c_ve, None)], inc=[c_ve])
        m_ve = c_ve.val
        E("scalar", lambda se: se.activation(
            ep_em[:, :], ep_m[:, :], ACT.Exp, scale=-1.0),
          wait=[(c_ve, m_ve)], inc=[c_se])
        em_se = c_se.val
        E("vector", lambda ve: ve.tensor_tensor(
            view4(swide), view4(swide), m_bc(ep_m), AX.subtract),
          wait=[(c_ve, m_ve)], inc=[c_ve])
        E("scalar", lambda se: se.activation(
            swide[:, :], swide[:, :], ACT.Exp),
          wait=[(c_ve, None)], inc=[c_se])
        e_se = c_se.val
        E("vector", lambda ve: ve.tensor_reduce(
            ep_den[:, :], view4(swide), AXL.X, AX.add),
          wait=[(c_se, e_se)], inc=[c_ve])
        E("vector", lambda ve: ve.scalar_tensor_tensor(
            ep_den[:, :], ep_em[:, :], float(S - 10), ep_den[:, :],
            AX.mult, AX.add),
          wait=[(c_ve, None), (c_se, em_se)], inc=[c_ve])
        E("vector", lambda ve: ve.reciprocal(ep_dinv[:, :], ep_den[:, :]),
          wait=[(c_ve, None)], inc=[c_ve])
        dinv_ve = c_ve.val
        E("vector", lambda ve: ve.tensor_tensor(
            ep_a[:, :], ep_em[:, :], ep_dinv[:, :], AX.mult),
          wait=[(c_ve, dinv_ve)], inc=[c_ve])
        a_ve = c_ve.val
        for wb_ in range(NWB3):
            def ebc(tile, wb_=wb_):
                base = tile[:, wb_ * NH:(wb_ + 1) * NH]
                return AP(base.tensor, base.offset,
                          [list(base.ap[0]), list(base.ap[1]), [0, 128]])
            emb, dvb = ebc(ep_em), ebc(ep_dinv)
            E("vector", lambda ve, emb=emb: ve.tensor_tensor(
                ep_t[:, :].rearrange("p (h k) -> p h k", k=128),
                emb,
                mask01_s[:, :].rearrange("p (h k) -> p h k", k=128),
                AX.mult),
              wait=[(c_se, e_se), (c_ve, None)], inc=[c_ve])
            E("vector", lambda ve, wb_=wb_: ve.tensor_tensor(
                swide[:, wb_ * 2048:(wb_ + 1) * 2048],
                swide[:, wb_ * 2048:(wb_ + 1) * 2048],
                ep_t[:, :], AX.subtract),
              wait=[(c_ve, None)], inc=[c_ve])
            E("vector", lambda ve, wb_=wb_, dvb=dvb: ve.tensor_tensor(
                swide[:, wb_ * 2048:(wb_ + 1) * 2048]
                .rearrange("p (h k) -> p h k", k=128),
                swide[:, wb_ * 2048:(wb_ + 1) * 2048]
                .rearrange("p (h k) -> p h k", k=128),
                dvb, AX.mult),
              wait=[(c_ve, None)], inc=[c_ve])
        epp_ve = c_ve.val
        E("vector", lambda ve: ve.tensor_reduce(
            ep_cse[:, :], view4(swide), AXL.X, AX.add),
          wait=[(c_ve, epp_ve)], inc=[c_ve])
        cse_ve = c_ve.val
        if debug and "d_epi" in dbg:
            for j, tile in enumerate((ep_a, ep_m, ep_em, ep_den, ep_dinv,
                                      ep_cse)):
                st(dbg["d_epi"][:, j * 48:(j + 1) * 48],
                   tile[:, 0:NWB3 * NH], wait=[(c_ve, cse_ve)])
            dbar()

        for wb in range(NWB3):
            E("tensor", lambda te, wb=wb: te.transpose(
                pb[0][0:NH, 0:128], ep_a[:, wb * NH:(wb + 1) * NH],
                ident_f[:, :]),
              wait=[(c_ve, a_ve), (c_ve, None)], inc=[c_te])
            E("vector", lambda ve, wb=wb: ve.tensor_copy(
                a16t[0:NH, wb * 128:(wb + 1) * 128], pb[0][0:NH, 0:128]),
              wait=[(c_te, None)], inc=[c_ve])
            E("tensor", lambda te, wb=wb: te.transpose(
                pb[1][0:NH, 0:128], ep_cse[:, wb * NH:(wb + 1) * NH],
                ident_f[:, :]),
              wait=[(c_ve, cse_ve), (c_ve, None)], inc=[c_te])
            E("vector", lambda ve, wb=wb: ve.tensor_copy(
                cset[0:NH, wb * 128:(wb + 1) * 128], pb[1][0:NH, 0:128]),
              wait=[(c_te, None)], inc=[c_ve])
        a16_ve = c_ve.val

        ek_ev = {b_: a16_ve for b_ in range(4)}
        for h in range(NH):
            for wdx in range(NW):
                wb, wq = wdx // 4, wdx % 4
                bk = (h * NW + wdx) % 4
                if wq == 3:
                    E("scalar", lambda se, wb=wb, h=h: se.activation(
                        stg[0:RPW, :],
                        swide[96:96 + RPW,
                              wb * 2048 + h * 128:
                              wb * 2048 + (h + 1) * 128],
                        ACT.Copy),
                      wait=[(c_ve, epp_ve), (c_te, None)], inc=[c_se])
                    E("tensor", lambda te, bk=bk: te.transpose(
                        pb[bk][0:128, 0:RPW], stg[0:RPW, :],
                        ident_f[0:RPW, 0:RPW]),
                      wait=[(c_se, None), (c_ve, ek_ev[bk])], inc=[c_te])
                else:
                    E("tensor", lambda te, wb=wb, wq=wq, h=h, bk=bk:
                      te.transpose(
                          pb[bk][0:128, 0:RPW],
                          swide[32 * wq:32 * wq + RPW,
                                wb * 2048 + h * 128:
                                wb * 2048 + (h + 1) * 128],
                          ident_f[32 * wq:32 * wq + RPW,
                                  32 * wq:32 * wq + RPW]),
                      wait=[(c_ve, max(epp_ve, ek_ev[bk]))], inc=[c_te])
                tv = c_te.val
                E("vector", lambda ve, h=h, wdx=wdx, bk=bk: ve.tensor_copy(
                    ekml[wdx][:, h * RPW:h * RPW + RPW],
                    pb[bk][0:128, 0:RPW]),
                  wait=[(c_te, tv)], inc=[c_ve])
                ek_ev[bk] = c_ve.val
            for wi_ in range(8 * h, 8 * h + 8):
                emit_w2(wi_, thr_ve)
        wpark_ctr = stream_ctr("wp0")
        wpark_st = wpark_ctr.val
        wpark_ctr2 = stream_ctr("wp1")
        wpark_st2 = wpark_ctr2.val
        ekm_ve = c_ve.val
        for buf in (ep_m, ep_em, ep_den, ep_dinv, ep_t, stg):
            AR.free(buf)
        if debug and "d_ekm" in dbg:
            for wdx in range(NW):
                st(dbg["d_ekm"][:, wdx * NH * RPW:(wdx + 1) * NH * RPW],
                   ekml[wdx][:, :], wait=[(c_ve, ekm_ve)])
            dbar()
        if phases <= 2:
            _finish_stub(E, AR, c_ve, c_st, y_out, biasr,
                         [(c_te, None), (c_ve, None), (c_se, None)])
            plan.run(ent(nc.Block()))
            return nc, dram, dbg

        dbar()
        for buf in [swide, qc, ep_a, ep_cse, maskb_s, mask01_s,
                    wblk[0], wblk[1]] + t1w + wintl:
            AR.free(buf)
        NWB = 5
        shrow = AR.alloc(128, F32)
        csumT = AR.alloc(16, F32)
        svbT = AR.alloc(32, F32)

        E("vector", lambda ve: ve.memset(shrow[:, 0:128], 0.0),
          wait=[(c_ve, csum_ve)], inc=[c_ve])
        shz = c_ve.val
        _, rowv = ld(shrow[0:16, 0:128], AP(csum.tensor, csum.offset,
                                            [[csum.ap[0][0], 1],
                                             [128, 16], [1, 128]]),
                     q="gpsimd", wait=[(c_ve, shz)], stream="rowT")
        E("tensor", lambda te: te.transpose(
            pb[6][0:128, 0:128], shrow[0:128, 0:128], ident_f[:, :]),
          wait=[(streams["rowT"], rowv)], inc=[c_te])
        E("vector", lambda ve: ve.tensor_copy(csumT[:, 0:16],
                                              pb[6][0:128, 0:16]),
          wait=[(c_te, None)], inc=[c_ve])
        csumT_ve = c_ve.val
        AR.free(csum)
        AR.free(onesr)
        wvb = [AR.alloc(D, F32) for _ in range(2)]
        wob = wvb
        xwin = [AR.alloc(D, F32)]
        svrow = AR.alloc(D, F32)
        ctxc = AR.alloc(NH * B, F32)
        gbl = [AR.alloc(4 * NH * RPW, F32) for _ in range(NCH)]

        E("vector", lambda ve: ve.memset(ctxc[:, :], 0.0), inc=[c_ve])
        ctxz_ve = c_ve.val
        ctx_bank = {2: (c_ve, 0), 3: (c_ve, 0)}
        sv_ve = 0
        for bi, (w0, nwn) in enumerate(((0, 4), (4, 4), (8, 2))):
            for wi in range(nwn):
                wdx = w0 + wi
                bank = wdx % 2
                k0 = wdx * WSTR
                kn = min(128, T - k0)
                vc, vv = ld(xwin[0][0:kn, :], xtm[k0:k0 + kn, :],
                            q="gpsimd", wait=[(c_te, None)], stream="xw")
                for cc in range(NCH):
                    E("tensor", lambda te, cc=cc, wdx=wdx, bank=bank:
                      te.matmul(
                          pb[bank][:, 0:NH * RPW],
                          xwin[0][:, cc * 128:(cc + 1) * 128],
                          ekml[wdx][:, :],
                          start=True, stop=True),
                      wait=([(vc, vv), (c_ve, ekm_ve), (c_ve, None)]
                            if cc == 0 else [(c_ve, None)]),
                      inc=[c_te])
                    E("vector", lambda ve, cc=cc, wi=wi, bank=bank,
                        nwn=nwn: ve.tensor_copy(
                          apx(gbl[cc], wi * RPW,
                              [[nwn * RPW, NH], [1, RPW]]),
                          pb[bank][:, 0:NH * RPW]),
                      wait=[(c_te, None)], inc=[c_ve])
            gb_ve = c_ve.val
            cbw = [ctx_bank[2], ctx_bank[3]]
            for cc in range(NCH):
                par = cc % 2
                vc, vv = ld(wvb[par][:, :],
                            wv_d[cc * 128:(cc + 1) * 128, :],
                            wait=[(c_te, None)], stream=f"wv{par}")
                if bi == 0:
                    for ns in range(4):
                        E("tensor", lambda te, cc=cc, ns=ns, par=par:
                          te.matmul(
                              pb[4 + ns][0:1, 0:512], csumT[:, cc:cc + 1],
                              wvb[par][:, ns * 512:(ns + 1) * 512],
                              start=(cc == 0), stop=False),
                          wait=([(c_ve, csumT_ve), (vc, vv)]
                                if ns == 0 else ()))
                for h in range(NH):
                    hb = 2 + h // 8
                    hc = (h % 8) * nwn * RPW
                    E("tensor", lambda te, cc=cc, h=h, hb=hb, hc=hc,
                        par=par, nwn=nwn: te.matmul(
                          pb[hb][:, hc:hc + nwn * RPW],
                          wvb[par][:, h * 128:(h + 1) * 128],
                          gbl[cc][:, h * nwn * RPW:(h + 1) * nwn * RPW],
                          start=(cc == 0 and h % 8 == 0),
                          stop=(cc == 15 and h % 8 == 7)),
                      wait=((cbw + [(vc, vv), (c_ve, gb_ve)])
                            if (cc == 0 and h == 0) else
                            ([(vc, vv)] if h == 0 else ())),
                      inc=([c_te] if h == NH - 1 else []))
            cmm = c_te.val
            if bi == 0:
                for ns in range(4):
                    E("tensor", lambda te, ns=ns: te.matmul(
                        pb[4 + ns][0:1, 0:512], sconst[64:65, 0:1],
                        biasr[64:65, ns * 512:(ns + 1) * 512],
                        start=False, stop=True),
                      inc=([c_te] if ns == 3 else []))
                svmm = c_te.val
                for ns in range(4):
                    E("vector", lambda ve, ns=ns: ve.tensor_copy(
                        svrow[0:1, ns * 512:(ns + 1) * 512],
                        pb[4 + ns][0:1, 0:512]),
                      wait=[(c_te, svmm)], inc=[c_ve])
                sv_ve = c_ve.val
                dstore("d_sv", svrow[0:1, :], wait=[(c_ve, sv_ve)])
            ecnt = min(B - RPW * w0, nwn * RPW)
            pbv2, pbv3 = pb[2].ap(), pb[3].ap()
            for half, pbv in ((0, pbv2), (1, pbv3)):
                E("vector", lambda ve, w0=w0, half=half, ecnt=ecnt,
                    pbv=pbv, nwn=nwn: ve.tensor_copy(
                      apx(ctxc, (half * 8) * B + RPW * w0,
                          [[B, 8], [1, ecnt]]),
                      AP(pbv.tensor, pbv.offset,
                         [list(pbv.ap[0]), [nwn * RPW, 8], [1, ecnt]])),
                  wait=[(c_te, cmm), (c_ve, ctxz_ve)], inc=[c_ve])
                ctx_bank[2 + half] = (c_ve, c_ve.val)
            if bi == 0:
                dstore("d_ctx0", ctxc[:, :], wait=[(c_ve, c_ve.val)])
        ctx_ve = c_ve.val
        dstore("d_ctx", ctxc[:, :], wait=[(c_ve, ctx_ve)])

        _, rowv = ld(shrow[0:16, 0:128], AP(svrow.tensor, svrow.offset,
                                            [[svrow.ap[0][0], 1],
                                             [128, 16], [1, 128]]),
                     q="gpsimd", wait=[(c_ve, sv_ve), (c_te, None)],
                     stream="rowT")
        E("tensor", lambda te: te.transpose(
            pb[6][0:128, 0:128], shrow[0:128, 0:128], ident_f[:, :]),
          wait=[(streams["rowT"], rowv), (c_te, None)], inc=[c_te])
        E("vector", lambda ve: ve.tensor_copy(
            apx(svbT, 0, [[2, 16]]), pb[6][0:128, 0:16]),
          wait=[(c_te, None)], inc=[c_ve])
        svT_ve = c_ve.val
        _, rowv = ld(shrow[0:16, 0:128],
                     AP(biasr.tensor, biasr.offset + 64 * biasr.ap[0][0],
                        [[biasr.ap[0][0], 1], [128, 16], [1, 128]]),
                     q="gpsimd", wait=[(c_ve, svT_ve)], stream="rowT")
        E("tensor", lambda te: te.transpose(
            pb[6][0:128, 0:128], shrow[0:128, 0:128], ident_f[:, :]),
          wait=[(streams["rowT"], rowv), (c_te, None)], inc=[c_te])
        E("vector", lambda ve: ve.tensor_copy(
            apx(svbT, 1, [[2, 16]]), pb[6][0:128, 0:16]),
          wait=[(c_te, None)], inc=[c_ve])
        bvT_ve = c_ve.val

        for buf in [xwin[0], svrow, shrow] + gbl:
            AR.free(buf)
        uprime = AR.alloc(D, F32)
        bandtm = AR.alloc(D, F32)
        aprime = AR.alloc(T, F32)
        ustg = AR.alloc(D, F32)

        E("vector", lambda ve: ve.memset(uprime[:, :], 0.0), inc=[c_ve])
        upz = c_ve.val
        for cc in range(NCH):
            par = cc % 2
            vc, vv = ld(wob[par][:, :], wo_d[cc * 128:(cc + 1) * 128, :],
                        wait=[(c_te, None)],
                        stream=f"wo{par}")
            for ns in range(4):
                E("tensor", lambda te, cc=cc, ns=ns, par=par: te.matmul(
                    pb[ns][0:B, 0:512], ctxc[:, cc * B:(cc + 1) * B],
                    wob[par][:, ns * 512:(ns + 1) * 512],
                    start=(cc == 0), stop=(cc == 15)),
                  wait=([(vc, vv), (c_ve, ctx_ve), (c_ve, None)]
                        if ns == 0 else ()),
                  inc=([c_te] if ns == 3 else []))
            h = cc
            for ns in range(4):
                E("tensor", lambda te, h=h, ns=ns, par=par: te.matmul(
                      pb[4][0:2, 0:512], svbT[:, 2 * h:2 * h + 2],
                      wob[par][:, ns * 512:(ns + 1) * 512],
                      start=True, stop=True),
                  wait=([(vc, vv), (c_se, None), (c_ve, bvT_ve)]
                        if ns == 0 else [(c_se, None)]),
                  inc=[c_te])
                umm = c_te.val
                E("scalar", lambda se, ns=ns: se.activation(
                    ustg[0:2, ns * 512:(ns + 1) * 512],
                    pb[4][0:2, 0:512], ACT.Copy),
                  wait=[(c_te, umm)] +
                       ([(streams["upr"], None)]
                        if (ns == 0 and "upr" in streams) else []),
                  inc=[c_se])
            usev = c_se.val
            for ui in range(2):
                tgt = h if ui == 0 else 32 + h
                E("gpsimd", lambda e, tgt=tgt, ui=ui: e.dma_start(
                    out=uprime[tgt:tgt + 1, :], in_=ustg[ui:ui + 1, :]),
                  wait=[(c_se, usev), (c_ve, upz)],
                  inc=[stream_ctr("upr")], incn=16)
        bmm = c_te.val
        for ns in range(4):
            E("vector", lambda ve, ns=ns: ve.tensor_copy(
                bandtm[0:B, ns * 512:(ns + 1) * 512], pb[ns][0:B, 0:512]),
              wait=[(c_te, bmm)], inc=[c_ve])
        bandtm_ve = c_ve.val
        E("scalar", lambda se: se.activation(
            uprime[64:65, :], biasr[96:97, :], ACT.Copy),
          wait=[(c_ld, consts_ld), (c_ve, upz)], inc=[c_se])
        up_se = c_se.val
        up_dma = stream_ctr("upr").val

        E("vector", lambda ve: ve.memset(aprime[:, :], 0.0),
          wait=[(c_ve, None)], inc=[c_ve])
        E("vector", lambda ve: ve.memset(aprime[0:16, :], 1.0 / S),
          wait=[(c_ve, None)], inc=[c_ve])
        E("vector", lambda ve: ve.memset(aprime[64:65, :], 1.0),
          inc=[c_ve])
        apz = c_ve.val
        for wdx in range(NW):
            wb, wq = wdx // 4, wdx % 4
            cnt = min(RPW, (T - WSTR * wdx + 9) // 10)
            for rbase, srct in ((0, a16t), (32, cset)):
                E("vector", lambda ve, wdx=wdx, wb=wb, wq=wq, cnt=cnt,
                    rbase=rbase, srct=srct: ve.tensor_copy(
                      aprime[rbase:rbase + 16,
                             WSTR * wdx: WSTR * wdx + 10 * cnt]
                      .rearrange("p (b s) -> p b s", s=10)[:, :, 0],
                      srct[0:16, wb * 128 + 32 * wq:
                           wb * 128 + 32 * wq + cnt]),
                  wait=[(c_ve, apz), (c_ve, a16_ve)], inc=[c_ve])
        ap_ve = c_ve.val
        dstore("d_u", uprime[0:65, :], wait=[(c_se, up_se),
                                             (stream_ctr("upr"), None)])
        dstore("d_a", aprime[0:65, :], wait=[(c_ve, ap_ve)])

        for buf in [ustg, a16t, cset, csumT, svbT,
                    wvb[0], wvb[1], ctxc, biasr] + ekml:
            AR.free(buf)
        selr = AR.alloc(NSUB * 128, F32, tag="selr")
        zfw = [(c_te, None), (c_ve, None), (c_se, None)] + \
            [(cc, None) for cc in streams.values()]
        ld(selr[0:16, :], sel_d, q="sync", wait=list(zfw))
        sel_ld = c_ld.val
        zsub = [AR.alloc(D, F32, tag="zsub") for _ in range(2)]
        xsub = [AR.alloc(D, F32, tag="xsub") for _ in range(2)]
        bsel = [AR.alloc(D, F32, tag="bsel") for _ in range(2)]
        zb_ev = []
        bs_te = {0: 0, 1: 0}
        zs_ve = {0: 0, 1: 0}
        for si in range(NSUB):
            par = si % 2
            r0, rn = SUBS[si]
            i0 = -(-r0 // 10)
            i1 = -(-(r0 + rn) // 10)
            cnt = i1 - i0
            fw = list(zfw) if si < 2 else []
            vx, vxv = ld(xsub[par][0:rn, :], xtm[r0:r0 + rn, :],
                         q="gpsimd",
                         wait=fw + ([(c_ve, zs_ve[par])] if si >= 2
                                    else []),
                         stream=f"xz{par}")
            vb, vbv = ld(bsel[par][0:cnt, :], bandtm[i0:i0 + cnt, :],
                         q="gpsimd",
                         wait=fw + [(c_ve, bandtm_ve)]
                         + ([(c_te, bs_te[par])] if si >= 2 else []),
                         stream=f"bz{par}")
            for ns in range(4):
                bank = 5 + (si * 4 + ns) % 3
                wb_ = [(c_se, up_se), (c_ve, ap_ve),
                       (stream_ctr("upr"), up_dma)]
                if si * 4 + ns >= 3:
                    wb_.append((c_ve, zb_ev[-3]))
                E("tensor", lambda te, ns=ns, r0=r0, rn=rn, bank=bank:
                  te.matmul(
                      pb[bank][0:rn, 0:512],
                      aprime[0:65, r0:r0 + rn],
                      uprime[0:65, ns * 512:(ns + 1) * 512],
                      start=True, stop=False),
                  wait=wb_, inc=[c_te])
                E("tensor", lambda te, ns=ns, rn=rn, bank=bank, si=si,
                    cnt=cnt, par=par: te.matmul(
                      pb[bank][0:rn, 0:512],
                      selr[0:cnt, si * 128:si * 128 + rn],
                      bsel[par][0:cnt, ns * 512:(ns + 1) * 512],
                      start=False, stop=True),
                  wait=([(vb, vbv), (c_ld, sel_ld)] if ns == 0 else ()),
                  inc=[c_te])
                zmm = c_te.val
                E("vector", lambda ve, ns=ns, rn=rn, bank=bank, par=par:
                  ve.tensor_tensor(
                      zsub[par][0:rn, ns * 512:(ns + 1) * 512],
                      xsub[par][0:rn, ns * 512:(ns + 1) * 512],
                      pb[bank][0:rn, 0:512], AX.add),
                  wait=[(c_te, zmm), (vx, vxv), (c_se, None),
                        (stream_ctr(f"zp{par}"), None)], inc=[c_ve])
                zb_ev.append(c_ve.val)
            bs_te[par] = c_te.val
            zs_ve[par] = c_ve.val
            st(z_park[si], zsub[par][:, :],
               wait=[(c_ve, zs_ve[par]),
                     (stream_ctr(f"zp{par}"), None)],
               stream=f"zp{par}")
            if debug and "d_z" in dbg:
                st(dbg["d_z"][si], zsub[par][:, :],
                   wait=[(c_ve, zs_ve[par])])
                dbar()

        if phases <= 4:
            _finish_stub(E, AR, c_ve, c_st, y_out, zsub[0],
                         [(c_te, None), (c_ve, None), (c_se, None)])
            plan.run(ent(nc.Block()))
            return nc, dram, dbg
        dbar()
        for buf in (uprime, aprime, bandtm,
                    zsub[0], zsub[1], xsub[0], xsub[1], bsel[0], bsel[1],
                    selr):
            AR.free(buf)

        xqfm = AR.alloc(NCH * T, BF16, tag="xqfm")
        s1t = AR.alloc(16, F32, tag="s1t")
        o2t = AR.alloc(16, F32, tag="o2t")
        zst = AR.alloc(16, F32, tag="zst")
        sq = AR.alloc(D, F32, tag="sq")
        y1s = AR.alloc(D, F32, tag="y1s")
        zld = [AR.alloc(D, F32, tag="zld") for _ in range(2)]
        pfm_h = AR.alloc(D, BF16, tag="pfm_h")
        pfm_l = AR.alloc(D, BF16, tag="pfm_l")
        g1a = AR.alloc(D, F32, tag="g1a")
        g1b = AR.alloc(D, F32, tag="g1b")

        bankst = {b: (c_ve, 0) for b in range(8)}
        bank_i = [0]

        def bank():
            b = bank_i[0] % 8
            bank_i[0] += 1
            return b, [bankst[b]]

        def bank_done(b, cv=None):
            bankst[b] = (c_ve, c_ve.val) if cv is None else cv

        def ln_stats(zin, rn, prev_w):
            E("vector", lambda ve: ve.tensor_reduce(
                zst[0:rn, 2:3], zin, AXL.X, AX.add),
              wait=prev_w, inc=[c_ve])
            E("scalar", lambda se: se.activation(sq[0:rn, :], zin,
                                                 ACT.Square),
              wait=prev_w, inc=[c_se])
            E("vector", lambda ve: ve.tensor_reduce(
                zst[0:rn, 3:4], sq[0:rn, :], AXL.X, AX.add),
              wait=[(c_se, None)], inc=[c_ve])
            E("vector", lambda ve: ve.tensor_scalar(
                zst[0:rn, 0:1], zst[0:rn, 2:3], 1.0 / D, None, AX.mult),
              wait=[(c_ve, None)], inc=[c_ve])
            E("vector", lambda ve: ve.tensor_scalar(
                zst[0:rn, 3:4], zst[0:rn, 3:4], 1.0 / D, None, AX.mult),
              wait=[(c_ve, None)], inc=[c_ve])
            E("vector", lambda ve: ve.tensor_tensor(
                zst[0:rn, 4:5], zst[0:rn, 0:1], zst[0:rn, 0:1], AX.mult),
              wait=[(c_ve, None)], inc=[c_ve])
            E("vector", lambda ve: ve.scalar_tensor_tensor(
                zst[0:rn, 4:5], zst[0:rn, 3:4], 1.0, zst[0:rn, 4:5],
                AX.bypass, AX.subtract),
              wait=[(c_ve, None)], inc=[c_ve])
            E("vector", lambda ve: ve.tensor_scalar(
                zst[0:rn, 4:5], zst[0:rn, 4:5], 1e-5, None, AX.add),
              wait=[(c_ve, None)], inc=[c_ve])
            E("scalar", lambda se: se.activation(
                zst[0:rn, 5:6], zst[0:rn, 4:5], ACT.Sqrt),
              wait=[(c_ve, None)], inc=[c_se])
            E("vector", lambda ve: ve.reciprocal(
                zst[0:rn, 6:7], zst[0:rn, 5:6]),
              wait=[(c_se, None)], inc=[c_ve])
            E("vector", lambda ve: ve.tensor_tensor(
                zst[0:rn, 6:7], zst[0:rn, 4:5], zst[0:rn, 6:7], AX.mult),
              wait=[(c_ve, None)], inc=[c_ve])
            E("vector", lambda ve: ve.tensor_tensor(
                zst[0:rn, 6:7], zst[0:rn, 6:7], zst[0:rn, 5:6], AX.add),
              wait=[(c_ve, None)], inc=[c_ve])
            E("vector", lambda ve: ve.tensor_scalar(
                zst[0:rn, 6:7], zst[0:rn, 6:7], 0.5, None, AX.mult),
              wait=[(c_ve, None)], inc=[c_ve])
            E("vector", lambda ve: ve.reciprocal(
                zst[0:rn, 1:2], zst[0:rn, 6:7]),
              wait=[(c_ve, None)], inc=[c_ve])
            return c_ve.val

        def recipn(dst, srcv, tmp, rn, w=()):
            E("vector", lambda ve: ve.reciprocal(dst, srcv),
              wait=list(w), inc=[c_ve])
            E("vector", lambda ve: ve.tensor_tensor(
                tmp, srcv, dst, AX.mult), wait=[(c_ve, None)], inc=[c_ve])
            E("vector", lambda ve: ve.tensor_scalar(
                tmp, tmp, -1.0, 2.0, AX.mult, AX.add),
              wait=[(c_ve, None)], inc=[c_ve])
            E("vector", lambda ve: ve.tensor_tensor(
                dst, dst, tmp, AX.mult), wait=[(c_ve, None)], inc=[c_ve])
            return c_ve.val

        def ladder(u_ap, w_ap, t_ap, q_ap, wait0):
            E("vector", lambda ve: ve.tensor_scalar(
                t_ap, u_ap, 128.0, -128.0, AX.is_ge, AX.mult),
              wait=list(wait0), inc=[c_ve])
            E("vector", lambda ve: ve.tensor_tensor(
                w_ap, u_ap, t_ap, AX.add),
              wait=[(c_ve, None)], inc=[c_ve])
            for k in (64.0, 32.0, 16.0, 8.0, 4.0, 2.0, 1.0):
                E("vector", lambda ve, k=k: ve.tensor_scalar(
                    t_ap, w_ap, k, -k, AX.is_ge, AX.mult),
                  wait=[(c_ve, None)], inc=[c_ve])
                E("vector", lambda ve: ve.tensor_tensor(
                    w_ap, w_ap, t_ap, AX.add),
                  wait=[(c_ve, None)], inc=[c_ve])
            E("vector", lambda ve: ve.tensor_tensor(
                t_ap, u_ap, w_ap, AX.subtract),
              wait=[(c_ve, None)], inc=[c_ve])
            E("vector", lambda ve: ve.tensor_scalar(
                q_ap, t_ap, 128.0, None, AX.subtract),
              wait=[(c_ve, None)], inc=[c_ve])
            return c_ve.val

        xq_evs = []
        for si in range(NSUB):
            par = si % 2
            r0, rn = SUBS[si]
            vz, vzv = ld(zld[par][:, :], z_park[si], q="gpsimd",
                         wait=[(stream_ctr(f"zp{par}"), 16 * (si // 2 + 1)),
                               (c_ve, None)],
                         stream=f"zl{par}")
            zin = zld[par][0:rn, :]
            prev_w = [(vz, vzv), (c_ve, None), (c_se, None)]
            stats_ve = ln_stats(zin, rn, prev_w)
            E("vector", lambda ve, zin=zin, rn=rn: ve.tensor_scalar(
                y1s[0:rn, :], zin, zst[0:rn, 0:1], zst[0:rn, 1:2],
                AX.subtract, AX.mult),
              wait=[(c_ve, stats_ve), (stream_ctr("y1p"), None),
                    (c_st, None)], inc=[c_ve])
            y1_ve = c_ve.val
            st(y1_park[si], y1s[:, :], wait=[(c_ve, y1_ve)],
               stream="y1p")
            if debug and "d_y1" in dbg:
                st(dbg["d_y1"][si], y1s[:, :], wait=[(c_ve, y1_ve)])
                dbar()
            for cc in range(NCH):
                b, bw = bank()
                E("tensor", lambda te, cc=cc, b=b: te.transpose(
                    pb[b][0:128, 0:128],
                    y1s[0:128, cc * 128:(cc + 1) * 128], ident_f[:, :]),
                  wait=bw + [(c_ve, y1_ve)], inc=[c_te])
                tv = c_te.val
                E("scalar", lambda se, cc=cc, b=b: se.activation(
                    pfm_h[:, cc * 128:(cc + 1) * 128],
                    pb[b][0:128, 0:128], ACT.Copy),
                  wait=[(c_te, tv)], inc=[c_se])
                E("vector", lambda ve, cc=cc, b=b:
                  ve.scalar_tensor_tensor(
                      pfm_l[:, cc * 128:(cc + 1) * 128],
                      pb[b][0:128, 0:128], 1.0,
                      pfm_h[:, cc * 128:(cc + 1) * 128],
                      AX.bypass, AX.subtract),
                  wait=[(c_se, None)], inc=[c_ve])
                bank_done(b)
            pfm_ve = c_ve.val
            ga_ev = []
            for g in range(4):
                b, bw = bank()
                n_mm = 0
                for j in range(4):
                    for tsrc in (pfm_h, pfm_l):
                        E("tensor", lambda te, g=g, j=j, tsrc=tsrc, b=b,
                            rn=rn, n_mm=n_mm: te.matmul(
                              pb[b][:, 0:512],
                              tsrc[:, (4 * g + j) * 128:
                                   (4 * g + j + 1) * 128],
                              hh4_b[:, j * 512:(j + 1) * 512],
                              start=(n_mm == 0), stop=(n_mm == 7)),
                          wait=(bw + [(c_ve, pfm_ve), (c_se, None)]
                                if n_mm == 0 else ()),
                          inc=([c_te] if n_mm == 7 else []))
                        n_mm += 1
                mmv = c_te.val
                E("scalar", lambda se, g=g, b=b, rn=rn: se.activation(
                    g1a[0:rn, g * 512:(g + 1) * 512], pb[b][0:rn, 0:512],
                    ACT.Copy),
                  wait=[(c_te, mmv)], inc=[c_se])
                bank_done(b, (c_se, c_se.val))
            E("vector", lambda ve, rn=rn: ve.tensor_tensor(
                g1b[0:rn, 0:512], g1a[0:rn, 0:512], g1a[0:rn, 512:1024],
                AX.add), wait=[(c_ve, None), (c_se, None)], inc=[c_ve])
            E("vector", lambda ve, rn=rn: ve.tensor_tensor(
                g1b[0:rn, 512:1024], g1a[0:rn, 0:512], g1a[0:rn, 512:1024],
                AX.subtract), inc=[c_ve])
            E("vector", lambda ve, rn=rn: ve.tensor_tensor(
                g1b[0:rn, 1024:1536], g1a[0:rn, 1024:1536],
                g1a[0:rn, 1536:2048], AX.add), inc=[c_ve])
            E("vector", lambda ve, rn=rn: ve.tensor_tensor(
                g1b[0:rn, 1536:2048], g1a[0:rn, 1024:1536],
                g1a[0:rn, 1536:2048], AX.subtract), inc=[c_ve])
            bst1 = c_ve.val
            E("vector", lambda ve, rn=rn: ve.tensor_tensor(
                g1a[0:rn, 0:512], g1b[0:rn, 0:512], g1b[0:rn, 1024:1536],
                AX.add), wait=[(c_ve, bst1)], inc=[c_ve])
            E("vector", lambda ve, rn=rn: ve.tensor_tensor(
                g1a[0:rn, 512:1024], g1b[0:rn, 512:1024],
                g1b[0:rn, 1536:2048], AX.add), inc=[c_ve])
            E("vector", lambda ve, rn=rn: ve.tensor_tensor(
                g1a[0:rn, 1024:1536], g1b[0:rn, 0:512],
                g1b[0:rn, 1024:1536], AX.subtract), inc=[c_ve])
            E("vector", lambda ve, rn=rn: ve.tensor_tensor(
                g1a[0:rn, 1536:2048], g1b[0:rn, 512:1024],
                g1b[0:rn, 1536:2048], AX.subtract), inc=[c_ve])
            g1_ve = c_ve.val
            E("vector", lambda ve, rn=rn: ve.tensor_reduce(
                zst[0:rn, 8:9], g1a[0:rn, :], AXL.X, AX.max,
                apply_absolute_value=True),
              wait=[(c_ve, g1_ve), (c_se, None)], inc=[c_ve])
            E("vector", lambda ve, rn=rn: ve.tensor_scalar(
                zst[0:rn, 8:9], zst[0:rn, 8:9], C1, 1e-5, AX.mult, AX.max),
              wait=[(c_ve, None)], inc=[c_ve])
            recipn(zst[0:rn, 10:11], zst[0:rn, 8:9], zst[0:rn, 11:12], rn,
                   w=[(c_ve, None)])
            E("vector", lambda ve, rn=rn, si=si: ve.tensor_scalar(
                s1t[0:rn, si:si + 1], zst[0:rn, 10:11], 127.0, None,
                AX.mult),
              wait=[(c_ve, None)], inc=[c_ve])
            E("vector", lambda ve, rn=rn, si=si: ve.tensor_scalar(
                zst[0:rn, 9:10], s1t[0:rn, si:si + 1], -C1, None, AX.mult),
              wait=[(c_ve, None)], inc=[c_ve])
            s1_ve = c_ve.val
            E("scalar", lambda se, rn=rn: se.activation(
                sq[0:rn, :], g1a[0:rn, :], ACT.Copy, bias=128.5,
                scale=zst[0:rn, 9:10]),
              wait=[(c_ve, s1_ve)], inc=[c_se])
            ladder(sq[0:rn, :], y1s[0:rn, :], zld[par][0:rn, :],
                   g1b[0:rn, :],
                   [(c_se, None), (stream_ctr("y1p"), None),
                    (stream_ctr(f"zl{par}"), None)])
            xqtm_ve = c_ve.val
            for cc in range(NCH):
                b, bw = bank()
                E("tensor", lambda te, cc=cc, b=b: te.transpose(
                    pb[b][0:128, 0:128],
                    g1b[0:128, cc * 128:(cc + 1) * 128], ident_f[:, :]),
                  wait=bw + [(c_ve, xqtm_ve)], inc=[c_te])
                tv = c_te.val
                E("scalar", lambda se, cc=cc, b=b, r0=r0, rn=rn:
                  se.activation(xqfm[:, cc * T + r0: cc * T + r0 + rn],
                                pb[b][0:128, 0:rn], ACT.Copy),
                  wait=[(c_te, tv)], inc=[c_se])
                bank_done(b, (c_se, c_se.val))
            xq_evs.append(c_ve.val)
        xq_ve = c_ve.val
        dstore("d_xq", xqfm[:, :].bitcast(F32)[:, 0:NCH * T // 2],
               wait=[(c_ve, xq_ve)])
        dstore("d_s1", s1t[:, 0:NSUB], wait=[(c_ve, xq_ve)])

        dbar()
        for buf in (y1s, zld[0], zld[1], pfm_h, pfm_l, g1a, g1b):
            AR.free(buf)


        QT = [(0, 256, (0, 1)), (256, 256, (2, 3)), (512, 256, (4, 5)),
              (768, 272, (6, 7, 8))]
        g2a = AR.alloc(DF, F32, tag="g2a")
        g2b = AR.alloc(DF, F32, tag="g2b")
        h2h = AR.alloc(NFCH * 272, BF16, tag="h2h")
        h2l = AR.alloc(NFCH * 272, BF16, tag="h2l")
        hqst = [AR.alloc(DF, BF16, tag="hqst")]
        wq2 = AR.alloc(DF, F32, tag="wq2")
        tq2 = AR.alloc(DF, F32, tag="tq2")
        wsl = [AR.alloc(NCH * 128, BF16, tag="wsl") for _ in range(2)]
        rfp = AR.alloc(272, F32, tag="rfp")
        pfp = AR.alloc(272, F32, tag="pfp")
        qst = AR.alloc(16, F32, tag="qst")
        h2_ve = {"v": 0}
        uste = {0: 0, 1: 0}
        for qi, (o4, n4, qsubs) in enumerate(QT):
            h2evs = []
            for fc in range(NFCH):
                par = fc % 2
                vc, vv = ld(wsl[par][:, :]
                            .rearrange("p (k f) -> p k f", f=128),
                            wup_int[:, :, fc * 128:(fc + 1) * 128]
                            .transpose([1, 0, 2]),
                            q="sync",
                            wait=([(wpark_ctr, wpark_st),
                               (wpark_ctr2, wpark_st2), (c_te, None)]
                                  if (qi == 0 and fc < 2)
                                  else [(c_te, uste[par])]),
                            stream=f"us{par}")
                b, bw = bank()
                for kk in range(NCH):
                    E("tensor", lambda te, fc=fc, kk=kk, b=b, par=par,
                        o4=o4, n4=n4: te.matmul(
                          pb[b][:, 0:n4],
                          wsl[par][:, kk * 128:(kk + 1) * 128],
                          xqfm[:, kk * T + o4: kk * T + o4 + n4],
                          start=(kk == 0), stop=(kk == 15)),
                      wait=(bw + [(vc, vv), (c_ve, xq_ve)] if kk == 0
                            else ()),
                      inc=([c_te] if kk == 15 else []))
                hmm_v = c_te.val
                uste[par] = hmm_v
                E("scalar", lambda se, b=b, n4=n4: se.activation(
                    rfp[:, 0:n4], pb[b][:, 0:n4], ACT.Relu),
                  wait=[(c_te, hmm_v), (c_ve, None)], inc=[c_se])
                bank_done(b, (c_se, c_se.val))
                E("vector", lambda ve, n4=n4: ve.tensor_tensor(
                    pfp[:, 0:n4], rfp[:, 0:n4], rfp[:, 0:n4], AX.mult),
                  wait=[(c_se, None)], inc=[c_ve])
                E("scalar", lambda se, fc=fc, n4=n4: se.activation(
                    h2h[:, fc * n4: fc * n4 + n4], pfp[:, 0:n4], ACT.Copy),
                  wait=[(c_ve, None)], inc=[c_se])
                E("vector", lambda ve, fc=fc, n4=n4: ve.scalar_tensor_tensor(
                    h2l[:, fc * n4: fc * n4 + n4], pfp[:, 0:n4], 1.0,
                    h2h[:, fc * n4: fc * n4 + n4], AX.bypass, AX.subtract),
                  wait=[(c_se, None)], inc=[c_ve])
                h2evs.append(c_ve.val)
            h2_ready = c_ve.val
            if debug and "d_h" in dbg and qi == 0:
                st(dbg["d_h"][:, 0:NFCH * 136],
                   h2h[:, 0:NFCH * 272].bitcast(F32),
                   wait=[(c_ve, h2_ready)])
                dbar()
            for sj, si in enumerate(qsubs):
                r0, rn = SUBS[si]
                so = r0 - o4
                ga_done = []
                for g in range(8):
                    b, bw = bank()
                    n_mm = 0
                    for j in range(4):
                        for tsrc in (h2h, h2l):
                            E("tensor", lambda te, g=g, j=j, tsrc=tsrc,
                                b=b, rn=rn, so=so, n4=n4, n_mm=n_mm:
                              te.matmul(
                                  pb[b][0:rn, 0:512],
                                  tsrc[:, (4 * g + j) * n4 + so:
                                       (4 * g + j) * n4 + so + rn],
                                  hh4_b[:, j * 512:(j + 1) * 512],
                                  start=(n_mm == 0), stop=(n_mm == 7)),
                              wait=(bw + [(c_ve, h2_ready)] if n_mm == 0
                                    else ()),
                              inc=([c_te] if n_mm == 7 else []))
                            n_mm += 1
                    mmv = c_te.val
                    E("scalar", lambda se, g=g, b=b, rn=rn:
                      se.activation(
                          g2a[0:rn, g * 512:(g + 1) * 512],
                          pb[b][0:rn, 0:512], ACT.Copy),
                      wait=[(c_te, mmv)], inc=[c_se])
                    bank_done(b, (c_se, c_se.val))
                def bfly(dst, srca, srcb, op, rn, w=None):
                    E("vector", lambda ve: ve.tensor_tensor(
                        dst[0:rn, :], srca[0:rn, :], srcb[0:rn, :], op),
                      wait=(w or ()), inc=[c_ve])
                BL = [g2a[:, i * 512:(i + 1) * 512] for i in range(8)]
                CL = [g2b[:, i * 512:(i + 1) * 512] for i in range(8)]
                w0_ = [(c_ve, c_ve.val), (c_se, c_se.val)]
                for i in range(4):
                    bfly(CL[i], BL[i], BL[i + 4], AX.add, rn,
                         w0_ if i == 0 else None)
                    bfly(CL[i + 4], BL[i], BL[i + 4], AX.subtract, rn)
                wv_ = [(c_ve, c_ve.val)]
                for base in (0, 4):
                    for i in range(2):
                        bfly(BL[base + i], CL[base + i], CL[base + i + 2],
                             AX.add, rn, wv_ if (base + i) == 0 else None)
                        bfly(BL[base + i + 2], CL[base + i],
                             CL[base + i + 2], AX.subtract, rn)
                wv2_ = [(c_ve, c_ve.val)]
                for base in (0, 2, 4, 6):
                    bfly(CL[base], BL[base], BL[base + 1], AX.add, rn,
                         wv2_ if base == 0 else None)
                    bfly(CL[base + 1], BL[base], BL[base + 1], AX.subtract,
                         rn)
                g2_ve = c_ve.val
                E("vector", lambda ve, rn=rn: ve.tensor_reduce(
                    qst[0:rn, 0:1], g2b[0:rn, :], AXL.X, AX.max,
                    apply_absolute_value=True),
                  wait=[(c_ve, g2_ve), (c_se, None)], inc=[c_ve])
                recipn(qst[0:rn, 6:7], s1t[0:rn, si:si + 1],
                       qst[0:rn, 7:8], rn,
                       w=[(c_ve, None), (c_ve, thr_ve)])
                E("vector", lambda ve, rn=rn: ve.tensor_tensor(
                    qst[0:rn, 1:2], wscb[0:rn, 0:1], qst[0:rn, 6:7],
                    AX.mult), wait=[(c_ve, None)], inc=[c_ve])
                E("vector", lambda ve, rn=rn: ve.tensor_tensor(
                    qst[0:rn, 2:3], qst[0:rn, 1:2], qst[0:rn, 1:2],
                    AX.mult), wait=[(c_ve, None)], inc=[c_ve])
                E("vector", lambda ve, rn=rn: ve.tensor_scalar(
                    qst[0:rn, 2:3], qst[0:rn, 2:3], C2, None, AX.mult),
                  wait=[(c_ve, None)], inc=[c_ve])
                E("vector", lambda ve, rn=rn: ve.tensor_tensor(
                    qst[0:rn, 3:4], qst[0:rn, 0:1], qst[0:rn, 2:3],
                    AX.mult), wait=[(c_ve, None)], inc=[c_ve])
                E("vector", lambda ve, rn=rn: ve.tensor_scalar(
                    qst[0:rn, 3:4], qst[0:rn, 3:4], 1e-5, None, AX.max),
                  wait=[(c_ve, None)], inc=[c_ve])
                recipn(qst[0:rn, 6:7], qst[0:rn, 3:4], qst[0:rn, 7:8], rn,
                       w=[(c_ve, None)])
                E("vector", lambda ve, rn=rn: ve.tensor_scalar(
                    qst[0:rn, 4:5], qst[0:rn, 6:7], 127.0, None, AX.mult),
                  wait=[(c_ve, None)], inc=[c_ve])
                E("vector", lambda ve, rn=rn: ve.tensor_tensor(
                    qst[0:rn, 5:6], qst[0:rn, 4:5], qst[0:rn, 2:3],
                    AX.mult), wait=[(c_ve, None)], inc=[c_ve])
                E("vector", lambda ve, rn=rn: ve.tensor_scalar(
                    qst[0:rn, 5:6], qst[0:rn, 5:6], -1.0, None, AX.mult),
                  wait=[(c_ve, None)], inc=[c_ve])
                recipn(qst[0:rn, 6:7], qst[0:rn, 4:5], qst[0:rn, 7:8], rn,
                       w=[(c_ve, None)])
                E("vector", lambda ve, rn=rn, si=si: ve.tensor_tensor(
                    o2t[0:rn, si:si + 1], wscb[0:rn, 1:2], qst[0:rn, 6:7],
                    AX.mult), wait=[(c_ve, None)], inc=[c_ve])
                q2_ve = c_ve.val
                E("scalar", lambda se, rn=rn: se.activation(
                    g2a[0:rn, :], g2b[0:rn, :], ACT.Copy, bias=128.5,
                    scale=qst[0:rn, 5:6]),
                  wait=[(c_ve, q2_ve)], inc=[c_se])
                ladder(g2a[0:rn, :], wq2[0:rn, :], tq2[0:rn, :],
                       g2b[0:rn, :], [(c_se, None)])
                hq_ve = c_ve.val
                hpar = 0
                for fcc in range(NFCH):
                    b, bw = bank()
                    E("tensor", lambda te, fcc=fcc, b=b: te.transpose(
                        pb[b][0:128, 0:128],
                        g2b[0:128, fcc * 128:(fcc + 1) * 128],
                        ident_f[:, :]),
                      wait=bw + [(c_ve, hq_ve)]
                      + ([(stream_ctr(f"hqp{hpar}"), None)]
                         if fcc == 0 else []),
                      inc=[c_te])
                    tv = c_te.val
                    E("scalar", lambda se, fcc=fcc, b=b, rn=rn, hpar=hpar:
                      se.activation(
                          hqst[hpar][:, fcc * 128:fcc * 128 + rn],
                          pb[b][0:128, 0:rn], ACT.Copy),
                      wait=[(c_te, tv)], inc=[c_se])
                    bank_done(b, (c_se, c_se.val))
                E("sync", lambda e, r0=r0, rn=rn, hpar=hpar: e.dma_start(
                    out=hq_park[:, :, r0:r0 + rn].transpose([1, 0, 2]),
                    in_=hqst[hpar][:, :]
                    .rearrange("p (f c) -> p f c", c=128)[:, :, 0:rn]),
                  wait=[(c_ve, None), (c_se, None)],
                  inc=[stream_ctr(f"hqp{hpar}")], incn=16)
            h2_ve["v"] = c_ve.val
        hq_done = c_ve.val
        dstore("d_o2", o2t[:, 0:NSUB], wait=[(c_ve, hq_done)])

        dbar()
        for buf in (h2h, h2l, g2a, g2b, rfp, pfp, hqst[0],
                    wq2, tq2, xqfm, wsl[0], wsl[1]):
            AR.free(buf)

        ffnfm = AR.alloc(NCH * T, F32, tag="ffnfm")
        hqfm = AR.alloc(NFCH * T, BF16, tag="hqfm")
        wsl_d = [AR.alloc(NFCH * 128, BF16, tag="wsl_dn")
                 for _ in range(2)]
        hql_v = []
        for fc in range(NFCH):
            sk = f"hql{fc % 4}"
            _, hv = ld(hqfm[:, fc * T:(fc + 1) * T], hq_park[fc],
                       q="gpsimd",
                       wait=([(stream_ctr("hqp0"), None)] if fc < 4
                             else [(stream_ctr(sk), None)]),
                       stream=sk)
            hql_v.append((stream_ctr(sk), hv))
        dn_evs = []
        dste = {0: 0, 1: 0}
        for dc in range(NCH):
            par = dc % 2
            vc, vv = ld(wsl_d[par][:, 0:NFCH * 128]
                        .rearrange("p (k f) -> p k f", f=128),
                        wdn_int[:, :, dc * 128:(dc + 1) * 128]
                        .transpose([1, 0, 2]),
                        q="sync",
                        wait=([(c_te, dste[par])] if dc >= 2
                              else [(wpark_ctr, wpark_st),
                                    (wpark_ctr2, wpark_st2)]),
                        stream=f"us{par}")
            for gi, (g0, gn) in enumerate(TG):
                b, bw = bank()
                for fc in range(NFCH):
                    E("tensor", lambda te, dc=dc, fc=fc, b=b, par=par,
                        g0=g0, gn=gn: te.matmul(
                          pb[b][:, 0:gn],
                          wsl_d[par][:, fc * 128:(fc + 1) * 128],
                          hqfm[:, fc * T + g0: fc * T + g0 + gn],
                          start=(fc == 0), stop=(fc == 31)),
                      wait=((bw + [(vc, vv), (c_ve, hq_done)]
                             if fc == 0 else ())
                            if dc + gi > 0 else
                            (bw + [(vc, vv), (c_ve, hq_done),
                                   hql_v[fc]] if fc == 0
                             else [hql_v[fc]])),
                      inc=([c_te] if fc == 31 else []))
                dmm = c_te.val
                E("scalar", lambda se, dc=dc, g0=g0, gn=gn, b=b:
                  se.activation(ffnfm[:, dc * T + g0: dc * T + g0 + gn],
                                pb[b][:, 0:gn], ACT.Copy),
                  wait=[(c_te, dmm)], inc=[c_se])
                bank_done(b, (c_se, c_se.val))
            dste[par] = c_te.val
        ffn_se = c_se.val
        dstore("d_ffn", ffnfm[:, :], wait=[(c_se, ffn_se)])

        z2s = AR.alloc(D, F32, tag="z2s")
        y1r = AR.alloc(D, F32, tag="y1r")
        y2s = AR.alloc(D, F32, tag="y2s")
        final_sts = []
        for si in range(NSUB):
            r0, rn = SUBS[si]
            vc, vv = ld(y1r[:, :], y1_park[si], q="gpsimd",
                        wait=[(c_ve, None),
                              (stream_ctr("y1p"), None)],
                        stream="y1r")
            for cc in range(NCH):
                b, bw = bank()
                E("tensor", lambda te, cc=cc, b=b, r0=r0, rn=rn:
                  te.transpose(
                      pb[b][0:rn, 0:128],
                      ffnfm[0:128, cc * T + r0: cc * T + r0 + rn],
                      ident_f[:, :]),
                  wait=bw + [(c_se, ffn_se)], inc=[c_te])
                tv = c_te.val
                E("vector", lambda ve, cc=cc, b=b, rn=rn, si=si:
                  ve.scalar_tensor_tensor(
                      z2s[0:rn, cc * 128:(cc + 1) * 128],
                      pb[b][0:rn, 0:128], o2t[0:rn, si:si + 1],
                      y1r[0:rn, cc * 128:(cc + 1) * 128],
                      AX.mult, AX.add),
                  wait=[(c_te, tv), (vc, vv)], inc=[c_ve])
                bank_done(b)
            z2_ve = c_ve.val
            stats_ve = ln_stats(z2s[0:rn, :], rn,
                                [(c_ve, z2_ve), (c_se, None)])
            E("vector", lambda ve, rn=rn: ve.tensor_scalar(
                y2s[0:rn, :], z2s[0:rn, :], zst[0:rn, 0:1], zst[0:rn, 1:2],
                AX.subtract, AX.mult),
              wait=[(c_ve, stats_ve), (c_st, None)], inc=[c_ve])
            y2_ve = c_ve.val
            final_sts.append(st(y_out[r0:r0 + rn, :], y2s[0:rn, :],
                                wait=[(c_ve, y2_ve)]))
        fv = c_st.val
        E("sync", lambda e: None,
          wait=[(c_st, fv), (c_ldg, None)]
          + [(cc, None) for cc in streams.values()])
        plan.run(ent(nc.Block()))
    return nc, dram, dbg
def _finish_stub(E, AR, c_ve, c_st, y_out, zz, waits=()):
    E("vector", lambda ve: ve.memset(zz[:, 0:64], 0.0),
      wait=list(waits), inc=[c_ve])
    v = c_ve.val
    for i in range(NSUB):
        r0, rn = SUBS[i]
        for j in range(D // 64):
            E("sync", lambda e, r0=r0, rn=rn, j=j: e.dma_start(
                out=y_out[r0:r0 + rn, 64 * j:64 * j + 64],
                in_=zz[0:rn, 0:64]),
              wait=[(c_ve, v)], inc=[c_st], incn=16)
    fv = c_st.val
    E("sync", lambda e: None, wait=[(c_st, fv)])


def kernel(**inputs):
    maps = _stage(inputs)
    nc, dram, dbg = build_program(debug=0)
    res = run_bass_kernel_spmd(nc, maps, core_ids=list(range(NCORES)))
    out = np.zeros((S, D), dtype=np.float32)
    for c in range(NCORES):
        out[T0S[c]:T0S[c] + SIZES[c]] = res.results[c]["y_out"][:SIZES[c]]
    return out.reshape(1, S, D)


# version 18 (final)
# speedup vs baseline: 1.1598x; 1.0499x over previous
import math
import contextlib
import numpy as np

import concourse.bass as bass
import concourse.mybir as mybir
from concourse.ap import AP
from concourse.bass_utils import run_bass_kernel_spmd

F32 = mybir.dt.float32
I8 = mybir.dt.int8
BF16 = mybir.dt.bfloat16
AX = mybir.AluOpType
ACT = mybir.ActivationFunctionType
AXL = mybir.AxisListType

S, D, DF = 8192, 2048, 4096
NH = 16
NCORES = 8
T, B = 1040, 104
NW, WSTR, RPW = 10, 110, 11
NR = NW * RPW
NCH, NFCH = 16, 32
NSUB = 9
SUBS = [(i * 128, (128 if i < 8 else 16)) for i in range(NSUB)]
TG = [(0, 512), (512, 512), (1024, 16)]
THIRDS = [(0, 384), (384, 384), (768, 272)]
T0S = [(1024 * c) // 10 * 10 for c in range(NCORES)]
SIZES = [T0S[c + 1] - T0S[c] for c in range(NCORES - 1)] + [S - T0S[-1]]
ISQD = 1.0 / math.sqrt(128.0)
NEG_BIG = -1.0e30
C1 = 1.0 / math.sqrt(float(D))
B3 = 12582912.0
C2 = 1.0 / math.sqrt(float(DF))
WMEAN = 1.0 / (D * DF)
GBW = [(0, 2), (2, 2), (4, 2), (6, 2), (8, 2)]


class Ctr:
    __slots__ = ("sem", "val", "name")

    def __init__(self, sem, name):
        self.sem, self.val, self.name = sem, 0, name


class Plan:
    def __init__(self):
        self.steps = {k: [] for k in
                      ("tensor", "vector", "scalar", "gpsimd", "sync")}
        self.pending = {k: [] for k in self.steps}
        self.auto = {}

    def emit(self, eng, fn, wait=(), inc=(), incn=1):
        waits = [(c, (c.val if v is None else v)) for (c, v) in wait if c]
        inc = list(inc)
        ac = self.auto.get(eng)
        if ac is not None and incn == 1:
            if eng != "tensor":
                waits.append((ac, ac.val))
            if ac not in inc:
                inc.append(ac)
        if self.pending[eng]:
            waits = self.pending[eng] + waits
            self.pending[eng] = []
        for c in inc:
            c.val += incn
        self.steps[eng].append((fn, waits, list(inc), incn))

    def barrier(self, pairs):
        pairs = [(c, (c.val if v is None else v)) for (c, v) in pairs]
        for k in self.pending:
            merged = {id(c): (c, v) for (c, v) in self.pending[k]}
            for c, v in pairs:
                if id(c) not in merged or merged[id(c)][1] < v:
                    merged[id(c)] = (c, v)
            self.pending[k] = list(merged.values())

    def run(self, block):
        def mk(steps):
            def body(eng):
                for fn, waits, incs, incn in steps:
                    for c, v in waits:
                        if v > 0:
                            eng.wait_ge(c.sem, v)
                    inst = fn(eng)
                    if inst is not None:
                        for c in incs:
                            inst.then_inc(c.sem, incn)
            return body
        for name in self.steps:
            getattr(block, name)(mk(self.steps[name]))


class Arena:

    def __init__(self, name):
        self.name = name
        self.pools = {}
        self.live = {}

    def set_pool(self, key, tensor, nbytes):
        self.pools[key] = [tensor, nbytes, [(0, nbytes)], 0]

    @staticmethod
    def _key(dt):
        if dt == BF16:
            return "b"
        if dt == I8:
            return "i"
        return "f"

    def alloc(self, cols, dt=F32, align=64, tag=""):
        esz = {"b": 2, "i": 1}.get(self._key(dt), 4)
        key = self._key(dt)
        tensor, nbytes, free_list, _ = self.pools[key]
        need = cols * esz
        order = (range(len(free_list)) if need > 1024 else
                 range(len(free_list) - 1, -1, -1))
        for idx in order:
            off, ln = free_list[idx]
            if need <= 1024:
                aoff = (off + ln - need) // align * align
                if aoff < off:
                    continue
            else:
                aoff = -(-off // align) * align
            if aoff + need <= off + ln:
                pre = (off, aoff - off)
                post = (aoff + need, off + ln - (aoff + need))
                repl = [p for p in (pre, post) if p[1] > 0]
                free_list[idx:idx + 1] = repl
                view = tensor.ap()[:, aoff // esz: aoff // esz + cols]
                self.live[id(view)] = (key, aoff, need, tag)
                used = nbytes - sum(l for _, l in free_list)
                self.pools[key][3] = max(self.pools[key][3], used)
                return view
        raise AssertionError(
            f"arena {self.name}/{key} OOM for {need}B tag={tag}; "
            f"free={free_list}")

    def free(self, view):
        key, off, ln, _ = self.live.pop(id(view))
        free_list = self.pools[key][2]
        free_list.append((off, ln))
        free_list.sort()
        merged = []
        for o, l in free_list:
            if merged and merged[-1][0] + merged[-1][1] == o:
                merged[-1] = (merged[-1][0], merged[-1][1] + l)
            else:
                merged.append((o, l))
        self.pools[key][2] = [tuple(m) for m in merged]


def _hadamard(n):
    h = np.array([[1.0]], dtype=np.float32)
    while h.shape[0] < n:
        h = np.block([[h, h], [h, -h]])
    return h.astype(np.float32)


def _make_consts():
    h128, h4 = _hadamard(128), _hadamard(4)
    hh4 = np.zeros((128, 4 * 512), dtype=np.float32)
    for j in range(4):
        for cp in range(4):
            hh4[:, j * 512 + cp * 128: j * 512 + cp * 128 + 128] = \
                h4[cp, j] * h128
    maskb = np.full((128, 128), NEG_BIG, dtype=np.float32)
    mask01 = np.zeros((128, 128), dtype=np.float32)
    for wq in range(4):
        for ip in range(RPW):
            p = 32 * wq + ip
            maskb[p, 10 * ip: 10 * ip + 10] = 0.0
            mask01[p, 10 * ip: 10 * ip + 10] = 1.0
    sel = np.zeros((16, NSUB * 128), dtype=np.float32)
    for si in range(NSUB):
        r0, rn = SUBS[si]
        i0 = -(-r0 // 10)
        i1 = -(-(r0 + rn) // 10)
        for b_ in range(i1 - i0):
            c = 10 * (i0 + b_) - r0
            if 0 <= c < rn:
                sel[b_, si * 128 + c] = 1.0
    return {
        "hh4": hh4,
        "ident": np.eye(128, dtype=np.float32),
        "maskb": np.ascontiguousarray(np.tile(maskb, (1, NH))),
        "mask01": np.ascontiguousarray(np.tile(mask01, (1, NH))),
        "sel": sel,
    }


def _stage(inputs):
    x = np.ascontiguousarray(
        np.asarray(inputs["L0_attn_hidden_states"], np.float32).reshape(S, D))
    shared = {
        "x_full": x,
        "wqt": np.ascontiguousarray(np.asarray(inputs["L0_attn_wq"],
                                               np.float32)
                                    .reshape(D, NH, 128).transpose(1, 0, 2)),
        "wkt": np.ascontiguousarray(np.asarray(inputs["L0_attn_wk"],
                                               np.float32)
                                    .reshape(D, NH, 128).transpose(1, 0, 2)),
        "wv": np.ascontiguousarray(np.asarray(inputs["L0_attn_wv"],
                                              np.float32)),
        "wo": np.ascontiguousarray(np.asarray(inputs["L0_attn_wo"],
                                              np.float32)),
        "wupT": np.ascontiguousarray(np.asarray(inputs["L0_ffn_w_up"],
                                                np.float32).T),
        "wdownT": np.ascontiguousarray(np.asarray(inputs["L0_ffn_w_down"],
                                                  np.float32).T),
        "bias": np.stack([np.asarray(inputs[k], np.float32) for k in
                          ("L0_attn_bq", "L0_attn_bk", "L0_attn_bv",
                           "L0_attn_bo")]),
        **_make_consts(),
    }
    maps = []
    for c in range(NCORES):
        t0 = T0S[c]
        valid = min(S - t0, T)
        xtm = np.zeros((T, D), dtype=np.float32)
        xtm[:valid] = x[t0:t0 + valid]
        m = dict(shared)
        m["xtm"] = xtm
        m["xfm"] = np.ascontiguousarray(xtm.T)
        m["onesv"] = (np.arange(T) < valid).astype(np.float32).reshape(1, T)
        maps.append(m)
    return maps



def apx(view, off_elems, dims):
    return AP(view.tensor, view.offset + off_elems,
              [list(view.ap[0])] + [list(d) for d in dims])

def build_program(debug=0, phases=99):
    nc = bass.Bass("TRN2", target_bir_lowering=False)
    dram = {}

    def din(name, shape, dt=F32):
        dram[name] = nc.dram_tensor(name, shape, dt, kind="ExternalInput").ap()
        return dram[name]

    x_full = din("x_full", [S, D])
    xtm = din("xtm", [T, D])
    xfm = din("xfm", [D, T])
    wqt = din("wqt", [NH, D, 128])
    wkt = din("wkt", [NH, D, 128])
    wv_d = din("wv", [D, D])
    wo_d = din("wo", [D, D])
    wupT = din("wupT", [D, DF])
    wdownT = din("wdownT", [DF, D])
    bias_d = din("bias", [4, D])
    hh4_d = din("hh4", [128, 4 * 512])
    ident_d = din("ident", [128, 128])
    maskb_d = din("maskb", [128, NH * 128])
    mask01_d = din("mask01", [128, NH * 128])
    sel_d = din("sel", [16, NSUB * 128])
    onesv_d = din("onesv", [1, T])

    y_out = nc.dram_tensor("y_out", [T, D], F32, kind="ExternalOutput").ap()
    wup_int = nc.dram_tensor("wup_int", [NCH, 128, DF], BF16,
                             kind="Internal").ap()
    wdn_int = nc.dram_tensor("wdn_int", [NFCH, 128, D], BF16,
                             kind="Internal").ap()
    z_park = nc.dram_tensor("z_park", [NSUB, 128, D], F32,
                            kind="Internal").ap()
    hq_park = nc.dram_tensor("hq_park", [NFCH, 128, T], BF16,
                             kind="Internal").ap()
    y1_park = nc.dram_tensor("y1_park", [NSUB, 128, D], F32,
                             kind="Internal").ap()

    dbg = {}

    def dbg_out(name, shape, dt=F32):
        if debug:
            dbg[name] = nc.dram_tensor(name, shape, dt,
                                       kind="ExternalOutput").ap()

    for nm, shp in [("d_colsum", [1, D]), ("d_q", [128, NH * B]),
                    ("d_swide", [128, 3 * NH * 128]),
                    ("d_ekm", [128, NW * NH * RPW]),
                    ("d_epi", [128, 6 * 3 * NH]), ("d_ctx", [128, NH * B]),
                    ("d_gb", [128, NCH * 2 * NH * RPW]),
                    ("d_ctx0", [128, NH * B]),
                    ("d_sv", [1, D]), ("d_u", [65, D]), ("d_a", [65, T]),
                    ("d_z", [NSUB, 128, D]), ("d_y1", [NSUB, 128, D]),
                    ("d_xq", [128, NCH * T // 2]), ("d_wsc", [1, 6]),
                    ("d_h", [128, NFCH * 136]), ("d_hq", [128, NFCH * T // 2]),
                    ("d_ffn", [128, NCH * T]), ("d_s1", [128, NSUB]),
                    ("d_o2", [128, NSUB])]:
        dbg_out(nm, shp)

    plan = Plan()
    E = plan.emit
    stk = contextlib.ExitStack()
    with stk:
        ent = stk.enter_context
        AR = Arena("A")
        AR.set_pool("f", ent(nc.sbuf_tensor(
            "arena", [128, 122624 // 4], F32)), 122624)
        AR.set_pool("b", ent(nc.sbuf_tensor(
            "arenab", [128, 90112 // 2], BF16)), 90112)
        pb = [ent(nc.psum_tensor(f"pb{i}", [128, 512], F32)) for i in range(8)]

        def ctr(name):
            return Ctr(ent(nc.semaphore(name)), name)

        c_ld = ctr("c_ld")
        c_ldg = ctr("c_ldg")
        c_st = ctr("c_st")
        c_te = ctr("c_te")
        c_ve = ctr("c_ve")
        c_se = ctr("c_se")
        c_ge = ctr("c_ge")
        plan.auto = {"tensor": c_te, "vector": c_ve, "scalar": c_se,
                     "gpsimd": c_ge}

        streams = {}

        def stream_ctr(name):
            if name not in streams:
                streams[name] = ctr("cs_" + name)
            return streams[name]

        def ld(dst, src, q="sync", wait=(), stream=None):
            cc = stream_ctr(stream) if stream else \
                (c_ld if q == "sync" else c_ldg)
            E(q, lambda e, d=dst, s=src: e.dma_start(out=d, in_=s),
              wait=wait, inc=[cc], incn=16)
            return (cc, cc.val)

        def lda(dst, src, wait=(), stream=None):
            cc = stream_ctr(stream) if stream else c_ldg
            E("gpsimd", lambda e, d=dst, s=src: e.dma_start(
                out=d, in_=s, accum_op=AX.add),
              wait=wait, inc=[cc], incn=16)
            return (cc, cc.val)

        def st(dst, src, wait=(), stream=None):
            cc = stream_ctr(stream) if stream else c_st
            E("sync", lambda e, d=dst, s=src: e.dma_start(out=d, in_=s),
              wait=wait, inc=[cc], incn=16)
            return (cc, cc.val)

        def dstore(name, src, wait=(), stream=None):
            if debug and name in dbg:
                r = st(dbg[name], src, wait=wait, stream=stream)
                dbar()
                return r
            return None

        def dbar():
            plan.barrier([(c_ve, None), (c_se, None), (c_te, None),
                          (c_ge, None), (c_st, None), (c_ld, None),
                          (c_ldg, None)]
                         + [(cc, None) for cc in streams.values()])

        hh4_b = AR.alloc(4 * 512, BF16)
        ident_f = AR.alloc(128, F32)
        maskb_s = AR.alloc(NH * 128, F32)
        mask01_s = AR.alloc(NH * 128, F32)
        biasr = AR.alloc(D, F32)
        onesc = AR.alloc(16, F32)
        onesr = AR.alloc(T, F32)
        sconst = AR.alloc(16, F32)
        cscr = AR.alloc(4 * 512, F32)

        ld(cscr[:, 0:2048], hh4_d)
        ld(ident_f[:, :], ident_d)
        ld(maskb_s[:, :], maskb_d)
        ld(mask01_s[:, :], mask01_d)
        for bi4 in range(4):
            ld(biasr[32 * bi4:32 * bi4 + 1, :], bias_d[bi4:bi4 + 1, :])
        consts_ld = c_ld.val
        E("vector", lambda ve: ve.tensor_copy(hh4_b[:, :], cscr[:, 0:2048]),
          wait=[(c_ld, consts_ld)], inc=[c_ve])
        AR.free(cscr)
        E("vector", lambda ve: ve.memset(onesc[:, :], 1.0), inc=[c_ve])
        E("vector", lambda ve: ve.memset(onesr[:, :], 1.0), inc=[c_ve])
        E("vector", lambda ve: ve.memset(sconst[:, :], float(S)), inc=[c_ve])
        consts_ve = c_ve.val
        ld(onesr[0:1, :], onesv_d, wait=[(c_ve, consts_ve)])
        ld(onesr[32:33, :], onesv_d, wait=[(c_ve, consts_ve)])
        consts_ld = c_ld.val

        wblk = [AR.alloc(1024, F32) for _ in range(2)]
        t1w = [AR.alloc(1024, F32)]
        wintl = [AR.alloc(1024, BF16)]
        wabs = AR.alloc(16, F32)
        thrp = AR.alloc(16, F32)
        wsc = AR.alloc(16, F32)
        WBLOCKS = []
        for b2 in range(NCH):
            for hf in range(4):
                WBLOCKS.append((0, wupT[b2 * 128:(b2 + 1) * 128,
                                        hf * 1024:(hf + 1) * 1024],
                                wup_int[b2][:, hf * 1024:(hf + 1) * 1024]))
        for b2 in range(NFCH):
            for hf in range(2):
                WBLOCKS.append((1, wdownT[b2 * 128:(b2 + 1) * 128,
                                          hf * 1024:(hf + 1) * 1024],
                                wdn_int[b2][:, hf * 1024:(hf + 1) * 1024]))
        N_UP = NCH * 4

        def emit_w1(i):
            mat, srcs, dsts = WBLOCKS[i]
            par = i % 2
            vc, vv = ld(wblk[par][:, :], srcs, q="sync",
                        wait=[(c_ve, None)],
                        stream=f"wb{par}")
            E("vector", lambda ve, par=par: ve.tensor_reduce(
                wabs[:, 8:9], wblk[par][:, :], AXL.X, AX.add,
                apply_absolute_value=True),
              wait=[(vc, vv), (c_ve, None)], inc=[c_ve])
            if i == 0 or i == N_UP:
                E("vector", lambda ve, mat=mat: ve.tensor_copy(
                    wabs[:, mat:mat + 1], wabs[:, 8:9]),
                  wait=[(c_ve, None)], inc=[c_ve])
            else:
                E("vector", lambda ve, mat=mat: ve.tensor_tensor(
                    wabs[:, mat:mat + 1], wabs[:, mat:mat + 1],
                    wabs[:, 8:9], AX.add),
                  wait=[(c_ve, None)], inc=[c_ve])

        wp_done = {0: 0, 1: 0}

        def emit_w2(i, thr_ve):
            mat, srcs, dsts = WBLOCKS[i]
            par = i % 2
            wint = wintl[par % len(wintl)]
            t1 = t1w[par % len(t1w)]
            wps = stream_ctr(f"wp{par % len(wintl)}")
            vc, vv = ld(wblk[par][:, :], srcs,
                        q=("sync" if par == 0 else "gpsimd"),
                        wait=[(c_ve, None)],
                        stream=(f"wb{par}" if par == 0 else f"wg{par}"))
            thr_bc = AP(thrp.tensor, thrp.offset + mat,
                        [list(thrp.ap[0]), [0, 1024]])
            E("vector", lambda ve, par=par, thr_bc=thr_bc, t1=t1:
              ve.tensor_tensor(t1[:, :], wblk[par][:, :],
                               thr_bc, AX.is_gt),
              wait=[(vc, vv), (c_ve, thr_ve), (c_ve, None)],
              inc=[c_ve])
            E("vector", lambda ve, par=par, mat=mat, wint=wint, t1=t1:
              ve.scalar_tensor_tensor(
                  wint[:, :], wblk[par][:, :],
                  thrp[:, 4 + mat:5 + mat], t1[:, :],
                  AX.is_lt, AX.subtract),
              wait=[(c_ve, None), (wps, wp_done[par % len(wintl)])],
              inc=[c_ve])
            wq_ve = c_ve.val
            E("sync", lambda e, dsts=dsts, wint=wint: e.dma_start(
                out=dsts, in_=wint[:, :]),
              wait=[(c_ve, wq_ve)], inc=[wps], incn=16)
            wp_done[par % len(wintl)] = wps.val


        xh = AR.alloc(NCH * T, BF16)
        xl = AR.alloc(NCH * T, BF16)
        xload = [AR.alloc(T, F32) for _ in range(2)]
        wsl_f = [AR.alloc(NCH * 128, F32) for _ in range(2)]
        wsl_h = [AR.alloc(NCH * 128, BF16) for _ in range(2)]
        wsl_l = [AR.alloc(NCH * 128, BF16) for _ in range(2)]
        kf = [AR.alloc(T, F32) for _ in range(2)]
        xtailf = AR.alloc(NCH * 16, F32)
        qc = AR.alloc(NH * B, F32)
        swide = AR.alloc(3 * NH * 128, F32)
        cx = [AR.alloc(512, F32) for _ in range(2)]
        csum = AR.alloc(D, F32)

        E("vector", lambda ve: ve.memset(swide[:, :], 0.0), inc=[c_ve])
        swide_zeroed = c_ve.val

        xsp = []
        for blk in range(16):
            par = blk % 2
            w = [(c_ve, xsp[blk - 2])] if blk >= 2 else ()
            vc, vv = ld(xload[par][:, :],
                        xfm[blk * 128:(blk + 1) * 128, :],
                        q="gpsimd", wait=w, stream=f"xl{par}")
            sl = slice(blk * T, (blk + 1) * T)
            E("scalar", lambda se, sl=sl, par=par: se.activation(
                xh[:, sl], xload[par][:, :], ACT.Copy),
              wait=[(vc, vv)], inc=[c_se])
            E("scalar", lambda se, blk=blk, par=par: se.activation(
                xtailf[:, blk * 16:(blk + 1) * 16],
                xload[par][:, 1024:1040], ACT.Copy),
              wait=[(vc, vv)], inc=[c_se])
            E("vector", lambda ve, sl=sl, par=par: ve.scalar_tensor_tensor(
                xl[:, sl], xload[par][:, :], 1.0, xh[:, sl], AX.bypass,
                AX.subtract),
              wait=[(c_se, None)], inc=[c_ve])
            xsp.append(c_ve.val)
        xsplit_ve = c_ve.val
        xsplit_se = c_se.val

        cs = {"tt": 0, "incs": []}

        def colsum_step(n):
            for _ in range(n):
                tt = cs["tt"]
                if tt >= 256:
                    return
                cs["tt"] += 1
                par = tt % 2
                rb, qb = tt // 4, tt % 4
                w = [(c_te, cs["incs"][tt - 2])] if tt >= 2 else ()
                vc, vv = ld(cx[par][:, :],
                            x_full[rb * 128:(rb + 1) * 128,
                                   qb * 512:(qb + 1) * 512],
                            q="gpsimd", wait=w, stream=f"cs{par}")
                E("tensor", lambda te, par=par, tt=tt, qb=qb:
                  te.matmul(
                      pb[qb][0:1, 0:512], onesc[:, 0:1],
                      cx[par][:, 0:512],
                      start=(tt < 4), stop=(tt >= 252)),
                  wait=[(vc, vv), (c_ve, consts_ve)],
                  inc=[c_te])
                cs["incs"].append(c_te.val)

        kev = [0, 0]
        scev = [0, 0]
        prev_scores_ve = swide_zeroed
        prev_k_mm = 0
        wsl_free = [0, 0]
        for h in range(NH):
            par = h % 2
            colsum_step(8)
            vqc, vqv = ld(wsl_f[par][:, :]
                          .rearrange("p (c m) -> p c m", m=128),
                          wqt[h].rearrange("(c p) m -> c p m", p=128)
                          .transpose([1, 0, 2]),
                          wait=([(c_ve, None),
                                 (c_te, wsl_free[par])]
                                if h >= 2 else ()),
                          stream=f"ws{par}")
            E("scalar", lambda se, par=par: se.activation(
                wsl_h[par][:, :], wsl_f[par][:, :], ACT.Copy),
              wait=[(vqc, vqv), (c_te, wsl_free[par])], inc=[c_se])
            E("vector", lambda ve, par=par: ve.scalar_tensor_tensor(
                wsl_l[par][:, :], wsl_f[par][:, :], 1.0, wsl_h[par][:, :],
                AX.bypass, AX.subtract),
              wait=[(c_se, None), (c_ve, xsplit_ve)], inc=[c_ve])
            wq_split = c_ve.val

            first_w = [(c_ve, wq_split), (c_ve, prev_scores_ve),
                       (c_se, xsplit_se)]
            for kk in range(NCH):
                swh = wsl_h[par][:, kk * 128:(kk + 1) * 128]
                swl = wsl_l[par][:, kk * 128:(kk + 1) * 128]
                mvh = xh[:, kk * T:(kk + 1) * T].rearrange(
                    "p (b s) -> p b s", s=10)[:, :, 0]
                mvl = xl[:, kk * T:(kk + 1) * T].rearrange(
                    "p (b s) -> p b s", s=10)[:, :, 0]
                E("tensor", lambda te, sw=swh, mv=mvh, kk=kk: te.matmul(
                    pb[7][:, 0:B], sw, mv, start=(kk == 0), stop=False),
                  wait=(first_w if kk == 0 else ()))
                E("tensor", lambda te, sw=swh, mv=mvl: te.matmul(
                    pb[7][:, 0:B], sw, mv, start=False, stop=False))
                E("tensor", lambda te, sw=swl, mv=mvh: te.matmul(
                    pb[7][:, 0:B], sw, mv, start=False, stop=False))
            E("tensor", lambda te, h=h: te.matmul(
                pb[7][:, 0:B], biasr[0:1, h * 128:(h + 1) * 128],
                onesr[0:1, 0:B], start=False, stop=True),
              wait=([(c_ld, consts_ld)] if h == 0 else ()), inc=[c_te])
            qmm = c_te.val
            E("scalar", lambda se, h=h: se.activation(
                qc[:, h * B:(h + 1) * B], pb[7][:, 0:B], ACT.Copy),
              wait=[(c_te, qmm)], inc=[c_se])
            q_ev = c_se.val

            vkc, vkv = ld(wsl_f[par][:, :]
                          .rearrange("p (c m) -> p c m", m=128),
                          wkt[h].rearrange("(c p) m -> c p m", p=128)
                          .transpose([1, 0, 2]),
                          wait=[(c_ve, wq_split)], stream=f"ws{par}")
            E("scalar", lambda se, par=par: se.activation(
                wsl_h[par][:, :], wsl_f[par][:, :], ACT.Copy),
              wait=[(vkc, vkv), (c_te, qmm)], inc=[c_se])
            E("vector", lambda ve, par=par: ve.scalar_tensor_tensor(
                wsl_l[par][:, :], wsl_f[par][:, :], 1.0, wsl_h[par][:, :],
                AX.bypass, AX.subtract),
              wait=[(c_se, None)], inc=[c_ve])
            wk_split = c_ve.val

            colsum_step(4)
            first_w = [(c_ve, wk_split), (c_se, max(kev)),
                       (c_ve, scev[par])]
            for kk in range(NCH):
                swh = wsl_h[par][:, kk * 128:(kk + 1) * 128]
                swl = wsl_l[par][:, kk * 128:(kk + 1) * 128]
                for gi, (g0, gn) in enumerate(TG[0:2]):
                    mh = xh[:, kk * T + g0: kk * T + g0 + gn]
                    mlo = xl[:, kk * T + g0: kk * T + g0 + gn]
                    E("tensor", lambda te, sw=swh, m=mh, gi=gi, gn=gn,
                        kk=kk: te.matmul(pb[4 + gi][:, 0:gn], sw, m,
                                         start=(kk == 0), stop=False),
                      wait=(first_w if (kk == 0 and gi == 0) else ()))
                    E("tensor", lambda te, sw=swh, m=mlo, gi=gi, gn=gn:
                      te.matmul(pb[4 + gi][:, 0:gn], sw, m, start=False,
                                stop=False))
                    E("tensor", lambda te, sw=swl, m=mh, gi=gi, gn=gn:
                      te.matmul(pb[4 + gi][:, 0:gn], sw, m, start=False,
                                stop=False))
                E("tensor", lambda te, kk=kk, par=par: te.matmul(
                    pb[6][:, 0:16],
                    wsl_f[par][:, kk * 128:(kk + 1) * 128],
                    xtailf[:, kk * 16:(kk + 1) * 16],
                    start=(kk == 0), stop=False),
                  wait=([(c_se, xsplit_se)] if kk == 0 else ()))
            for gi, (g0, gn) in enumerate(TG):
                E("tensor", lambda te, h=h, g0=g0, gn=gn, gi=gi: te.matmul(
                    pb[4 + gi][:, 0:gn], biasr[32:33, h * 128:(h + 1) * 128],
                    onesr[32:33, g0:g0 + gn], start=False, stop=True),
                  inc=([c_te] if gi == 2 else []))
            kmm = c_te.val
            wsl_free[par] = kmm
            for gi, (g0, gn) in enumerate(TG):
                E("scalar", lambda se, g0=g0, gn=gn, gi=gi, par=par:
                  se.activation(kf[par][:, g0:g0 + gn], pb[4 + gi][:, 0:gn],
                                ACT.Copy, scale=ISQD),
                  wait=[(c_te, kmm)], inc=[c_se])
            k_ev = c_se.val
            kev[par] = k_ev

            for wdx in range(NW):
                k0 = wdx * WSTR
                kn = min(128, T - k0)
                rn = min(RPW, B - RPW * wdx)
                w = ([(c_se, k_ev), (c_se, q_ev)] if wdx == 0
                     else [(c_ve, None)])
                E("tensor", lambda te, h=h, wdx=wdx, k0=k0, kn=kn, rn=rn,
                    par=par: te.matmul(
                      pb[7][0:rn, 0:kn],
                      qc[:, h * B + RPW * wdx: h * B + RPW * wdx + rn],
                      kf[par][:, k0:k0 + kn], start=True, stop=True),
                  wait=w, inc=[c_te])
                smm = c_te.val
                wb2, wq2 = wdx // 4, wdx % 4
                E("vector", lambda ve, h=h, kn=kn, rn=rn, wb2=wb2, wq2=wq2:
                  ve.tensor_copy(
                      swide[32 * wq2:32 * wq2 + rn,
                            wb2 * 2048 + h * 128: wb2 * 2048 + h * 128 + kn],
                      pb[7][0:rn, 0:kn]),
                  wait=[(c_te, smm), (c_ve, swide_zeroed)], inc=[c_ve])
            scev[par] = c_ve.val
            prev_scores_ve = c_ve.val
            for wi_ in range(8 * h, 8 * h + 8):
                emit_w1(wi_)
        colsum_step(256)
        scores_done = c_ve.val

        wabs_ge = c_ve.val
        E("tensor", lambda te: te.matmul(
            pb[4][0:1, 0:2], onesc[:, 0:1], wabs[:, 0:2],
            start=True, stop=True),
          wait=[(c_ve, wabs_ge), (c_te, None)], inc=[c_te])
        E("vector", lambda ve: ve.tensor_copy(wsc[0:1, 0:2],
                                              pb[4][0:1, 0:2]),
          wait=[(c_te, None)], inc=[c_ve])
        E("vector", lambda ve: ve.tensor_scalar(
            wsc[0:1, 0:2], wsc[0:1, 0:2], WMEAN, 1e-5, AX.mult, AX.max),
          wait=[(c_ve, None)], inc=[c_ve])
        E("vector", lambda ve: ve.tensor_scalar(
            wsc[0:1, 2:4], wsc[0:1, 0:2], 0.5, None, AX.mult),
          wait=[(c_ve, None)], inc=[c_ve])
        E("vector", lambda ve: ve.tensor_copy(wsc[0:1, 4:6], wsc[0:1, 0:2]),
          wait=[(c_ve, None)], inc=[c_ve])
        wsc_ve = c_ve.val
        dstore("d_wsc", wsc[0:1, 0:6], wait=[(c_ve, wsc_ve)])
        E("tensor", lambda te: te.matmul(
            pb[4][0:128, 0:4], onesr[0:1, 0:128], wsc[0:1, 2:6],
            start=True, stop=True),
          wait=[(c_ve, wsc_ve), (c_te, None)], inc=[c_te])
        E("vector", lambda ve: ve.tensor_copy(thrp[:, 0:4],
                                              pb[4][0:128, 0:4]),
          wait=[(c_te, None)], inc=[c_ve])
        E("vector", lambda ve: ve.tensor_scalar(
            thrp[:, 4:6], thrp[:, 0:2], -1.0, None, AX.mult),
          wait=[(c_ve, None)], inc=[c_ve])
        thr_ve = c_ve.val
        wscb = thrp[:, 2:4]

        colsum_te = cs["incs"][-1]
        for ns in range(4):
            E("vector", lambda ve, ns=ns: ve.tensor_copy(
                csum[0:1, ns * 512:(ns + 1) * 512], pb[ns][0:1, 0:512]),
              wait=[(c_te, colsum_te)], inc=[c_ve])
        csum_ve = c_ve.val
        dstore("d_colsum", csum[0:1, :], wait=[(c_ve, csum_ve)])
        sw_dump = dstore("d_swide", swide[:, :], wait=[(c_ve, scores_done)],
                         stream="dsw")
        dstore("d_q", qc[:, :], wait=[(c_ve, scores_done)])
        if phases <= 1:
            _finish_stub(E, AR, c_ve, c_st, y_out, biasr,
                         [(c_te, None), (c_ve, None), (c_se, None)])
            plan.run(ent(nc.Block()))
            return nc, dram, dbg

        dbar()
        for buf in (xh, xl, xload[0], xload[1], wsl_f[0], wsl_f[1],
                    wsl_h[0], wsl_h[1], wsl_l[0], wsl_l[1], kf[0], kf[1],
                    cx[0], cx[1]):
            AR.free(buf)
        wintl.append(AR.alloc(1024, BF16))
        t1w.append(AR.alloc(1024, F32))

        NWB3 = 3
        ep_m = AR.alloc(NWB3 * NH, F32)
        ep_em = AR.alloc(NWB3 * NH, F32)
        ep_den = AR.alloc(NWB3 * NH, F32)
        ep_dinv = AR.alloc(NWB3 * NH, F32)
        ep_a = AR.alloc(NWB3 * NH, F32)
        ep_cse = AR.alloc(NWB3 * NH, F32)
        ep_t = AR.alloc(NH * 128, F32)
        ekml = [AR.alloc(NH * RPW, F32) for _ in range(NW)]
        stg = AR.alloc(128, F32)
        a16t = AR.alloc(NWB3 * 128, F32)
        cset = AR.alloc(NWB3 * 128, F32)

        def view4(ap_t):
            return ap_t[:, :].rearrange("p (bh k) -> p bh k", k=128)

        def m_bc(ap_tile):
            base = ap_tile[:, 0:NWB3 * NH]
            return AP(base.tensor, base.offset,
                      [list(base.ap[0]), list(base.ap[1]), [0, 128]])

        def msk_bc(ap_tile):
            base = ap_tile[:, 0:NH * 128]
            return AP(base.tensor, base.offset,
                      [list(base.ap[0]), [0, NWB3],
                       list(base.ap[1])])

        E("vector", lambda ve: ve.tensor_tensor(
            swide[:, :].rearrange("p (b hk) -> p b hk", b=NWB3),
            swide[:, :].rearrange("p (b hk) -> p b hk", b=NWB3),
            msk_bc(maskb_s), AX.add),
          wait=[(c_ve, scores_done), (c_ld, consts_ld)]
          + ([sw_dump] if sw_dump else []), inc=[c_ve])
        E("vector", lambda ve: ve.tensor_reduce(
            ep_m[:, :], view4(swide), AXL.X, AX.max),
          wait=[(c_ve, None)], inc=[c_ve])
        E("vector", lambda ve: ve.tensor_scalar(
            ep_m[:, :], ep_m[:, :], 0.0, None, AX.max),
          wait=[(c_ve, None)], inc=[c_ve])
        m_ve = c_ve.val
        E("scalar", lambda se: se.activation(
            ep_em[:, :], ep_m[:, :], ACT.Exp, scale=-1.0),
          wait=[(c_ve, m_ve)], inc=[c_se])
        em_se = c_se.val
        E("vector", lambda ve: ve.tensor_tensor(
            view4(swide), view4(swide), m_bc(ep_m), AX.subtract),
          wait=[(c_ve, m_ve)], inc=[c_ve])
        E("scalar", lambda se: se.activation(
            swide[:, :], swide[:, :], ACT.Exp),
          wait=[(c_ve, None)], inc=[c_se])
        e_se = c_se.val
        E("vector", lambda ve: ve.tensor_reduce(
            ep_den[:, :], view4(swide), AXL.X, AX.add),
          wait=[(c_se, e_se)], inc=[c_ve])
        E("vector", lambda ve: ve.scalar_tensor_tensor(
            ep_den[:, :], ep_em[:, :], float(S - 10), ep_den[:, :],
            AX.mult, AX.add),
          wait=[(c_ve, None), (c_se, em_se)], inc=[c_ve])
        E("vector", lambda ve: ve.reciprocal(ep_dinv[:, :], ep_den[:, :]),
          wait=[(c_ve, None)], inc=[c_ve])
        dinv_ve = c_ve.val
        E("vector", lambda ve: ve.tensor_tensor(
            ep_a[:, :], ep_em[:, :], ep_dinv[:, :], AX.mult),
          wait=[(c_ve, dinv_ve)], inc=[c_ve])
        a_ve = c_ve.val
        for wb_ in range(NWB3):
            def ebc(tile, wb_=wb_):
                base = tile[:, wb_ * NH:(wb_ + 1) * NH]
                return AP(base.tensor, base.offset,
                          [list(base.ap[0]), list(base.ap[1]), [0, 128]])
            emb, dvb = ebc(ep_em), ebc(ep_dinv)
            E("vector", lambda ve, emb=emb: ve.tensor_tensor(
                ep_t[:, :].rearrange("p (h k) -> p h k", k=128),
                emb,
                mask01_s[:, :].rearrange("p (h k) -> p h k", k=128),
                AX.mult),
              wait=[(c_se, e_se), (c_ve, None)], inc=[c_ve])
            E("vector", lambda ve, wb_=wb_: ve.tensor_tensor(
                swide[:, wb_ * 2048:(wb_ + 1) * 2048],
                swide[:, wb_ * 2048:(wb_ + 1) * 2048],
                ep_t[:, :], AX.subtract),
              wait=[(c_ve, None)], inc=[c_ve])
            E("vector", lambda ve, wb_=wb_, dvb=dvb: ve.tensor_tensor(
                swide[:, wb_ * 2048:(wb_ + 1) * 2048]
                .rearrange("p (h k) -> p h k", k=128),
                swide[:, wb_ * 2048:(wb_ + 1) * 2048]
                .rearrange("p (h k) -> p h k", k=128),
                dvb, AX.mult),
              wait=[(c_ve, None)], inc=[c_ve])
        epp_ve = c_ve.val
        E("vector", lambda ve: ve.tensor_reduce(
            ep_cse[:, :], view4(swide), AXL.X, AX.add),
          wait=[(c_ve, epp_ve)], inc=[c_ve])
        cse_ve = c_ve.val
        if debug and "d_epi" in dbg:
            for j, tile in enumerate((ep_a, ep_m, ep_em, ep_den, ep_dinv,
                                      ep_cse)):
                st(dbg["d_epi"][:, j * 48:(j + 1) * 48],
                   tile[:, 0:NWB3 * NH], wait=[(c_ve, cse_ve)])
            dbar()

        for wb in range(NWB3):
            E("tensor", lambda te, wb=wb: te.transpose(
                pb[0][0:NH, 0:128], ep_a[:, wb * NH:(wb + 1) * NH],
                ident_f[:, :]),
              wait=[(c_ve, a_ve), (c_ve, None)], inc=[c_te])
            E("vector", lambda ve, wb=wb: ve.tensor_copy(
                a16t[0:NH, wb * 128:(wb + 1) * 128], pb[0][0:NH, 0:128]),
              wait=[(c_te, None)], inc=[c_ve])
            E("tensor", lambda te, wb=wb: te.transpose(
                pb[1][0:NH, 0:128], ep_cse[:, wb * NH:(wb + 1) * NH],
                ident_f[:, :]),
              wait=[(c_ve, cse_ve), (c_ve, None)], inc=[c_te])
            E("vector", lambda ve, wb=wb: ve.tensor_copy(
                cset[0:NH, wb * 128:(wb + 1) * 128], pb[1][0:NH, 0:128]),
              wait=[(c_te, None)], inc=[c_ve])
        a16_ve = c_ve.val

        ek_ev = {b_: a16_ve for b_ in range(4)}
        for h in range(NH):
            for wdx in range(NW):
                wb, wq = wdx // 4, wdx % 4
                bk = (h * NW + wdx) % 4
                if wq == 3:
                    E("scalar", lambda se, wb=wb, h=h: se.activation(
                        stg[0:RPW, :],
                        swide[96:96 + RPW,
                              wb * 2048 + h * 128:
                              wb * 2048 + (h + 1) * 128],
                        ACT.Copy),
                      wait=[(c_ve, epp_ve), (c_te, None)], inc=[c_se])
                    E("tensor", lambda te, bk=bk: te.transpose(
                        pb[bk][0:128, 0:RPW], stg[0:RPW, :],
                        ident_f[0:RPW, 0:RPW]),
                      wait=[(c_se, None), (c_ve, ek_ev[bk])], inc=[c_te])
                else:
                    E("tensor", lambda te, wb=wb, wq=wq, h=h, bk=bk:
                      te.transpose(
                          pb[bk][0:128, 0:RPW],
                          swide[32 * wq:32 * wq + RPW,
                                wb * 2048 + h * 128:
                                wb * 2048 + (h + 1) * 128],
                          ident_f[32 * wq:32 * wq + RPW,
                                  32 * wq:32 * wq + RPW]),
                      wait=[(c_ve, max(epp_ve, ek_ev[bk]))], inc=[c_te])
                tv = c_te.val
                E("vector", lambda ve, h=h, wdx=wdx, bk=bk: ve.tensor_copy(
                    ekml[wdx][:, h * RPW:h * RPW + RPW],
                    pb[bk][0:128, 0:RPW]),
                  wait=[(c_te, tv)], inc=[c_ve])
                ek_ev[bk] = c_ve.val
            for wi_ in range(8 * h, 8 * h + 8):
                emit_w2(wi_, thr_ve)
        wpark_ctr = stream_ctr("wp0")
        wpark_st = wpark_ctr.val
        wpark_ctr2 = stream_ctr("wp1")
        wpark_st2 = wpark_ctr2.val
        ekm_ve = c_ve.val
        for buf in (ep_m, ep_em, ep_den, ep_dinv, ep_t, stg):
            AR.free(buf)
        if debug and "d_ekm" in dbg:
            for wdx in range(NW):
                st(dbg["d_ekm"][:, wdx * NH * RPW:(wdx + 1) * NH * RPW],
                   ekml[wdx][:, :], wait=[(c_ve, ekm_ve)])
            dbar()
        if phases <= 2:
            _finish_stub(E, AR, c_ve, c_st, y_out, biasr,
                         [(c_te, None), (c_ve, None), (c_se, None)])
            plan.run(ent(nc.Block()))
            return nc, dram, dbg

        dbar()
        for buf in [swide, qc, ep_a, ep_cse, maskb_s, mask01_s,
                    wblk[0], wblk[1]] + t1w + wintl:
            AR.free(buf)
        NWB = 5
        shrow = AR.alloc(128, F32)
        csumT = AR.alloc(16, F32)
        svbT = AR.alloc(32, F32)

        E("vector", lambda ve: ve.memset(shrow[:, 0:128], 0.0),
          wait=[(c_ve, csum_ve)], inc=[c_ve])
        shz = c_ve.val
        _, rowv = ld(shrow[0:16, 0:128], AP(csum.tensor, csum.offset,
                                            [[csum.ap[0][0], 1],
                                             [128, 16], [1, 128]]),
                     q="gpsimd", wait=[(c_ve, shz)], stream="rowT")
        E("tensor", lambda te: te.transpose(
            pb[6][0:128, 0:128], shrow[0:128, 0:128], ident_f[:, :]),
          wait=[(streams["rowT"], rowv)], inc=[c_te])
        E("vector", lambda ve: ve.tensor_copy(csumT[:, 0:16],
                                              pb[6][0:128, 0:16]),
          wait=[(c_te, None)], inc=[c_ve])
        csumT_ve = c_ve.val
        AR.free(csum)
        AR.free(onesr)
        wvb = [AR.alloc(D, F32) for _ in range(2)]
        wob = wvb
        xwin = [AR.alloc(D, F32)]
        svrow = AR.alloc(D, F32)
        ctxc = AR.alloc(NH * B, F32)
        gbl = [AR.alloc(4 * NH * RPW, F32) for _ in range(NCH)]

        E("vector", lambda ve: ve.memset(ctxc[:, :], 0.0), inc=[c_ve])
        ctxz_ve = c_ve.val
        ctx_bank = {2: (c_ve, 0), 3: (c_ve, 0)}
        sv_ve = 0
        wvte = {0: 0, 1: 0}
        for bi, (w0, nwn) in enumerate(((0, 4), (4, 4), (8, 2))):
            for wi in range(nwn):
                wdx = w0 + wi
                bank = wdx % 2
                k0 = wdx * WSTR
                kn = min(128, T - k0)
                vc, vv = ld(xwin[0][0:kn, :], xtm[k0:k0 + kn, :],
                            q="gpsimd", wait=[(c_te, None)], stream="xw")
                for cc in range(NCH):
                    E("tensor", lambda te, cc=cc, wdx=wdx, bank=bank:
                      te.matmul(
                          pb[bank][:, 0:NH * RPW],
                          xwin[0][:, cc * 128:(cc + 1) * 128],
                          ekml[wdx][:, :],
                          start=True, stop=True),
                      wait=([(vc, vv), (c_ve, ekm_ve), (c_ve, None)]
                            if cc == 0 else [(c_ve, None)]),
                      inc=[c_te])
                    E("vector", lambda ve, cc=cc, wi=wi, bank=bank,
                        nwn=nwn: ve.tensor_copy(
                          apx(gbl[cc], wi * RPW,
                              [[nwn * RPW, NH], [1, RPW]]),
                          pb[bank][:, 0:NH * RPW]),
                      wait=[(c_te, None)], inc=[c_ve])
            gb_ve = c_ve.val
            cbw = [ctx_bank[2], ctx_bank[3]]
            for cc in range(NCH):
                par = cc % 2
                vc, vv = ld(wvb[par][:, :],
                            wv_d[cc * 128:(cc + 1) * 128, :],
                            wait=[(c_te, wvte[par])], stream=f"wv{par}")
                if bi == 0:
                    for ns in range(4):
                        E("tensor", lambda te, cc=cc, ns=ns, par=par:
                          te.matmul(
                              pb[4 + ns][0:1, 0:512], csumT[:, cc:cc + 1],
                              wvb[par][:, ns * 512:(ns + 1) * 512],
                              start=(cc == 0), stop=False),
                          wait=([(c_ve, csumT_ve), (vc, vv)]
                                if ns == 0 else ()))
                for h in range(NH):
                    hb = 2 + h // 8
                    hc = (h % 8) * nwn * RPW
                    E("tensor", lambda te, cc=cc, h=h, hb=hb, hc=hc,
                        par=par, nwn=nwn: te.matmul(
                          pb[hb][:, hc:hc + nwn * RPW],
                          wvb[par][:, h * 128:(h + 1) * 128],
                          gbl[cc][:, h * nwn * RPW:(h + 1) * nwn * RPW],
                          start=(cc == 0 and h % 8 == 0),
                          stop=(cc == 15 and h % 8 == 7)),
                      wait=((cbw + [(vc, vv), (c_ve, gb_ve)])
                            if (cc == 0 and h == 0) else
                            ([(vc, vv)] if h == 0 else ())),
                      inc=([c_te] if h == NH - 1 else []))
                wvte[par] = c_te.val
            cmm = c_te.val
            if bi == 0:
                for ns in range(4):
                    E("tensor", lambda te, ns=ns: te.matmul(
                        pb[4 + ns][0:1, 0:512], sconst[64:65, 0:1],
                        biasr[64:65, ns * 512:(ns + 1) * 512],
                        start=False, stop=True),
                      inc=([c_te] if ns == 3 else []))
                svmm = c_te.val
                for ns in range(4):
                    E("vector", lambda ve, ns=ns: ve.tensor_copy(
                        svrow[0:1, ns * 512:(ns + 1) * 512],
                        pb[4 + ns][0:1, 0:512]),
                      wait=[(c_te, svmm)], inc=[c_ve])
                sv_ve = c_ve.val
                dstore("d_sv", svrow[0:1, :], wait=[(c_ve, sv_ve)])
            ecnt = min(B - RPW * w0, nwn * RPW)
            pbv2, pbv3 = pb[2].ap(), pb[3].ap()
            for half, pbv in ((0, pbv2), (1, pbv3)):
                E("vector", lambda ve, w0=w0, half=half, ecnt=ecnt,
                    pbv=pbv, nwn=nwn: ve.tensor_copy(
                      apx(ctxc, (half * 8) * B + RPW * w0,
                          [[B, 8], [1, ecnt]]),
                      AP(pbv.tensor, pbv.offset,
                         [list(pbv.ap[0]), [nwn * RPW, 8], [1, ecnt]])),
                  wait=[(c_te, cmm), (c_ve, ctxz_ve)], inc=[c_ve])
                ctx_bank[2 + half] = (c_ve, c_ve.val)
            if bi == 0:
                dstore("d_ctx0", ctxc[:, :], wait=[(c_ve, c_ve.val)])
        ctx_ve = c_ve.val
        dstore("d_ctx", ctxc[:, :], wait=[(c_ve, ctx_ve)])

        _, rowv = ld(shrow[0:16, 0:128], AP(svrow.tensor, svrow.offset,
                                            [[svrow.ap[0][0], 1],
                                             [128, 16], [1, 128]]),
                     q="gpsimd", wait=[(c_ve, sv_ve), (c_te, None)],
                     stream="rowT")
        E("tensor", lambda te: te.transpose(
            pb[6][0:128, 0:128], shrow[0:128, 0:128], ident_f[:, :]),
          wait=[(streams["rowT"], rowv), (c_te, None)], inc=[c_te])
        E("vector", lambda ve: ve.tensor_copy(
            apx(svbT, 0, [[2, 16]]), pb[6][0:128, 0:16]),
          wait=[(c_te, None)], inc=[c_ve])
        svT_ve = c_ve.val
        _, rowv = ld(shrow[0:16, 0:128],
                     AP(biasr.tensor, biasr.offset + 64 * biasr.ap[0][0],
                        [[biasr.ap[0][0], 1], [128, 16], [1, 128]]),
                     q="gpsimd", wait=[(c_ve, svT_ve)], stream="rowT")
        E("tensor", lambda te: te.transpose(
            pb[6][0:128, 0:128], shrow[0:128, 0:128], ident_f[:, :]),
          wait=[(streams["rowT"], rowv), (c_te, None)], inc=[c_te])
        E("vector", lambda ve: ve.tensor_copy(
            apx(svbT, 1, [[2, 16]]), pb[6][0:128, 0:16]),
          wait=[(c_te, None)], inc=[c_ve])
        bvT_ve = c_ve.val

        for buf in [xwin[0], svrow, shrow] + gbl:
            AR.free(buf)
        uprime = AR.alloc(D, F32)
        bandtm = AR.alloc(D, F32)
        aprime = AR.alloc(T, F32)
        ustg = AR.alloc(D, F32)

        E("vector", lambda ve: ve.memset(uprime[:, :], 0.0), inc=[c_ve])
        upz = c_ve.val
        wote = dict(wvte)
        for cc in range(NCH):
            par = cc % 2
            vc, vv = ld(wob[par][:, :], wo_d[cc * 128:(cc + 1) * 128, :],
                        wait=[(c_te, wote[par])],
                        stream=f"wo{par}")
            for ns in range(4):
                E("tensor", lambda te, cc=cc, ns=ns, par=par: te.matmul(
                    pb[ns][0:B, 0:512], ctxc[:, cc * B:(cc + 1) * B],
                    wob[par][:, ns * 512:(ns + 1) * 512],
                    start=(cc == 0), stop=(cc == 15)),
                  wait=([(vc, vv), (c_ve, ctx_ve), (c_ve, None)]
                        if ns == 0 else ()),
                  inc=([c_te] if ns == 3 else []))
            h = cc
            for ns in range(4):
                E("tensor", lambda te, h=h, ns=ns, par=par: te.matmul(
                      pb[4][0:2, 0:512], svbT[:, 2 * h:2 * h + 2],
                      wob[par][:, ns * 512:(ns + 1) * 512],
                      start=True, stop=True),
                  wait=([(vc, vv), (c_se, None), (c_ve, bvT_ve)]
                        if ns == 0 else [(c_se, None)]),
                  inc=[c_te])
                umm = c_te.val
                E("scalar", lambda se, ns=ns: se.activation(
                    ustg[0:2, ns * 512:(ns + 1) * 512],
                    pb[4][0:2, 0:512], ACT.Copy),
                  wait=[(c_te, umm)] +
                       ([(streams["upr"], None)]
                        if (ns == 0 and "upr" in streams) else []),
                  inc=[c_se])
            usev = c_se.val
            for ui in range(2):
                tgt = h if ui == 0 else 32 + h
                E("gpsimd", lambda e, tgt=tgt, ui=ui: e.dma_start(
                    out=uprime[tgt:tgt + 1, :], in_=ustg[ui:ui + 1, :]),
                  wait=[(c_se, usev), (c_ve, upz)],
                  inc=[stream_ctr("upr")], incn=16)
            wote[par] = c_te.val
        bmm = c_te.val
        for ns in range(4):
            E("vector", lambda ve, ns=ns: ve.tensor_copy(
                bandtm[0:B, ns * 512:(ns + 1) * 512], pb[ns][0:B, 0:512]),
              wait=[(c_te, bmm)], inc=[c_ve])
        bandtm_ve = c_ve.val
        E("scalar", lambda se: se.activation(
            uprime[64:65, :], biasr[96:97, :], ACT.Copy),
          wait=[(c_ld, consts_ld), (c_ve, upz)], inc=[c_se])
        up_se = c_se.val
        up_dma = stream_ctr("upr").val

        E("vector", lambda ve: ve.memset(aprime[:, :], 0.0),
          wait=[(c_ve, None)], inc=[c_ve])
        E("vector", lambda ve: ve.memset(aprime[0:16, :], 1.0 / S),
          wait=[(c_ve, None)], inc=[c_ve])
        E("vector", lambda ve: ve.memset(aprime[64:65, :], 1.0),
          inc=[c_ve])
        apz = c_ve.val
        for wdx in range(NW):
            wb, wq = wdx // 4, wdx % 4
            cnt = min(RPW, (T - WSTR * wdx + 9) // 10)
            for rbase, srct in ((0, a16t), (32, cset)):
                E("vector", lambda ve, wdx=wdx, wb=wb, wq=wq, cnt=cnt,
                    rbase=rbase, srct=srct: ve.tensor_copy(
                      aprime[rbase:rbase + 16,
                             WSTR * wdx: WSTR * wdx + 10 * cnt]
                      .rearrange("p (b s) -> p b s", s=10)[:, :, 0],
                      srct[0:16, wb * 128 + 32 * wq:
                           wb * 128 + 32 * wq + cnt]),
                  wait=[(c_ve, apz), (c_ve, a16_ve)], inc=[c_ve])
        ap_ve = c_ve.val
        dstore("d_u", uprime[0:65, :], wait=[(c_se, up_se),
                                             (stream_ctr("upr"), None)])
        dstore("d_a", aprime[0:65, :], wait=[(c_ve, ap_ve)])

        for buf in [ustg, a16t, cset, csumT, svbT,
                    wvb[0], wvb[1], ctxc, biasr] + ekml:
            AR.free(buf)
        selr = AR.alloc(NSUB * 128, F32, tag="selr")
        zfw = [(c_te, None), (c_ve, None), (c_se, None)] + \
            [(cc, None) for cc in streams.values()]
        ld(selr[0:16, :], sel_d, q="sync", wait=list(zfw))
        sel_ld = c_ld.val
        zsub = [AR.alloc(D, F32, tag="zsub") for _ in range(2)]
        xsub = [AR.alloc(D, F32, tag="xsub") for _ in range(2)]
        bsel = [AR.alloc(D, F32, tag="bsel") for _ in range(2)]
        zb_ev = []
        bs_te = {0: 0, 1: 0}
        zs_ve = {0: 0, 1: 0}
        for si in range(NSUB):
            par = si % 2
            r0, rn = SUBS[si]
            i0 = -(-r0 // 10)
            i1 = -(-(r0 + rn) // 10)
            cnt = i1 - i0
            fw = list(zfw) if si < 2 else []
            vx, vxv = ld(xsub[par][0:rn, :], xtm[r0:r0 + rn, :],
                         q="gpsimd",
                         wait=fw + ([(c_ve, zs_ve[par])] if si >= 2
                                    else []),
                         stream=f"xz{par}")
            vb, vbv = ld(bsel[par][0:cnt, :], bandtm[i0:i0 + cnt, :],
                         q="gpsimd",
                         wait=fw + [(c_ve, bandtm_ve)]
                         + ([(c_te, bs_te[par])] if si >= 2 else []),
                         stream=f"bz{par}")
            for ns in range(4):
                bank = 5 + (si * 4 + ns) % 3
                wb_ = [(c_se, up_se), (c_ve, ap_ve),
                       (stream_ctr("upr"), up_dma)]
                if si * 4 + ns >= 3:
                    wb_.append((c_ve, zb_ev[-3]))
                E("tensor", lambda te, ns=ns, r0=r0, rn=rn, bank=bank:
                  te.matmul(
                      pb[bank][0:rn, 0:512],
                      aprime[0:65, r0:r0 + rn],
                      uprime[0:65, ns * 512:(ns + 1) * 512],
                      start=True, stop=False),
                  wait=wb_, inc=[c_te])
                E("tensor", lambda te, ns=ns, rn=rn, bank=bank, si=si,
                    cnt=cnt, par=par: te.matmul(
                      pb[bank][0:rn, 0:512],
                      selr[0:cnt, si * 128:si * 128 + rn],
                      bsel[par][0:cnt, ns * 512:(ns + 1) * 512],
                      start=False, stop=True),
                  wait=([(vb, vbv), (c_ld, sel_ld)] if ns == 0 else ()),
                  inc=[c_te])
                zmm = c_te.val
                E("vector", lambda ve, ns=ns, rn=rn, bank=bank, par=par:
                  ve.tensor_tensor(
                      zsub[par][0:rn, ns * 512:(ns + 1) * 512],
                      xsub[par][0:rn, ns * 512:(ns + 1) * 512],
                      pb[bank][0:rn, 0:512], AX.add),
                  wait=[(c_te, zmm), (vx, vxv), (c_se, None),
                        (stream_ctr(f"zp{par}"), None)], inc=[c_ve])
                zb_ev.append(c_ve.val)
            bs_te[par] = c_te.val
            zs_ve[par] = c_ve.val
            st(z_park[si], zsub[par][:, :],
               wait=[(c_ve, zs_ve[par]),
                     (stream_ctr(f"zp{par}"), None)],
               stream=f"zp{par}")
            if debug and "d_z" in dbg:
                st(dbg["d_z"][si], zsub[par][:, :],
                   wait=[(c_ve, zs_ve[par])])
                dbar()

        if phases <= 4:
            _finish_stub(E, AR, c_ve, c_st, y_out, zsub[0],
                         [(c_te, None), (c_ve, None), (c_se, None)])
            plan.run(ent(nc.Block()))
            return nc, dram, dbg
        dbar()
        for buf in (uprime, aprime, bandtm,
                    zsub[0], zsub[1], xsub[0], xsub[1], bsel[0], bsel[1],
                    selr):
            AR.free(buf)

        xqfm = AR.alloc(NCH * T, BF16, tag="xqfm")
        s1t = AR.alloc(16, F32, tag="s1t")
        o2t = AR.alloc(16, F32, tag="o2t")
        zst = AR.alloc(16, F32, tag="zst")
        sq = AR.alloc(D, F32, tag="sq")
        y1s = AR.alloc(D, F32, tag="y1s")
        zld = [AR.alloc(D, F32, tag="zld") for _ in range(2)]
        pfm_h = AR.alloc(D, BF16, tag="pfm_h")
        pfm_l = AR.alloc(D, BF16, tag="pfm_l")
        g1a = AR.alloc(D, F32, tag="g1a")
        g1b = AR.alloc(D, F32, tag="g1b")

        bankst = {b: (c_ve, 0) for b in range(8)}
        bank_i = [0]

        def bank():
            b = bank_i[0] % 8
            bank_i[0] += 1
            return b, [bankst[b]]

        def bank_done(b, cv=None):
            bankst[b] = (c_ve, c_ve.val) if cv is None else cv

        def ln_stats(zin, rn, prev_w):
            E("vector", lambda ve: ve.tensor_reduce(
                zst[0:rn, 2:3], zin, AXL.X, AX.add),
              wait=prev_w, inc=[c_ve])
            E("scalar", lambda se: se.activation(sq[0:rn, :], zin,
                                                 ACT.Square),
              wait=prev_w, inc=[c_se])
            E("vector", lambda ve: ve.tensor_reduce(
                zst[0:rn, 3:4], sq[0:rn, :], AXL.X, AX.add),
              wait=[(c_se, None)], inc=[c_ve])
            E("vector", lambda ve: ve.tensor_scalar(
                zst[0:rn, 0:1], zst[0:rn, 2:3], 1.0 / D, None, AX.mult),
              wait=[(c_ve, None)], inc=[c_ve])
            E("vector", lambda ve: ve.tensor_scalar(
                zst[0:rn, 3:4], zst[0:rn, 3:4], 1.0 / D, None, AX.mult),
              wait=[(c_ve, None)], inc=[c_ve])
            E("vector", lambda ve: ve.tensor_tensor(
                zst[0:rn, 4:5], zst[0:rn, 0:1], zst[0:rn, 0:1], AX.mult),
              wait=[(c_ve, None)], inc=[c_ve])
            E("vector", lambda ve: ve.scalar_tensor_tensor(
                zst[0:rn, 4:5], zst[0:rn, 3:4], 1.0, zst[0:rn, 4:5],
                AX.bypass, AX.subtract),
              wait=[(c_ve, None)], inc=[c_ve])
            E("vector", lambda ve: ve.tensor_scalar(
                zst[0:rn, 4:5], zst[0:rn, 4:5], 1e-5, None, AX.add),
              wait=[(c_ve, None)], inc=[c_ve])
            E("scalar", lambda se: se.activation(
                zst[0:rn, 5:6], zst[0:rn, 4:5], ACT.Sqrt),
              wait=[(c_ve, None)], inc=[c_se])
            E("vector", lambda ve: ve.reciprocal(
                zst[0:rn, 6:7], zst[0:rn, 5:6]),
              wait=[(c_se, None)], inc=[c_ve])
            E("vector", lambda ve: ve.tensor_tensor(
                zst[0:rn, 6:7], zst[0:rn, 4:5], zst[0:rn, 6:7], AX.mult),
              wait=[(c_ve, None)], inc=[c_ve])
            E("vector", lambda ve: ve.tensor_tensor(
                zst[0:rn, 6:7], zst[0:rn, 6:7], zst[0:rn, 5:6], AX.add),
              wait=[(c_ve, None)], inc=[c_ve])
            E("vector", lambda ve: ve.tensor_scalar(
                zst[0:rn, 6:7], zst[0:rn, 6:7], 0.5, None, AX.mult),
              wait=[(c_ve, None)], inc=[c_ve])
            E("vector", lambda ve: ve.reciprocal(
                zst[0:rn, 1:2], zst[0:rn, 6:7]),
              wait=[(c_ve, None)], inc=[c_ve])
            return c_ve.val

        def recipn(dst, srcv, tmp, rn, w=()):
            E("vector", lambda ve: ve.reciprocal(dst, srcv),
              wait=list(w), inc=[c_ve])
            E("vector", lambda ve: ve.tensor_tensor(
                tmp, srcv, dst, AX.mult), wait=[(c_ve, None)], inc=[c_ve])
            E("vector", lambda ve: ve.tensor_scalar(
                tmp, tmp, -1.0, 2.0, AX.mult, AX.add),
              wait=[(c_ve, None)], inc=[c_ve])
            E("vector", lambda ve: ve.tensor_tensor(
                dst, dst, tmp, AX.mult), wait=[(c_ve, None)], inc=[c_ve])
            return c_ve.val

        def ladder(u_ap, w_ap, t_ap, q_ap, wait0):
            E("vector", lambda ve: ve.tensor_scalar(
                t_ap, u_ap, 128.0, -128.0, AX.is_ge, AX.mult),
              wait=list(wait0), inc=[c_ve])
            E("vector", lambda ve: ve.tensor_tensor(
                w_ap, u_ap, t_ap, AX.add),
              wait=[(c_ve, None)], inc=[c_ve])
            for k in (64.0, 32.0, 16.0, 8.0, 4.0, 2.0, 1.0):
                E("vector", lambda ve, k=k: ve.tensor_scalar(
                    t_ap, w_ap, k, -k, AX.is_ge, AX.mult),
                  wait=[(c_ve, None)], inc=[c_ve])
                E("vector", lambda ve: ve.tensor_tensor(
                    w_ap, w_ap, t_ap, AX.add),
                  wait=[(c_ve, None)], inc=[c_ve])
            E("vector", lambda ve: ve.tensor_tensor(
                t_ap, u_ap, w_ap, AX.subtract),
              wait=[(c_ve, None)], inc=[c_ve])
            E("vector", lambda ve: ve.tensor_scalar(
                q_ap, t_ap, 128.0, None, AX.subtract),
              wait=[(c_ve, None)], inc=[c_ve])
            return c_ve.val

        xq_evs = []
        for si in range(NSUB):
            par = si % 2
            r0, rn = SUBS[si]
            vz, vzv = ld(zld[par][:, :], z_park[si], q="gpsimd",
                         wait=[(stream_ctr(f"zp{par}"), 16 * (si // 2 + 1)),
                               (c_ve, None)],
                         stream=f"zl{par}")
            zin = zld[par][0:rn, :]
            prev_w = [(vz, vzv), (c_ve, None), (c_se, None)]
            stats_ve = ln_stats(zin, rn, prev_w)
            E("vector", lambda ve, zin=zin, rn=rn: ve.tensor_scalar(
                y1s[0:rn, :], zin, zst[0:rn, 0:1], zst[0:rn, 1:2],
                AX.subtract, AX.mult),
              wait=[(c_ve, stats_ve), (stream_ctr("y1p"), None),
                    (c_st, None)], inc=[c_ve])
            y1_ve = c_ve.val
            st(y1_park[si], y1s[:, :], wait=[(c_ve, y1_ve)],
               stream="y1p")
            if debug and "d_y1" in dbg:
                st(dbg["d_y1"][si], y1s[:, :], wait=[(c_ve, y1_ve)])
                dbar()
            for cc in range(NCH):
                b, bw = bank()
                E("tensor", lambda te, cc=cc, b=b: te.transpose(
                    pb[b][0:128, 0:128],
                    y1s[0:128, cc * 128:(cc + 1) * 128], ident_f[:, :]),
                  wait=bw + [(c_ve, y1_ve)], inc=[c_te])
                tv = c_te.val
                E("scalar", lambda se, cc=cc, b=b: se.activation(
                    pfm_h[:, cc * 128:(cc + 1) * 128],
                    pb[b][0:128, 0:128], ACT.Copy),
                  wait=[(c_te, tv)], inc=[c_se])
                E("vector", lambda ve, cc=cc, b=b:
                  ve.scalar_tensor_tensor(
                      pfm_l[:, cc * 128:(cc + 1) * 128],
                      pb[b][0:128, 0:128], 1.0,
                      pfm_h[:, cc * 128:(cc + 1) * 128],
                      AX.bypass, AX.subtract),
                  wait=[(c_se, None)], inc=[c_ve])
                bank_done(b)
            pfm_ve = c_ve.val
            ga_ev = []
            for g in range(4):
                b, bw = bank()
                n_mm = 0
                for j in range(4):
                    for tsrc in (pfm_h, pfm_l):
                        E("tensor", lambda te, g=g, j=j, tsrc=tsrc, b=b,
                            rn=rn, n_mm=n_mm: te.matmul(
                              pb[b][:, 0:512],
                              tsrc[:, (4 * g + j) * 128:
                                   (4 * g + j + 1) * 128],
                              hh4_b[:, j * 512:(j + 1) * 512],
                              start=(n_mm == 0), stop=(n_mm == 7)),
                          wait=(bw + [(c_ve, pfm_ve), (c_se, None)]
                                if n_mm == 0 else ()),
                          inc=([c_te] if n_mm == 7 else []))
                        n_mm += 1
                mmv = c_te.val
                E("scalar", lambda se, g=g, b=b, rn=rn: se.activation(
                    g1a[0:rn, g * 512:(g + 1) * 512], pb[b][0:rn, 0:512],
                    ACT.Copy),
                  wait=[(c_te, mmv)], inc=[c_se])
                bank_done(b, (c_se, c_se.val))
            E("vector", lambda ve, rn=rn: ve.tensor_tensor(
                g1b[0:rn, 0:512], g1a[0:rn, 0:512], g1a[0:rn, 512:1024],
                AX.add), wait=[(c_ve, None), (c_se, None)], inc=[c_ve])
            E("vector", lambda ve, rn=rn: ve.tensor_tensor(
                g1b[0:rn, 512:1024], g1a[0:rn, 0:512], g1a[0:rn, 512:1024],
                AX.subtract), inc=[c_ve])
            E("vector", lambda ve, rn=rn: ve.tensor_tensor(
                g1b[0:rn, 1024:1536], g1a[0:rn, 1024:1536],
                g1a[0:rn, 1536:2048], AX.add), inc=[c_ve])
            E("vector", lambda ve, rn=rn: ve.tensor_tensor(
                g1b[0:rn, 1536:2048], g1a[0:rn, 1024:1536],
                g1a[0:rn, 1536:2048], AX.subtract), inc=[c_ve])
            bst1 = c_ve.val
            E("vector", lambda ve, rn=rn: ve.tensor_tensor(
                g1a[0:rn, 0:512], g1b[0:rn, 0:512], g1b[0:rn, 1024:1536],
                AX.add), wait=[(c_ve, bst1)], inc=[c_ve])
            E("vector", lambda ve, rn=rn: ve.tensor_tensor(
                g1a[0:rn, 512:1024], g1b[0:rn, 512:1024],
                g1b[0:rn, 1536:2048], AX.add), inc=[c_ve])
            E("vector", lambda ve, rn=rn: ve.tensor_tensor(
                g1a[0:rn, 1024:1536], g1b[0:rn, 0:512],
                g1b[0:rn, 1024:1536], AX.subtract), inc=[c_ve])
            E("vector", lambda ve, rn=rn: ve.tensor_tensor(
                g1a[0:rn, 1536:2048], g1b[0:rn, 512:1024],
                g1b[0:rn, 1536:2048], AX.subtract), inc=[c_ve])
            g1_ve = c_ve.val
            E("vector", lambda ve, rn=rn: ve.tensor_reduce(
                zst[0:rn, 8:9], g1a[0:rn, :], AXL.X, AX.max,
                apply_absolute_value=True),
              wait=[(c_ve, g1_ve), (c_se, None)], inc=[c_ve])
            E("vector", lambda ve, rn=rn: ve.tensor_scalar(
                zst[0:rn, 8:9], zst[0:rn, 8:9], C1, 1e-5, AX.mult, AX.max),
              wait=[(c_ve, None)], inc=[c_ve])
            recipn(zst[0:rn, 10:11], zst[0:rn, 8:9], zst[0:rn, 11:12], rn,
                   w=[(c_ve, None)])
            E("vector", lambda ve, rn=rn, si=si: ve.tensor_scalar(
                s1t[0:rn, si:si + 1], zst[0:rn, 10:11], 127.0, None,
                AX.mult),
              wait=[(c_ve, None)], inc=[c_ve])
            E("vector", lambda ve, rn=rn, si=si: ve.tensor_scalar(
                zst[0:rn, 9:10], s1t[0:rn, si:si + 1], -C1, None, AX.mult),
              wait=[(c_ve, None)], inc=[c_ve])
            s1_ve = c_ve.val
            E("scalar", lambda se, rn=rn: se.activation(
                sq[0:rn, :], g1a[0:rn, :], ACT.Copy, bias=128.5,
                scale=zst[0:rn, 9:10]),
              wait=[(c_ve, s1_ve)], inc=[c_se])
            ladder(sq[0:rn, :], y1s[0:rn, :], zld[par][0:rn, :],
                   g1b[0:rn, :],
                   [(c_se, None), (stream_ctr("y1p"), None),
                    (stream_ctr(f"zl{par}"), None)])
            xqtm_ve = c_ve.val
            for cc in range(NCH):
                b, bw = bank()
                E("tensor", lambda te, cc=cc, b=b: te.transpose(
                    pb[b][0:128, 0:128],
                    g1b[0:128, cc * 128:(cc + 1) * 128], ident_f[:, :]),
                  wait=bw + [(c_ve, xqtm_ve)], inc=[c_te])
                tv = c_te.val
                E("scalar", lambda se, cc=cc, b=b, r0=r0, rn=rn:
                  se.activation(xqfm[:, cc * T + r0: cc * T + r0 + rn],
                                pb[b][0:128, 0:rn], ACT.Copy),
                  wait=[(c_te, tv)], inc=[c_se])
                bank_done(b, (c_se, c_se.val))
            xq_evs.append(c_ve.val)
        xq_ve = c_ve.val
        dstore("d_xq", xqfm[:, :].bitcast(F32)[:, 0:NCH * T // 2],
               wait=[(c_ve, xq_ve)])
        dstore("d_s1", s1t[:, 0:NSUB], wait=[(c_ve, xq_ve)])

        dbar()
        for buf in (y1s, zld[0], zld[1], pfm_h, pfm_l, g1a, g1b):
            AR.free(buf)


        QT = [(0, 256, (0, 1)), (256, 256, (2, 3)), (512, 256, (4, 5)),
              (768, 272, (6, 7, 8))]
        g2a = AR.alloc(DF, F32, tag="g2a")
        g2b = AR.alloc(DF, F32, tag="g2b")
        h2h = AR.alloc(NFCH * 272, BF16, tag="h2h")
        h2l = AR.alloc(NFCH * 272, BF16, tag="h2l")
        hqst = [AR.alloc(DF, BF16, tag="hqst")]
        wq2 = AR.alloc(DF, F32, tag="wq2")
        tq2 = AR.alloc(DF, F32, tag="tq2")
        wsl = [AR.alloc(NCH * 128, BF16, tag="wsl") for _ in range(2)]
        rfp = AR.alloc(272, F32, tag="rfp")
        pfp = AR.alloc(272, F32, tag="pfp")
        qst = AR.alloc(16, F32, tag="qst")
        h2_ve = {"v": 0}
        uste = {0: 0, 1: 0}
        for qi, (o4, n4, qsubs) in enumerate(QT):
            h2evs = []
            for fc in range(NFCH):
                par = fc % 2
                vc, vv = ld(wsl[par][:, :]
                            .rearrange("p (k f) -> p k f", f=128),
                            wup_int[:, :, fc * 128:(fc + 1) * 128]
                            .transpose([1, 0, 2]),
                            q="sync",
                            wait=([(wpark_ctr, wpark_st),
                               (wpark_ctr2, wpark_st2), (c_te, None)]
                                  if (qi == 0 and fc < 2)
                                  else [(c_te, uste[par])]),
                            stream=f"us{par}")
                b, bw = bank()
                for kk in range(NCH):
                    E("tensor", lambda te, fc=fc, kk=kk, b=b, par=par,
                        o4=o4, n4=n4: te.matmul(
                          pb[b][:, 0:n4],
                          wsl[par][:, kk * 128:(kk + 1) * 128],
                          xqfm[:, kk * T + o4: kk * T + o4 + n4],
                          start=(kk == 0), stop=(kk == 15)),
                      wait=(bw + [(vc, vv), (c_ve, xq_ve)] if kk == 0
                            else ()),
                      inc=([c_te] if kk == 15 else []))
                hmm_v = c_te.val
                uste[par] = hmm_v
                E("scalar", lambda se, b=b, n4=n4: se.activation(
                    rfp[:, 0:n4], pb[b][:, 0:n4], ACT.Relu),
                  wait=[(c_te, hmm_v), (c_ve, None)], inc=[c_se])
                bank_done(b, (c_se, c_se.val))
                E("vector", lambda ve, n4=n4: ve.tensor_tensor(
                    pfp[:, 0:n4], rfp[:, 0:n4], rfp[:, 0:n4], AX.mult),
                  wait=[(c_se, None)], inc=[c_ve])
                E("scalar", lambda se, fc=fc, n4=n4: se.activation(
                    h2h[:, fc * n4: fc * n4 + n4], pfp[:, 0:n4], ACT.Copy),
                  wait=[(c_ve, None)], inc=[c_se])
                E("vector", lambda ve, fc=fc, n4=n4: ve.scalar_tensor_tensor(
                    h2l[:, fc * n4: fc * n4 + n4], pfp[:, 0:n4], 1.0,
                    h2h[:, fc * n4: fc * n4 + n4], AX.bypass, AX.subtract),
                  wait=[(c_se, None)], inc=[c_ve])
                h2evs.append(c_ve.val)
            h2_ready = c_ve.val
            if debug and "d_h" in dbg and qi == 0:
                st(dbg["d_h"][:, 0:NFCH * 136],
                   h2h[:, 0:NFCH * 272].bitcast(F32),
                   wait=[(c_ve, h2_ready)])
                dbar()
            for sj, si in enumerate(qsubs):
                r0, rn = SUBS[si]
                so = r0 - o4
                ga_done = []
                for g in range(8):
                    b, bw = bank()
                    n_mm = 0
                    for j in range(4):
                        for tsrc in (h2h, h2l):
                            E("tensor", lambda te, g=g, j=j, tsrc=tsrc,
                                b=b, rn=rn, so=so, n4=n4, n_mm=n_mm:
                              te.matmul(
                                  pb[b][0:rn, 0:512],
                                  tsrc[:, (4 * g + j) * n4 + so:
                                       (4 * g + j) * n4 + so + rn],
                                  hh4_b[:, j * 512:(j + 1) * 512],
                                  start=(n_mm == 0), stop=(n_mm == 7)),
                              wait=(bw + [(c_ve, h2_ready)] if n_mm == 0
                                    else ()),
                              inc=([c_te] if n_mm == 7 else []))
                            n_mm += 1
                    mmv = c_te.val
                    E("scalar", lambda se, g=g, b=b, rn=rn:
                      se.activation(
                          g2a[0:rn, g * 512:(g + 1) * 512],
                          pb[b][0:rn, 0:512], ACT.Copy),
                      wait=[(c_te, mmv)], inc=[c_se])
                    bank_done(b, (c_se, c_se.val))
                def bfly(dst, srca, srcb, op, rn, w=None):
                    E("vector", lambda ve: ve.tensor_tensor(
                        dst[0:rn, :], srca[0:rn, :], srcb[0:rn, :], op),
                      wait=(w or ()), inc=[c_ve])
                BL = [g2a[:, i * 512:(i + 1) * 512] for i in range(8)]
                CL = [g2b[:, i * 512:(i + 1) * 512] for i in range(8)]
                w0_ = [(c_ve, c_ve.val), (c_se, c_se.val)]
                for i in range(4):
                    bfly(CL[i], BL[i], BL[i + 4], AX.add, rn,
                         w0_ if i == 0 else None)
                    bfly(CL[i + 4], BL[i], BL[i + 4], AX.subtract, rn)
                wv_ = [(c_ve, c_ve.val)]
                for base in (0, 4):
                    for i in range(2):
                        bfly(BL[base + i], CL[base + i], CL[base + i + 2],
                             AX.add, rn, wv_ if (base + i) == 0 else None)
                        bfly(BL[base + i + 2], CL[base + i],
                             CL[base + i + 2], AX.subtract, rn)
                wv2_ = [(c_ve, c_ve.val)]
                for base in (0, 2, 4, 6):
                    bfly(CL[base], BL[base], BL[base + 1], AX.add, rn,
                         wv2_ if base == 0 else None)
                    bfly(CL[base + 1], BL[base], BL[base + 1], AX.subtract,
                         rn)
                g2_ve = c_ve.val
                E("vector", lambda ve, rn=rn: ve.tensor_reduce(
                    qst[0:rn, 0:1], g2b[0:rn, :], AXL.X, AX.max,
                    apply_absolute_value=True),
                  wait=[(c_ve, g2_ve), (c_se, None)], inc=[c_ve])
                recipn(qst[0:rn, 6:7], s1t[0:rn, si:si + 1],
                       qst[0:rn, 7:8], rn,
                       w=[(c_ve, None), (c_ve, thr_ve)])
                E("vector", lambda ve, rn=rn: ve.tensor_tensor(
                    qst[0:rn, 1:2], wscb[0:rn, 0:1], qst[0:rn, 6:7],
                    AX.mult), wait=[(c_ve, None)], inc=[c_ve])
                E("vector", lambda ve, rn=rn: ve.tensor_tensor(
                    qst[0:rn, 2:3], qst[0:rn, 1:2], qst[0:rn, 1:2],
                    AX.mult), wait=[(c_ve, None)], inc=[c_ve])
                E("vector", lambda ve, rn=rn: ve.tensor_scalar(
                    qst[0:rn, 2:3], qst[0:rn, 2:3], C2, None, AX.mult),
                  wait=[(c_ve, None)], inc=[c_ve])
                E("vector", lambda ve, rn=rn: ve.tensor_tensor(
                    qst[0:rn, 3:4], qst[0:rn, 0:1], qst[0:rn, 2:3],
                    AX.mult), wait=[(c_ve, None)], inc=[c_ve])
                E("vector", lambda ve, rn=rn: ve.tensor_scalar(
                    qst[0:rn, 3:4], qst[0:rn, 3:4], 1e-5, None, AX.max),
                  wait=[(c_ve, None)], inc=[c_ve])
                recipn(qst[0:rn, 6:7], qst[0:rn, 3:4], qst[0:rn, 7:8], rn,
                       w=[(c_ve, None)])
                E("vector", lambda ve, rn=rn: ve.tensor_scalar(
                    qst[0:rn, 4:5], qst[0:rn, 6:7], 127.0, None, AX.mult),
                  wait=[(c_ve, None)], inc=[c_ve])
                E("vector", lambda ve, rn=rn: ve.tensor_tensor(
                    qst[0:rn, 5:6], qst[0:rn, 4:5], qst[0:rn, 2:3],
                    AX.mult), wait=[(c_ve, None)], inc=[c_ve])
                E("vector", lambda ve, rn=rn: ve.tensor_scalar(
                    qst[0:rn, 5:6], qst[0:rn, 5:6], -1.0, None, AX.mult),
                  wait=[(c_ve, None)], inc=[c_ve])
                recipn(qst[0:rn, 6:7], qst[0:rn, 4:5], qst[0:rn, 7:8], rn,
                       w=[(c_ve, None)])
                E("vector", lambda ve, rn=rn, si=si: ve.tensor_tensor(
                    o2t[0:rn, si:si + 1], wscb[0:rn, 1:2], qst[0:rn, 6:7],
                    AX.mult), wait=[(c_ve, None)], inc=[c_ve])
                q2_ve = c_ve.val
                E("scalar", lambda se, rn=rn: se.activation(
                    g2a[0:rn, :], g2b[0:rn, :], ACT.Copy, bias=128.5,
                    scale=qst[0:rn, 5:6]),
                  wait=[(c_ve, q2_ve)], inc=[c_se])
                ladder(g2a[0:rn, :], wq2[0:rn, :], tq2[0:rn, :],
                       g2b[0:rn, :], [(c_se, None)])
                hq_ve = c_ve.val
                hpar = 0
                for fcc in range(NFCH):
                    b, bw = bank()
                    E("tensor", lambda te, fcc=fcc, b=b: te.transpose(
                        pb[b][0:128, 0:128],
                        g2b[0:128, fcc * 128:(fcc + 1) * 128],
                        ident_f[:, :]),
                      wait=bw + [(c_ve, hq_ve)]
                      + ([(stream_ctr(f"hqp{hpar}"), None)]
                         if fcc == 0 else []),
                      inc=[c_te])
                    tv = c_te.val
                    E("scalar", lambda se, fcc=fcc, b=b, rn=rn, hpar=hpar:
                      se.activation(
                          hqst[hpar][:, fcc * 128:fcc * 128 + rn],
                          pb[b][0:128, 0:rn], ACT.Copy),
                      wait=[(c_te, tv)], inc=[c_se])
                    bank_done(b, (c_se, c_se.val))
                E("sync", lambda e, r0=r0, rn=rn, hpar=hpar: e.dma_start(
                    out=hq_park[:, :, r0:r0 + rn].transpose([1, 0, 2]),
                    in_=hqst[hpar][:, :]
                    .rearrange("p (f c) -> p f c", c=128)[:, :, 0:rn]),
                  wait=[(c_ve, None), (c_se, None)],
                  inc=[stream_ctr(f"hqp{hpar}")], incn=16)
            h2_ve["v"] = c_ve.val
        hq_done = c_ve.val
        dstore("d_o2", o2t[:, 0:NSUB], wait=[(c_ve, hq_done)])

        dbar()
        for buf in (h2h, h2l, g2a, g2b, rfp, pfp, hqst[0],
                    wq2, tq2, xqfm, wsl[0], wsl[1]):
            AR.free(buf)

        ffnfm = AR.alloc(NCH * T, F32, tag="ffnfm")
        hqfm = AR.alloc(NFCH * T, BF16, tag="hqfm")
        wsl_d = [AR.alloc(NFCH * 128, BF16, tag="wsl_dn")
                 for _ in range(2)]
        hql_v = []
        for fc in range(NFCH):
            sk = f"hql{fc % 4}"
            _, hv = ld(hqfm[:, fc * T:(fc + 1) * T], hq_park[fc],
                       q="gpsimd",
                       wait=([(stream_ctr("hqp0"), None)] if fc < 4
                             else [(stream_ctr(sk), None)]),
                       stream=sk)
            hql_v.append((stream_ctr(sk), hv))
        dn_evs = []
        dste = {0: 0, 1: 0}
        for dc in range(NCH):
            par = dc % 2
            vc, vv = ld(wsl_d[par][:, 0:NFCH * 128]
                        .rearrange("p (k f) -> p k f", f=128),
                        wdn_int[:, :, dc * 128:(dc + 1) * 128]
                        .transpose([1, 0, 2]),
                        q="sync",
                        wait=([(c_te, dste[par])] if dc >= 2
                              else [(wpark_ctr, wpark_st),
                                    (wpark_ctr2, wpark_st2)]),
                        stream=f"us{par}")
            for gi, (g0, gn) in enumerate(TG):
                b, bw = bank()
                for fc in range(NFCH):
                    E("tensor", lambda te, dc=dc, fc=fc, b=b, par=par,
                        g0=g0, gn=gn: te.matmul(
                          pb[b][:, 0:gn],
                          wsl_d[par][:, fc * 128:(fc + 1) * 128],
                          hqfm[:, fc * T + g0: fc * T + g0 + gn],
                          start=(fc == 0), stop=(fc == 31)),
                      wait=((bw + [(vc, vv), (c_ve, hq_done)]
                             if fc == 0 else ())
                            if dc + gi > 0 else
                            (bw + [(vc, vv), (c_ve, hq_done),
                                   hql_v[fc]] if fc == 0
                             else [hql_v[fc]])),
                      inc=([c_te] if fc == 31 else []))
                dmm = c_te.val
                E("scalar", lambda se, dc=dc, g0=g0, gn=gn, b=b:
                  se.activation(ffnfm[:, dc * T + g0: dc * T + g0 + gn],
                                pb[b][:, 0:gn], ACT.Copy),
                  wait=[(c_te, dmm)], inc=[c_se])
                bank_done(b, (c_se, c_se.val))
            dste[par] = c_te.val
        ffn_se = c_se.val
        dstore("d_ffn", ffnfm[:, :], wait=[(c_se, ffn_se)])

        z2s = AR.alloc(D, F32, tag="z2s")
        y1r = AR.alloc(D, F32, tag="y1r")
        y2s = AR.alloc(D, F32, tag="y2s")
        final_sts = []
        for si in range(NSUB):
            r0, rn = SUBS[si]
            vc, vv = ld(y1r[:, :], y1_park[si], q="gpsimd",
                        wait=[(c_ve, None),
                              (stream_ctr("y1p"), None)],
                        stream="y1r")
            for cc in range(NCH):
                b, bw = bank()
                E("tensor", lambda te, cc=cc, b=b, r0=r0, rn=rn:
                  te.transpose(
                      pb[b][0:rn, 0:128],
                      ffnfm[0:128, cc * T + r0: cc * T + r0 + rn],
                      ident_f[:, :]),
                  wait=bw + [(c_se, ffn_se)], inc=[c_te])
                tv = c_te.val
                E("vector", lambda ve, cc=cc, b=b, rn=rn, si=si:
                  ve.scalar_tensor_tensor(
                      z2s[0:rn, cc * 128:(cc + 1) * 128],
                      pb[b][0:rn, 0:128], o2t[0:rn, si:si + 1],
                      y1r[0:rn, cc * 128:(cc + 1) * 128],
                      AX.mult, AX.add),
                  wait=[(c_te, tv), (vc, vv)], inc=[c_ve])
                bank_done(b)
            z2_ve = c_ve.val
            stats_ve = ln_stats(z2s[0:rn, :], rn,
                                [(c_ve, z2_ve), (c_se, None)])
            E("vector", lambda ve, rn=rn: ve.tensor_scalar(
                y2s[0:rn, :], z2s[0:rn, :], zst[0:rn, 0:1], zst[0:rn, 1:2],
                AX.subtract, AX.mult),
              wait=[(c_ve, stats_ve), (c_st, None)], inc=[c_ve])
            y2_ve = c_ve.val
            final_sts.append(st(y_out[r0:r0 + rn, :], y2s[0:rn, :],
                                wait=[(c_ve, y2_ve)]))
        fv = c_st.val
        E("sync", lambda e: None,
          wait=[(c_st, fv), (c_ldg, None)]
          + [(cc, None) for cc in streams.values()])
        plan.run(ent(nc.Block()))
    return nc, dram, dbg
def _finish_stub(E, AR, c_ve, c_st, y_out, zz, waits=()):
    E("vector", lambda ve: ve.memset(zz[:, 0:64], 0.0),
      wait=list(waits), inc=[c_ve])
    v = c_ve.val
    for i in range(NSUB):
        r0, rn = SUBS[i]
        for j in range(D // 64):
            E("sync", lambda e, r0=r0, rn=rn, j=j: e.dma_start(
                out=y_out[r0:r0 + rn, 64 * j:64 * j + 64],
                in_=zz[0:rn, 0:64]),
              wait=[(c_ve, v)], inc=[c_st], incn=16)
    fv = c_st.val
    E("sync", lambda e: None, wait=[(c_st, fv)])


def kernel(**inputs):
    maps = _stage(inputs)
    nc, dram, dbg = build_program(debug=0)
    res = run_bass_kernel_spmd(nc, maps, core_ids=list(range(NCORES)))
    out = np.zeros((S, D), dtype=np.float32)
    for c in range(NCORES):
        out[T0S[c]:T0S[c] + SIZES[c]] = res.results[c]["y_out"][:SIZES[c]]
    return out.reshape(1, S, D)
